# version 16
# baseline (speedup 1.0000x reference)
import numpy as np
import ml_dtypes
from contextlib import ExitStack
import concourse.bass as bass
import concourse.mybir as mybir
from concourse.bass_utils import run_bass_kernel_spmd

F32 = mybir.dt.float32
BF16 = mybir.dt.bfloat16
AF = mybir.ActivationFunctionType
ALU = mybir.AluOpType
AX = mybir.AxisListType.X
BF = ml_dtypes.bfloat16

NCORE = 8
B = 4
S = 4096
T = 2048
D = 1024
NT = T // 128
NC = D // 128
EPS = 1e-5


class Sem:
    __slots__ = ("h", "n", "name")
    def __init__(self, h, name):
        self.h = h; self.n = 0; self.name = name
    def tick(self, k=1):
        self.n += k
        return self.n

ENGS = ("sync", "gpsimd", "vector", "scalar", "tensor")

class Prog:
    def __init__(self, name="prog"):
        self.nc = bass.Bass("TRN2", target_bir_lowering=False, debug=False)
        self.stack = ExitStack()
        self.steps = {e: [] for e in ENGS}
        self.name = name
        self._semi = 0
        self._sbi = 0
        self._allsems = []
        self._sems_by_name = {}
        self._phase_stack = None
        self._phase_name = ""
        self._bar = None

    def sbuf(self, shape, dtype, name=None):
        self._sbi += 1
        nm = f"{self._phase_name}{name or ''}_{self._sbi}"
        return (self._phase_stack or self.stack).enter_context(
            self.nc.sbuf_tensor(nm, shape, dtype))

    def psum(self, shape, dtype=F32, name=None):
        self._sbi += 1
        nm = f"{self._phase_name}{name or ''}_{self._sbi}"
        return (self._phase_stack or self.stack).enter_context(
            self.nc.psum_tensor(nm, shape, dtype))

    def sem(self, name=None):
        if name is not None and name in self._sems_by_name:
            return self._sems_by_name[name]
        self._semi += 1
        nm = name or f"sem{self._semi}"
        h = self.nc.alloc_semaphore(name=nm)
        self._allsems.append(h)
        s = Sem(h, nm)
        if name is not None:
            self._sems_by_name[name] = s
        return s

    def barrier(self):
        if self._bar is None:
            self._bar = self.sem("bar")
        tgt = self._bar.n + 5
        self._bar.n = tgt
        for eng in ENGS:
            self.on(eng, lambda e, h=self._bar.h: e.sem_inc(h, 1))
        for eng in ENGS:
            wait(self, eng, self._bar, tgt)

    def phase(self, name):
        if self._phase_stack is not None:
            self._phase_stack.close()
            self.barrier()
        self._phase_stack = ExitStack()
        self._phase_name = name + "_"

    def psem(self, name):
        return self.sem(self._phase_name + name)

    def on(self, eng, fn):
        self.steps[eng].append(fn)

    def finish(self):
        nc = self.nc
        if self._phase_stack is not None:
            self._phase_stack.close()
            self._phase_stack = None
        with nc.cleanup_on_exit():
            with nc.Block() as block:
                @block.sync
                def _(e):
                    for f in self.steps["sync"]: f(e)
                @block.gpsimd
                def _(e):
                    for f in self.steps["gpsimd"]: f(e)
                @block.vector
                def _(e):
                    for f in self.steps["vector"]: f(e)
                @block.scalar
                def _(e):
                    for f in self.steps["scalar"]: f(e)
                @block.tensor
                def _(e):
                    for f in self.steps["tensor"]: f(e)
            nc.all_engine_barrier()
            sems = sorted(s.num if hasattr(s, "num") else s for s in self._allsems)
            if sems:
                nc.gpsimd.dma_reset(range(min(sems), max(sems) + 1))
            nc.all_engine_barrier()
        self.stack.close()
        return nc


def dma(p, eng, sem, dst, src, **kw):
    t = sem.tick(16)
    p.on(eng, lambda e, d=dst, s=src, sm=sem.h, kw=kw: e.dma_start(d, s, **kw).then_inc(sm, 16))
    return t

def wait(p, eng, sem, val):
    p.on(eng, lambda e, sm=sem.h, v=val: e.wait_ge(sm, v))

def run(p, eng, fn, sem=None):
    if sem is None:
        p.on(eng, fn)
        return None
    t = sem.tick(1)
    def go(e, fn=fn, sm=sem.h):
        fn(e).then_inc(sm, 1)
    p.on(eng, go)
    return t


def _io(nc, t, name, shape, dtype, kind):
    if t is not None and name in t:
        return t[name]
    return nc.dram_tensor(name, shape, dtype, kind=kind).ap()

def to_bf(x):
    return np.ascontiguousarray(x).astype(BF)

def tm_ap(dram_ap):
    return dram_ap.rearrange("(t p) d -> p t d", p=128)

def fm_ap(dram_ap):
    return dram_ap.rearrange("(c p) n -> p c n", p=128)

IDENT_BF = np.eye(128, dtype=np.float32).astype(BF)
CCONST = np.tile(np.array([[EPS, 0, 0, 0, 0, 0, 0, 0]], np.float32), (128, 1))
IDENT_F32 = np.eye(128, dtype=np.float32)


def build_stage_ln(has_res, scale_mode, has_res3=False, p=None, t=None):
    p = p or Prog("ln")
    nc = p.nc
    xin = _io(nc, t, "xin", [T, D], F32, "ExternalInput")
    if has_res:
        ya = _io(nc, t, "ya", [T, D], F32, "ExternalInput")
        yf = _io(nc, t, "yf", [T, D], F32, "ExternalInput")
    if has_res3:
        y3 = _io(nc, t, "y3", [T, D], F32, "ExternalInput")
    identbf = _io(nc, t, "identbf", [128, 128], BF16, "ExternalInput")
    cconst = _io(nc, t, "cconst", [128, 8], F32, "ExternalInput")
    xnew_o = _io(nc, t, "xnew", [T, D], F32, "ExternalOutput")
    xn_o = _io(nc, t, "xn", [T, D], F32, "ExternalOutput")
    xnfm_o = _io(nc, t, "xnfm", [D, T], BF16, "ExternalOutput")
    if scale_mode == "c2":
        c2_o = _io(nc, t, "c2", [128, NT], F32, "ExternalOutput")

    X = p.sbuf([128, NT * D], F32, "X")
    XN = p.sbuf([128, NT * D], F32, "XN")
    SQ = p.sbuf([128, D], F32, "SQ")
    XBF = p.sbuf([128, 2 * D], BF16, "XBF")
    XNFM = p.sbuf([128, NC * T], BF16, "XNFM")
    IDB = p.sbuf([128, 128], BF16, "IDB")
    SUM = p.sbuf([128, NT], F32, "SUM")
    SQS = p.sbuf([128, NT], F32, "SQS")
    MU = p.sbuf([128, NT], F32, "MU")
    MU2 = p.sbuf([128, NT], F32, "MU2")
    VAR = p.sbuf([128, NT], F32, "VAR")
    STD = p.sbuf([128, NT], F32, "STD")
    RSTD = p.sbuf([128, NT], F32, "RSTD")
    YA = p.sbuf([128, 2 * D], F32, "YA")
    YF = p.sbuf([128, 2 * D], F32, "YF")
    if has_res3:
        Y3 = p.sbuf([128, 2 * D], F32, "Y3")
    if scale_mode == "c2":
        VPE = p.sbuf([128, NT], F32, "VPE")
        RVE = p.sbuf([128, NT], F32, "RVE")
        RAT = p.sbuf([128, NT], F32, "RAT")
        SR = p.sbuf([128, NT], F32, "SR")
        C2 = p.sbuf([128, NT], F32, "C2")
    CC = p.sbuf([128, 8], F32, "CC")
    PS = [p.psum([128, 128], BF16, f"PT{i}") for i in range(4)]

    lda = p.sem("lda"); va = p.sem("va"); ac = p.sem("ac"); pe = p.sem("pe")
    ev = p.sem("ev"); st = p.psem("st")
    ldy = [p.sem("ldy0"), p.sem("ldy1")]

    dma(p, "sync", lda, X[:, :], tm_ap(xin))
    dma(p, "sync", lda, IDB[:, :], identbf)
    dma(p, "sync", lda, CC[:, :], cconst)
    LDA = lda.n

    addv = []
    if has_res:
        for t in range(NT):
            bsl = slice((t % 2) * D, (t % 2) * D + D)
            if t >= 2:
                wait(p, "sync", va, addv[t - 2])
            dma(p, "sync", ldy[t % 2], YA[:, bsl], ya[t * 128:(t + 1) * 128, :])
            if has_res3:
                dma(p, "sync", ldy[t % 2], Y3[:, bsl], y3[t * 128:(t + 1) * 128, :])
            ty = dma(p, "sync", ldy[t % 2], YF[:, bsl], yf[t * 128:(t + 1) * 128, :])
            wait(p, "vector", lda, LDA)
            wait(p, "vector", ldy[t % 2], ty)
            run(p, "vector", lambda e, t=t, bsl=bsl: e.tensor_add(
                X[:, t * D:(t + 1) * D], X[:, t * D:(t + 1) * D], YA[:, bsl]))
            if has_res3:
                run(p, "vector", lambda e, t=t, bsl=bsl: e.tensor_add(
                    X[:, t * D:(t + 1) * D], X[:, t * D:(t + 1) * D], Y3[:, bsl]))
            addv.append(run(p, "vector", lambda e, t=t, bsl=bsl: e.tensor_add(
                X[:, t * D:(t + 1) * D], X[:, t * D:(t + 1) * D], YF[:, bsl]), va))
        wait(p, "sync", va, addv[-1])
    else:
        wait(p, "vector", lda, LDA)
        addv = [0] * NT
        wait(p, "sync", lda, LDA)
    dma(p, "sync", st, tm_ap(xnew_o), X[:, :])

    for t in range(NT):
        if has_res:
            wait(p, "scalar", va, addv[t])
        else:
            wait(p, "scalar", lda, LDA)
        run(p, "scalar", lambda e, t=t: e.activation(
            SQ[:, :], X[:, t * D:(t + 1) * D], AF.Square,
            accum_out=SQS[:, t:t + 1]), ac)
    last_red = None
    for t in range(NT):
        if has_res:
            wait(p, "vector", va, addv[t])
        last_red = run(p, "vector", lambda e, t=t: e.reduce_sum(
            SUM[:, t:t + 1], X[:, t * D:(t + 1) * D], axis=AX), va)
    wait(p, "vector", va, last_red)
    tmu = run(p, "vector", lambda e: e.tensor_scalar_mul(MU[:, :], SUM[:, :], 1.0 / D), va)
    wait(p, "vector", va, tmu)
    tmu2 = run(p, "vector", lambda e: e.tensor_mul(MU2[:, :], MU[:, :], MU[:, :]), va)
    wait(p, "vector", ac, ac.n)
    wait(p, "vector", va, tmu2)
    tvar = run(p, "vector", lambda e: e.scalar_tensor_tensor(
        VAR[:, :], SQS[:, :], 1.0 / D, MU2[:, :], ALU.mult, ALU.subtract), va)
    wait(p, "scalar", va, tvar)
    wait(p, "scalar", lda, LDA)
    tstd = run(p, "scalar", lambda e: e.activation(STD[:, :], VAR[:, :], AF.Sqrt, bias=CC[:, 0:1]), ac)
    wait(p, "vector", ac, tstd)
    trs = run(p, "vector", lambda e: e.reciprocal(RSTD[:, :], STD[:, :]), va)

    if scale_mode == "c2":
        t1 = run(p, "vector", lambda e: e.tensor_scalar_add(VPE[:, :], VAR[:, :], EPS), va)
        wait(p, "vector", va, t1)
        t2 = run(p, "vector", lambda e: e.reciprocal(RVE[:, :], VPE[:, :]), va)
        wait(p, "vector", va, t2)
        t3 = run(p, "vector", lambda e: e.tensor_mul(RAT[:, :], VAR[:, :], RVE[:, :]), va)
        wait(p, "scalar", va, t3)
        t4 = run(p, "scalar", lambda e: e.activation(SR[:, :], RAT[:, :], AF.Sqrt, bias=CC[:, 0:1]), ac)
        wait(p, "vector", ac, t4)
        t5 = run(p, "vector", lambda e: e.reciprocal(C2[:, :], SR[:, :]), va)
        wait(p, "sync", va, t5)
        dma(p, "sync", st, c2_o, C2[:, :])

    wait(p, "vector", va, trs)
    xnv = []
    for t in range(NT):
        xnv.append(run(p, "vector", lambda e, t=t: e.tensor_scalar(
            XN[:, t * D:(t + 1) * D], X[:, t * D:(t + 1) * D],
            MU[:, t:t + 1], RSTD[:, t:t + 1], ALU.subtract, ALU.mult), va))
    wait(p, "sync", va, xnv[-1])
    dma(p, "sync", st, tm_ap(xn_o), XN[:, :])

    sem_cast = va if scale_mode == "c2" else ac
    castv = []
    evn = []
    for t in range(NT):
        bsl = slice((t % 2) * D, (t % 2) * D + D)
        eng = "vector" if scale_mode == "c2" else "scalar"
        if scale_mode == "c2":
            wait(p, eng, va, max(xnv[t], t5))
        else:
            wait(p, eng, va, xnv[t])
        if t >= 2:
            wait(p, eng, pe, (t - 1) * NC)
        if scale_mode == "c2":
            castv.append(run(p, "vector", lambda e, t=t, bsl=bsl: e.tensor_scalar(
                XBF[:, bsl], XN[:, t * D:(t + 1) * D], C2[:, t:t + 1], None, ALU.mult), va))
        else:
            castv.append(run(p, "scalar", lambda e, t=t, bsl=bsl: e.copy(
                XBF[:, bsl], XN[:, t * D:(t + 1) * D]), ac))
        for c in range(NC):
            k = t * NC + c
            bank = k % 4
            wait(p, "tensor", sem_cast, castv[t])
            if k == 0:
                wait(p, "tensor", lda, LDA)
            if k >= 4:
                wait(p, "tensor", ev, evn[k - 4])
            tp = run(p, "tensor", lambda e, t=t, c=c, bank=bank: e.transpose(
                PS[bank][:, :], XBF[:, (t % 2) * D + c * 128:(t % 2) * D + (c + 1) * 128],
                IDB[:, :]), pe)
            wait(p, "vector", pe, tp)
            evn.append(run(p, "vector", lambda e, t=t, c=c, bank=bank: e.tensor_copy(
                XNFM[:, c * T + t * 128:c * T + (t + 1) * 128], PS[bank][:, :]), ev))
    wait(p, "sync", ev, evn[-1])
    dma(p, "sync", st, fm_ap(xnfm_o), XNFM[:, :])
    wait(p, "sync", st, st.n)
    return p



def pack_lhsT(w_din_dout):
    return np.ascontiguousarray(w_din_dout.astype(np.float32))

def load_w_fm(p, _unused, dram_ap, din, dout, name):
    W = p.sbuf([128, (din // 128) * dout], BF16, name)
    s = p.sem("w_" + name)
    dma(p, "gpsimd", s, W[:, :], dram_ap.rearrange("(c p) n -> p c n", p=128))
    return W, s

def emit_proj_fm(p, sems, XF, W, OUT, dout, *, bias_col=None, nci=NC, ntok=T,
                 pre=(), evac_eng="scalar", extra_wait=None,
                 evac_wait=None):
    va, ac, pe, ev = sems["va"], sems["ac"], sems["pe"], sems["ev"]
    PSB = sems["psumpool"]
    nco = dout // 128
    ntc = ntok // 512
    k = 0
    outv = []
    for co in range(nco):
        for tc in range(ntc):
            bank = k % len(PSB)
            for (sm, vv) in pre:
                wait(p, "tensor", sm, vv)
            if extra_wait is not None:
                extra_wait(k)
            if k >= len(PSB):
                wait(p, "tensor", ev, outv[k - len(PSB)])
            for ci in range(nci):
                st_ = (ci == 0); sp_ = (ci == nci - 1)
                run(p, "tensor", lambda e, co=co, tc=tc, ci=ci, bank=bank, st_=st_, sp_=sp_:
                    e.matmul(PSB[bank][:, :],
                             W[:, ci * dout + co * 128:ci * dout + (co + 1) * 128],
                             XF[:, ci * ntok + tc * 512:ci * ntok + tc * 512 + 512],
                             start=st_, stop=sp_),
                    pe if sp_ else None)
            wait(p, evac_eng, pe, pe.n)
            if evac_wait is not None:
                evac_wait(k)
            if bias_col is not None:
                outv.append(run(p, evac_eng, lambda e, co=co, tc=tc, bank=bank: e.tensor_scalar(
                    OUT[:, co * ntok + tc * 512:co * ntok + tc * 512 + 512], PSB[bank][:, :],
                    bias_col[:, co:co + 1], None, ALU.add), ev))
            else:
                fn = (lambda e, co=co, tc=tc, bank=bank: e.copy(
                    OUT[:, co * ntok + tc * 512:co * ntok + tc * 512 + 512], PSB[bank][:, :])) \
                    if evac_eng == "scalar" else (lambda e, co=co, tc=tc, bank=bank: e.tensor_copy(
                    OUT[:, co * ntok + tc * 512:co * ntok + tc * 512 + 512], PSB[bank][:, :]))
                outv.append(run(p, evac_eng, fn, ev))
            k += 1
    return outv

def emit_rope_fm(p, sems, QF, QR, COSF, SINF, *, ntok=T, deps=None):
    va = sems["va"]
    outs = []
    S0 = NC * ntok
    S1 = (NC + 1) * ntok
    for c0 in range(4):
        c1 = c0 + 4
        if deps is not None:
            deps(c0); deps(c1)
        run(p, "vector", lambda e, c1=c1, c0=c0: e.tensor_mul(
            QR[:, S0:S0 + ntok], QF[:, c1 * ntok:(c1 + 1) * ntok],
            SINF[:, c0 * ntok:(c0 + 1) * ntok]))
        run(p, "vector", lambda e, c0=c0: e.tensor_mul(
            QR[:, S1:S1 + ntok], QF[:, c0 * ntok:(c0 + 1) * ntok],
            SINF[:, c0 * ntok:(c0 + 1) * ntok]))
        run(p, "vector", lambda e, c0=c0: e.tensor_mul(
            QR[:, c0 * ntok:(c0 + 1) * ntok], QF[:, c0 * ntok:(c0 + 1) * ntok],
            COSF[:, c0 * ntok:(c0 + 1) * ntok]))
        run(p, "vector", lambda e, c1=c1, c0=c0: e.tensor_mul(
            QR[:, c1 * ntok:(c1 + 1) * ntok], QF[:, c1 * ntok:(c1 + 1) * ntok],
            COSF[:, c0 * ntok:(c0 + 1) * ntok]))
        outs.append(run(p, "vector", lambda e, c0=c0: e.tensor_sub(
            QR[:, c0 * ntok:(c0 + 1) * ntok], QR[:, c0 * ntok:(c0 + 1) * ntok],
            QR[:, S0:S0 + ntok]), va))
        outs.append(run(p, "vector", lambda e, c1=c1: e.tensor_add(
            QR[:, c1 * ntok:(c1 + 1) * ntok], QR[:, c1 * ntok:(c1 + 1) * ntok],
            QR[:, S1:S1 + ntok]), va))
    return outs


def build_stage_l0qk(p=None, t=None):
    p = p or Prog("l0qk")
    nc = p.nc
    xnfm_i = _io(nc, t, "xnfm", [D, T], BF16, "ExternalInput")
    wq_i = _io(nc, t, "wq", [D, D], F32, "ExternalInput")
    wk_i = _io(nc, t, "wk", [D, D], F32, "ExternalInput")
    cos_i = _io(nc, t, "cosf", [512, T], BF16, "ExternalInput")
    sin_i = _io(nc, t, "sinf", [512, T], BF16, "ExternalInput")
    qr_o = _io(nc, t, "qr", [D, T], BF16, "ExternalOutput")
    kr_o = _io(nc, t, "kr", [D, T], BF16, "ExternalOutput")

    XF = p.sbuf([128, NC * T], BF16, "XF")
    QF = p.sbuf([128, NC * T], BF16, "QF")
    QR = p.sbuf([128, (NC + 2) * T], BF16, "QR")
    COSF = p.sbuf([128, 4 * T], BF16, "COSF")
    SINF = p.sbuf([128, 4 * T], BF16, "SINF")
    PSB = [p.psum([128, 512], F32, f"pp{i}") for i in range(4)]
    lda = p.sem("lda"); va = p.sem("va"); ac = p.sem("ac"); pe = p.sem("pe")
    ev = p.sem("ev"); st = p.psem("st")
    sems = dict(va=va, ac=ac, pe=pe, ev=ev, psumpool=PSB)

    dma(p, "sync", lda, XF[:, :], fm_ap(xnfm_i))
    dma(p, "sync", lda, COSF[:, :], cos_i.rearrange("(c p) n -> p c n", p=128))
    dma(p, "sync", lda, SINF[:, :], sin_i.rearrange("(c p) n -> p c n", p=128))
    LDA = lda.n
    WQ, wsq = load_w_fm(p, None, wq_i, D, D, "WQ")
    WK, wsk = load_w_fm(p, None, wk_i, D, D, "WK")

    qo = emit_proj_fm(p, sems, XF, WQ, QF, D, pre=[(lda, LDA), (wsq, wsq.n)])
    wait(p, "vector", lda, LDA)
    def qdeps(c, qo=qo):
        wait(p, "vector", ev, qo[c * (T // 512) + (T // 512) - 1])
    qrv = emit_rope_fm(p, sems, QF, QR, COSF, SINF, deps=qdeps)
    wait(p, "sync", va, qrv[-1])
    dma(p, "sync", st, fm_ap(qr_o), QR[:, 0:NC * T])
    def kwait(k, qrv=qrv):
        wait(p, "scalar", va, qrv[-1])
    ko = emit_proj_fm(p, sems, XF, WK, QF, D, pre=[(lda, LDA), (wsk, wsk.n)], evac_wait=kwait)
    def kdeps(c, ko=ko):
        wait(p, "vector", ev, ko[c * (T // 512) + (T // 512) - 1])
    wait(p, "vector", st, 16)
    krv = emit_rope_fm(p, sems, QF, QR, COSF, SINF, deps=kdeps)
    wait(p, "sync", va, krv[-1])
    dma(p, "sync", st, fm_ap(kr_o), QR[:, 0:NC * T])
    wait(p, "sync", st, st.n)
    return p



def build_stage_l0attn(p=None, t=None):
    p = p or Prog("l0attn")
    nc = p.nc
    qr_i = _io(nc, t, "qr", [D, T], BF16, "ExternalInput")
    kag0 = t.get("krf_ag") if t else None
    if kag0 is None:
        krf_i = _io(nc, t, "krf", [D, S], BF16, "ExternalInput")
    fc1p_i = _io(nc, t, "fc1p", [1, 16], F32, "ExternalInput")
    fc1b_i = _io(nc, t, "fc1b", [16, 1], F32, "ExternalInput")
    fc2p_i = _io(nc, t, "fc2p", [17, D], F32, "ExternalInput")
    bcn_i = _io(nc, t, "bcn", [128, 4 * 32], F32, "ExternalInput")
    smsk2_i = _io(nc, t, "smsk2", [128, 8 * 512], BF16, "ExternalInput")
    y0a_o = _io(nc, t, "y0a", [T, D], F32, "ExternalOutput")

    QR = p.sbuf([128, NC * T], BF16, "QR")
    KRF = p.sbuf([128, NC * S], BF16, "KRF")
    EXPT = p.sbuf([128, 2 * 512], BF16, "EXPT")
    BCN = p.sbuf([128, 4 * 32], F32, "BCN")
    SMSK = p.sbuf([128, 8 * 512], BF16, "SMSK")
    SCR = p.sbuf([128, 512], F32, "SCR")
    ONESC = p.sbuf([128, 1], BF16, "ONESC")
    OS = p.sbuf([1, T], F32, "OS")
    FC1 = p.sbuf([1, 16], F32, "FC1")
    FC1B = p.sbuf([16, 1], F32, "FC1B")
    FC2 = p.sbuf([17, D], BF16, "FC2")
    H1 = p.sbuf([17, T], BF16, "H1")
    Y = p.sbuf([128, NT * D], F32, "Y")
    PSC = [p.psum([128, 512], F32, f"psc{i}") for i in range(2)]
    POS = p.psum([1, 512], F32, "pos")
    PH = p.psum([16, 512], F32, "ph")
    PO = [p.psum([128, 512], F32, f"po{i}") for i in range(2)]

    lda = p.sem("lda"); va = p.sem("va"); ac = p.sem("ac"); pe = p.sem("pe")
    ev = p.sem("ev"); st = p.psem("st"); gld = p.sem("gld")

    dma(p, "sync", lda, QR[:, :], fm_ap(qr_i))
    if kag0 is None:
        dma(p, "sync", lda, KRF[:, :], fm_ap(krf_i))
    else:
        for ci in range(NC):
            for gg in range(2):
                dma(p, "sync", lda, KRF[:, ci * S + gg * T:ci * S + gg * T + T],
                    kag0[gg * D + ci * 128:gg * D + ci * 128 + 128, :])
    dma(p, "sync", lda, BCN[:, :], bcn_i)
    dma(p, "sync", lda, SMSK[:, :], smsk2_i)
    dma(p, "sync", lda, FC1[:, :], fc1p_i)
    dma(p, "sync", lda, FC1B[:, :], fc1b_i)
    LDA = lda.n
    dma(p, "gpsimd", gld, FC2[:, :], fc2p_i)
    GLD = gld.n
    run(p, "vector", lambda e: e.memset(ONESC[:, :], 1.0))
    von = run(p, "vector", lambda e: e.memset(H1[:, :], 1.0), va)

    expn = []
    pemm = []
    osev = []
    for qc in range(4):
        slots = [qc * 4 + j for j in range(4)] + [16 + qc * 4 + j for j in range(4)]
        for kt in range(32):
            bank = kt % 2
            wait(p, "tensor", lda, LDA)
            if len(expn) >= 2:
                wait(p, "tensor", ac, expn[-2])
            for ci in range(NC):
                run(p, "tensor", lambda e, ci=ci, kt=kt, qc=qc, bank=bank:
                    e.matmul(PSC[bank][:, :],
                             KRF[:, ci * S + kt * 128:ci * S + (kt + 1) * 128],
                             QR[:, ci * T + qc * 512:ci * T + qc * 512 + 512],
                             start=(ci == 0), stop=(ci == NC - 1)),
                    pe if ci == NC - 1 else None)
            if len(pemm) >= 2:
                wait(p, "scalar", pe, pemm[-2])
            bcol = qc * 32 + kt
            if kt in slots:
                sj = slots.index(kt)
                wait(p, "vector", pe, pe.n)
                if expn:
                    wait(p, "vector", ac, expn[-1])
                wait(p, "vector", lda, LDA)
                tvm = run(p, "vector", lambda e, bank=bank, sj=sj: e.tensor_add(
                    SCR[:, :], PSC[bank][:, :], SMSK[:, sj * 512:(sj + 1) * 512]), va)
                wait(p, "scalar", va, tvm)
                expn.append(run(p, "scalar", lambda e, kt=kt, bcol=bcol: e.activation(
                    EXPT[:, (kt % 2) * 512:(kt % 2) * 512 + 512], SCR[:, :], AF.Exp,
                    scale=1.0 / 32.0, bias=BCN[:, bcol:bcol + 1]), ac))
            else:
                wait(p, "scalar", pe, pe.n)
                expn.append(run(p, "scalar", lambda e, kt=kt, bank=bank, bcol=bcol: e.activation(
                    EXPT[:, (kt % 2) * 512:(kt % 2) * 512 + 512], PSC[bank][:, :], AF.Exp,
                    scale=1.0 / 32.0, bias=BCN[:, bcol:bcol + 1]), ac))
            wait(p, "tensor", ac, expn[-1])
            if qc > 0 and kt == 0:
                wait(p, "tensor", ev, osev[qc - 1])
            pemm.append(run(p, "tensor", lambda e, kt=kt: e.matmul(
                POS[:, :], ONESC[:, :], EXPT[:, (kt % 2) * 512:(kt % 2) * 512 + 512],
                start=(kt == 0), stop=(kt == 31)), pe))
        wait(p, "vector", pe, pe.n)
        osev.append(run(p, "vector", lambda e, qc=qc: e.tensor_copy(
            OS[:, qc * 512:qc * 512 + 512], POS[:, :]), ev))
    h1n = []
    for qc in range(4):
        wait(p, "tensor", ev, osev[qc])
        wait(p, "tensor", lda, LDA)
        if qc > 0:
            wait(p, "tensor", ac, h1n[qc - 1])
        tph = run(p, "tensor", lambda e, qc=qc: e.matmul(
            PH[:, :], FC1[:, :], OS[:, qc * 512:qc * 512 + 512], start=True, stop=True), pe)
        wait(p, "scalar", pe, tph)
        wait(p, "scalar", lda, LDA)
        h1n.append(run(p, "scalar", lambda e, qc=qc: e.activation(
            H1[0:16, qc * 512:qc * 512 + 512], PH[:, :], AF.Relu, bias=FC1B[:, 0:1]), ac))
    yv = []
    for tt in range(NT):
        for dh in range(2):
            k = tt * 2 + dh
            bank = k % 2
            wait(p, "tensor", ac, h1n[tt // 4])
            wait(p, "tensor", va, von)
            wait(p, "tensor", gld, GLD)
            if k >= 2:
                wait(p, "tensor", ev, yv[k - 2])
            tpo = run(p, "tensor", lambda e, tt=tt, dh=dh, bank=bank: e.matmul(
                PO[bank][:, :], H1[0:17, tt * 128:(tt + 1) * 128],
                FC2[0:17, dh * 512:(dh + 1) * 512], start=True, stop=True), pe)
            wait(p, "vector", pe, tpo)
            yv.append(run(p, "vector", lambda e, tt=tt, dh=dh, bank=bank: e.tensor_copy(
                Y[:, tt * D + dh * 512:tt * D + (dh + 1) * 512], PO[bank][:, :]), ev))
    wait(p, "sync", ev, yv[-1])
    dma(p, "sync", st, tm_ap(y0a_o), Y[:, :])
    wait(p, "sync", st, st.n)
    return p



def emit_quant(p, sems, SRC, DST, W, QS, *, src_off=0, dst_off=0, dep=None, tick=None):
    va, ac = sems["va"], sems["ac"]
    NB = W // 16
    BM, SC, RS = QS["BM"], QS["SC"], QS["RS"]
    V, SG, Aq, Bq, Qq, Q2 = QS["V"], QS["SG"], QS["A"], QS["B"], QS["Q"], QS["Q2"]
    s3 = lambda ap, off, w: ap[:, off:off + w].rearrange("p (nb k) -> p nb k", k=16)
    if dep is not None:
        dep()
    t = run(p, "vector", lambda e: e.reduce_max(
        BM[:, 0:NB], s3(SRC, src_off, W), axis=AX, apply_absolute_value=True), va)
    wait(p, "vector", va, t)
    t = run(p, "vector", lambda e: e.tensor_scalar(
        SC[:, 0:NB], BM[:, 0:NB], 1.0 / 6.0, 1e-12, ALU.mult, ALU.max), va)
    wait(p, "vector", va, t)
    t = run(p, "vector", lambda e: e.reciprocal(RS[:, 0:NB], SC[:, 0:NB]), va)
    wait(p, "vector", va, t)
    rsb = lambda: bass.AP(RS, 0, [[RS.shape[1], 128], [1, NB], [0, 16]])
    t = run(p, "vector", lambda e: e.tensor_tensor(
        s3(V, 0, W), s3(SRC, src_off, W), rsb(), ALU.mult), va)
    wait(p, "scalar", va, t)
    tsg = run(p, "scalar", lambda e: e.activation(SG[:, 0:W], V[:, 0:W], AF.Sign), ac)
    tab = run(p, "scalar", lambda e: e.activation(Q2[:, 0:W], V[:, 0:W], AF.Abs), ac)
    wait(p, "vector", ac, tab)
    run(p, "vector", lambda e: e.tensor_scalar(Aq[:, 0:W], Q2[:, 0:W], 0.25, None, ALU.is_gt))
    run(p, "vector", lambda e: e.tensor_scalar(Bq[:, 0:W], Q2[:, 0:W], 0.75, None, ALU.is_gt))
    run(p, "vector", lambda e: e.tensor_add(Qq[:, 0:W], Aq[:, 0:W], Bq[:, 0:W]))
    run(p, "vector", lambda e: e.tensor_scalar(Aq[:, 0:W], Q2[:, 0:W], 1.25, None, ALU.is_gt))
    run(p, "vector", lambda e: e.tensor_scalar(Bq[:, 0:W], Q2[:, 0:W], 1.75, None, ALU.is_gt))
    run(p, "vector", lambda e: e.tensor_add(Aq[:, 0:W], Aq[:, 0:W], Bq[:, 0:W]))
    run(p, "vector", lambda e: e.tensor_add(Qq[:, 0:W], Qq[:, 0:W], Aq[:, 0:W]))
    run(p, "vector", lambda e: e.tensor_scalar(Aq[:, 0:W], Q2[:, 0:W], 2.5, None, ALU.is_gt))
    run(p, "vector", lambda e: e.tensor_scalar(Bq[:, 0:W], Q2[:, 0:W], 3.5, None, ALU.is_gt))
    run(p, "vector", lambda e: e.tensor_scalar_mul(Qq[:, 0:W], Qq[:, 0:W], 0.5))
    run(p, "vector", lambda e: e.tensor_add(Aq[:, 0:W], Aq[:, 0:W], Bq[:, 0:W]))
    run(p, "vector", lambda e: e.tensor_scalar(Bq[:, 0:W], Q2[:, 0:W], 5.0, None, ALU.is_gt))
    run(p, "vector", lambda e: e.tensor_add(Qq[:, 0:W], Qq[:, 0:W], Aq[:, 0:W]))
    t = run(p, "vector", lambda e: e.scalar_tensor_tensor(
        Qq[:, 0:W], Bq[:, 0:W], 2.0, Qq[:, 0:W], ALU.mult, ALU.add), va)
    wait(p, "vector", va, t)
    wait(p, "vector", ac, tsg)
    t = run(p, "vector", lambda e: e.tensor_mul(Qq[:, 0:W], Qq[:, 0:W], SG[:, 0:W]), va)
    wait(p, "vector", va, t)
    scb = lambda: bass.AP(SC, 0, [[SC.shape[1], 128], [1, NB], [0, 16]])
    t = run(p, "vector", lambda e: e.tensor_tensor(
        s3(DST, dst_off, W), s3(Qq, 0, W), scb(), ALU.mult), va if tick is None else tick)
    return t

def quant_scratch(p, W):
    return dict(BM=p.sbuf([128, W // 16], F32, None), SC=p.sbuf([128, W // 16], F32, None),
                RS=p.sbuf([128, W // 16], F32, None), V=p.sbuf([128, W], F32, None),
                SG=p.sbuf([128, W], BF16, None), A=p.sbuf([128, W], BF16, None),
                B=p.sbuf([128, W], BF16, None), Q=p.sbuf([128, W], BF16, None),
                Q2=p.sbuf([128, W], F32, None))


def build_stage_l0wq(ntiles, W, p=None, t=None):
    p = p or Prog("l0wq")
    nc = p.nc
    fz = t is not None and "wqt" in t
    w_i = _io(nc, t, "w", [ntiles * 128, W], F32, "ExternalInput")
    if fz:
        wqt = t["wqt"]
    else:
        w_o = _io(nc, t, "wq", [ntiles * 128, W], BF16, "ExternalOutput")
    SRC = p.sbuf([128, 2 * W], F32, "SRC")
    DST = p.sbuf([128, 2 * W], BF16, "DST")
    QS = quant_scratch(p, W)
    va = p.sem("va"); ac = p.sem("ac")
    ldp = [p.sem("ldp0"), p.sem("ldp1")]
    stp = [p.psem("stp0"), p.psem("stp1")]
    sems = dict(va=va, ac=ac)
    if fz:
        pe = p.sem("pe"); ev = p.sem("ev")
        PT = [p.psum([128, 128], BF16, f"pt{i}") for i in range(2)]
        TEV = p.sbuf([128, 2 * 128], BF16, "TEV")
        IDB = p.sbuf([128, 128], BF16, "IDB")
        ldi = dma(p, "sync", ldp[0], IDB[:, :], t["identbf"])
    qticks = []
    sdone = []
    ktr = 0
    trv = []
    for tt in range(ntiles):
        off = (tt % 2) * W
        if tt >= 2:
            wait(p, "sync", va, qticks[tt - 2])
        tl = dma(p, "sync", ldp[tt % 2], SRC[:, off:off + W], w_i[tt * 128:(tt + 1) * 128, :])
        def dep(tl=tl, tt=tt):
            wait(p, "vector", ldp[tt % 2], tl)
            if tt >= 2:
                wait(p, "vector", stp[0], sdone[tt - 2][0])
                wait(p, "vector", stp[1], sdone[tt - 2][1])
        tq = emit_quant(p, sems, SRC, DST, W, QS, src_off=off, dst_off=off, dep=dep)
        qticks.append(tq)
        if fz:
            for tc in range(W // 128):
                bank = ktr % 2
                wait(p, "tensor", va, tq)
                if ktr == 0:
                    wait(p, "tensor", ldp[0], ldi)
                if ktr >= 2:
                    wait(p, "tensor", ev, trv[ktr - 2])
                tp = run(p, "tensor", lambda e, off=off, tc=tc, bank=bank: e.transpose(
                    PT[bank][:, :], DST[:, off + tc * 128:off + (tc + 1) * 128],
                    IDB[:, :]), pe)
                wait(p, "vector", pe, tp)
                if ktr >= 2:
                    wait(p, "vector", stp[bank], 16 * (ktr // 2))
                trv.append(run(p, "vector", lambda e, bank=bank: e.tensor_copy(
                    TEV[:, bank * 128:bank * 128 + 128], PT[bank][:, :]), ev))
                wait(p, "sync", ev, trv[-1])
                dma(p, "sync", stp[bank], wqt[tc * 128:(tc + 1) * 128, tt * 128:(tt + 1) * 128],
                    TEV[:, bank * 128:bank * 128 + 128])
                ktr += 1
        else:
            wait(p, "sync", va, tq)
            dma(p, "sync", stp[tt % 2], w_o[tt * 128:(tt + 1) * 128, :], DST[:, off:off + W])
        sdone.append((stp[0].n, stp[1].n))
    wait(p, "sync", stp[0], stp[0].n)
    wait(p, "sync", stp[1], stp[1].n)
    return p


def build_stage_l0ffn(p=None, t=None):
    I = 2816
    NI = I // 128
    p = p or Prog("l0ffn")
    nc = p.nc
    xn_i = _io(nc, t, "xn", [T, D], F32, "ExternalInput")
    wags = t.get("w_ags") if t else None
    if wags is None:
        wg_i = _io(nc, t, "wgq", [D, I], BF16, "ExternalInput")
        wu_i = _io(nc, t, "wuq", [D, I], BF16, "ExternalInput")
        wd_i = _io(nc, t, "wdq", [I, D], BF16, "ExternalInput")
    identbf = _io(nc, t, "identbf", [128, 128], BF16, "ExternalInput")
    y_o = _io(nc, t, "y0f", [T, D], F32, "ExternalOutput")

    XNT = p.sbuf([128, D], F32, "XNT")
    XQ = p.sbuf([128, D], BF16, "XQ")
    XQF = p.sbuf([128, 2 * NC * 128], BF16, "XQF")
    WG = p.sbuf([128, NC * I], BF16, "WG")
    WU = p.sbuf([128, NC * I], BF16, "WU")
    WD = p.sbuf([128, NI * D], BF16, "WD")
    IDB = p.sbuf([128, 128], BF16, "IDB")
    GS = p.sbuf([128, 512], F32, "GS")
    HQ = p.sbuf([128, I], BF16, "HQ")
    HQF = p.sbuf([128, NI * 128], BF16, "HQF")
    YS = p.sbuf([128, D], F32, "YS")
    QS = quant_scratch(p, I)
    PSB = [p.psum([128, 512], F32, f"pp{i}") for i in range(2)]
    PT = [p.psum([128, 128], BF16, f"pt{i}") for i in range(2)]
    PY = [p.psum([128, 512], F32, f"py{i}") for i in range(2)]

    ldw = p.sem("ldw"); va = p.sem("va"); ac = p.sem("ac")
    pe = p.sem("pe"); ev = p.sem("ev"); st = p.psem("st")
    ldx = [p.sem("ldx0"), p.sem("ldx1")]
    sems = dict(va=va, ac=ac, pe=pe, ev=ev)

    dma(p, "sync", ldw, IDB[:, :], identbf)
    if wags is None:
        dma(p, "sync", ldw, WG[:, :], fm_ap(wg_i))
        dma(p, "sync", ldw, WU[:, :], fm_ap(wu_i))
        dma(p, "sync", ldw, WD[:, :], fm_ap(wd_i))
    else:
        wg_ag, wu_ag, wd_ag, per = wags
        offs = [0]
        for v in per:
            offs.append(offs[-1] + v)
        for ci in range(NC):
            for k in range(8):
                o0 = offs[k] * 128
                w_ = per[k] * 128
                dma(p, "sync", ldw, WG[:, ci * I + o0:ci * I + o0 + w_],
                    wg_ag[k * D + ci * 128:k * D + ci * 128 + 128, 0:w_])
                dma(p, "sync", ldw, WU[:, ci * I + o0:ci * I + o0 + w_],
                    wu_ag[k * D + ci * 128:k * D + ci * 128 + 128, 0:w_])
        for ic in range(NI):
            for k in range(8):
                dma(p, "sync", ldw, WD[:, ic * D + k * 128:ic * D + (k + 1) * 128],
                    wd_ag[k * I + ic * 128:k * I + ic * 128 + 128, :])
    LDW = ldw.n

    ICH = [512] * 5 + [256]
    prev_tq = [None]
    for tt in range(NT):
        off = 0
        xoff = (tt % 2) * NC * 128
        if prev_tq[0] is not None:
            wait(p, "sync", va, prev_tq[0])
        tl = dma(p, "sync", ldx[tt % 2], XNT[:, off:off + D], xn_i[tt * 128:(tt + 1) * 128, :])
        def dep(tl=tl, tt=tt):
            wait(p, "vector", ldx[tt % 2], tl)
            if tt >= 1:
                wait(p, "vector", pe, pe.n)
        tq = emit_quant(p, sems, XNT, XQ, D, QS, src_off=off, dst_off=off, dep=dep)
        prev_tq[0] = tq
        tps = []
        for c in range(NC):
            wait(p, "tensor", va, tq)
            if c == 0:
                wait(p, "tensor", ldw, LDW)
            bank = c % 2
            if c >= 2:
                wait(p, "tensor", ev, tps[c - 2])
            tp = run(p, "tensor", lambda e, c=c, bank=bank, off=off: e.transpose(
                PT[bank][:, :], XQ[:, off + c * 128:off + (c + 1) * 128], IDB[:, :]), pe)
            wait(p, "vector", pe, tp)
            tps.append(run(p, "vector", lambda e, c=c, bank=bank, xoff=xoff: e.tensor_copy(
                XQF[:, xoff + c * 128:xoff + (c + 1) * 128], PT[bank][:, :]), ev))
        V = QS["V"]
        th = None
        io0 = 0
        for icn, cw in enumerate(ICH):
            wait(p, "tensor", ldw, LDW)
            wait(p, "tensor", ev, tps[-1])
            if icn > 0:
                wait(p, "tensor", va, th)
                wait(p, "tensor", ac, ac.n)
            for hc in range(NC):
                run(p, "tensor", lambda e, hc=hc, io0=io0, cw=cw, xoff=xoff:
                    e.matmul(PSB[0][:, 0:cw],
                             XQF[:, xoff + hc * 128:xoff + (hc + 1) * 128],
                             WG[:, hc * I + io0:hc * I + io0 + cw],
                             start=(hc == 0), stop=(hc == NC - 1)),
                    pe if hc == NC - 1 else None)
            tg = pe.n
            for hc in range(NC):
                run(p, "tensor", lambda e, hc=hc, io0=io0, cw=cw, xoff=xoff:
                    e.matmul(PSB[1][:, 0:cw],
                             XQF[:, xoff + hc * 128:xoff + (hc + 1) * 128],
                             WU[:, hc * I + io0:hc * I + io0 + cw],
                             start=(hc == 0), stop=(hc == NC - 1)),
                    pe if hc == NC - 1 else None)
            tu = pe.n
            wait(p, "scalar", pe, tg)
            tsg_ = run(p, "scalar", lambda e, cw=cw: e.activation(
                GS[:, 0:cw], PSB[0][:, 0:cw], AF.Silu), ac)
            wait(p, "vector", pe, tu)
            wait(p, "vector", ac, tsg_)
            th = run(p, "vector", lambda e, io0=io0, cw=cw: e.scalar_tensor_tensor(
                V[:, io0:io0 + cw], PSB[1][:, 0:cw], 1.0, GS[:, 0:cw],
                ALU.mult, ALU.mult), va)
            io0 += cw
        def dep2(th=th):
            wait(p, "vector", va, th)
        tqh = emit_quant(p, sems, V, HQ, I, QS, dep=dep2)
        tps2 = []
        for c in range(NI):
            wait(p, "tensor", va, tqh)
            bank = c % 2
            if c >= 2:
                wait(p, "tensor", ev, tps2[c - 2])
            tp = run(p, "tensor", lambda e, c=c, bank=bank: e.transpose(
                PT[bank][:, :], HQ[:, c * 128:(c + 1) * 128], IDB[:, :]), pe)
            wait(p, "vector", pe, tp)
            tps2.append(run(p, "vector", lambda e, c=c, bank=bank: e.tensor_copy(
                HQF[:, c * 128:(c + 1) * 128], PT[bank][:, :]), ev))
        yev = []
        for dh in range(2):
            bank = dh % 2
            wait(p, "tensor", ev, tps2[-1])
            wait(p, "tensor", ldw, LDW)

            for icn in range(NI):
                run(p, "tensor", lambda e, icn=icn, dh=dh, bank=bank:
                    e.matmul(PY[bank][:, :],
                             HQF[:, icn * 128:(icn + 1) * 128],
                             WD[:, icn * D + dh * 512:icn * D + dh * 512 + 512],
                             start=(icn == 0), stop=(icn == NI - 1)),
                    pe if icn == NI - 1 else None)
            wait(p, "vector", pe, pe.n)
            if tt > 0 and dh == 0:
                wait(p, "vector", st, 16 * tt)
            yev.append(run(p, "vector", lambda e, dh=dh, bank=bank: e.tensor_copy(
                YS[:, dh * 512:dh * 512 + 512], PY[bank][:, :]), ev))
        wait(p, "sync", ev, yev[-1])
        dma(p, "sync", st, y_o[tt * 128:(tt + 1) * 128, :], YS[:, :])
    wait(p, "sync", st, st.n)
    return p



def build_stage_l1qkv(zn_scale=False, rope=True, name="l1qkv", p=None, t=None):
    p = p or Prog(name)
    nc = p.nc
    xnfm_i = _io(nc, t, "xnfm", [D, T], BF16, "ExternalInput")
    wq_i = _io(nc, t, "wq", [D, D], F32, "ExternalInput")
    wk_i = _io(nc, t, "wk", [D, D], F32, "ExternalInput")
    wv_i = _io(nc, t, "wv", [D, D], F32, "ExternalInput")
    bq_i = _io(nc, t, "bq", [128, 8], F32, "ExternalInput")
    bk_i = _io(nc, t, "bk", [128, 8], F32, "ExternalInput")
    bv_i = _io(nc, t, "bv", [1, D], F32, "ExternalInput")
    if rope:
        cos_i = _io(nc, t, "cosr", [128, T], BF16, "ExternalInput")
        sin_i = _io(nc, t, "sinr", [128, T], BF16, "ExternalInput")
        rp_i = _io(nc, t, "rotperm", [128, 128], BF16, "ExternalInput")
    if zn_scale:
        lnw_i = _io(nc, t, "lnw", [128, 8], F32, "ExternalInput")
        lnb_i = _io(nc, t, "lnb", [128, 8], F32, "ExternalInput")
    qr_o = _io(nc, t, "qr", [D, T], BF16, "ExternalOutput")
    kr_o = _io(nc, t, "kr", [D, T], BF16, "ExternalOutput")
    vem = t.get("vemode") if t else None
    if vem is None:
        v_o = _io(nc, t, "v", [T, D], BF16, "ExternalOutput")

    XF = p.sbuf([128, NC * T], BF16, "XF")
    QF = p.sbuf([128, NC * T], BF16, "QF")
    QR = p.sbuf([128, NC * T], BF16, "QR")
    if rope:
        SCA = p.sbuf([128, 1024], F32, "SCA")
        RP = p.sbuf([128, 128], BF16, "RP")
        COSR = p.sbuf([128, T], BF16, "COSR")
        SINR = p.sbuf([128, T], BF16, "SINR")
    if zn_scale:
        LNW = p.sbuf([128, 8], F32, "LNW")
        LNB = p.sbuf([128, 8], F32, "LNB")
    BQ = p.sbuf([128, 8], F32, "BQ")
    BK = p.sbuf([128, 8], F32, "BK")
    BV = p.sbuf([1, D], BF16, "BV")
    ONESR = p.sbuf([1, 128], BF16, "ONESR")
    VT = p.sbuf([128, 2 * 512], BF16, "VT")
    PSB = [p.psum([128, 512], F32, f"pp{i}") for i in range(4)]
    lda = p.sem("lda"); gld = p.sem("gld"); va = p.sem("va"); ac = p.sem("ac")
    pe = p.sem("pe"); ev = p.sem("ev"); st = p.psem("st")
    stv = [p.psem("stv0"), p.psem("stv1")]
    sems = dict(va=va, ac=ac, pe=pe, ev=ev, psumpool=PSB)

    dma(p, "sync", lda, XF[:, :], fm_ap(xnfm_i))
    if rope:
        dma(p, "sync", lda, COSR[:, :], cos_i)
        dma(p, "sync", lda, SINR[:, :], sin_i)
    if zn_scale:
        dma(p, "sync", lda, LNW[:, :], lnw_i)
        dma(p, "sync", lda, LNB[:, :], lnb_i)
    dma(p, "sync", lda, BQ[:, :], bq_i)
    dma(p, "sync", lda, BK[:, :], bk_i)
    LDA = lda.n
    dma(p, "gpsimd", gld, BV[:, :], bv_i)
    if rope:
        dma(p, "gpsimd", gld, RP[:, :], rp_i)
    GLD = gld.n
    WQ, wsq = load_w_fm(p, None, wq_i, D, D, "WQ")
    WK, wsk = load_w_fm(p, None, wk_i, D, D, "WK")
    WV, wsv = load_w_fm(p, None, wv_i, D, D, "WV")
    run(p, "vector", lambda e: e.memset(ONESR[:, :], 1.0))

    def _rope_perm_fm(SRC, DST, deps_ev, outs):
        for c in range(NC):
            deps_ev(c)
            for hf in range(2):
                sl = slice(c * T + hf * 1024, c * T + (hf + 1) * 1024)
                bank = (c * 2 + hf) % 2
                wait(p, "tensor", gld, GLD)
                if c * 2 + hf >= 2:
                    wait(p, "tensor", va, outs[-2] if len(outs) >= 2 else outs[-1])
                tm_ = run(p, "tensor", lambda e, sl=sl, bank=bank: e.matmul(
                    PSB[bank][:, 0:512], RP[:, :], SRC[:, sl.start:sl.start + 512],
                    start=True, stop=True), pe)
                tm2 = run(p, "tensor", lambda e, sl=sl, bank=bank: e.matmul(
                    PSB[2 + bank][:, 0:512], RP[:, :], SRC[:, sl.start + 512:sl.start + 1024],
                    start=True, stop=True), pe)
                run(p, "vector", lambda e, sl=sl: e.tensor_mul(
                    DST[:, sl], SRC[:, sl], COSR[:, (sl.start % T):(sl.start % T) + 1024]))
                wait(p, "vector", pe, tm2)
                run(p, "vector", lambda e, sl=sl, bank=bank: e.tensor_mul(
                    SCA[:, 0:512], PSB[bank][:, 0:512], SINR[:, (sl.start % T):(sl.start % T) + 512]))
                run(p, "vector", lambda e, sl=sl, bank=bank: e.tensor_mul(
                    SCA[:, 512:1024], PSB[2 + bank][:, 0:512],
                    SINR[:, (sl.start % T) + 512:(sl.start % T) + 1024]))
                run(p, "vector", lambda e, sl=sl: e.tensor_add(
                    DST[:, sl.start:sl.start + 512], DST[:, sl.start:sl.start + 512],
                    SCA[:, 0:512]))
                outs.append(run(p, "vector", lambda e, sl=sl: e.tensor_add(
                    DST[:, sl.start + 512:sl.start + 1024], DST[:, sl.start + 512:sl.start + 1024],
                    SCA[:, 512:1024]), va))

    if zn_scale:
        wait(p, "vector", lda, LDA)
        zt = None
        for c in range(NC):
            zt = run(p, "vector", lambda e, c=c: e.tensor_scalar(
                XF[:, c * T:(c + 1) * T], XF[:, c * T:(c + 1) * T],
                LNW[:, c:c + 1], LNB[:, c:c + 1], ALU.mult, ALU.add), va)
        wait(p, "tensor", va, zt)
        wait(p, "scalar", va, zt)
    wait(p, "vector", lda, LDA)
    qo = emit_proj_fm(p, sems, XF, WQ, QF, D, bias_col=BQ, pre=[(lda, LDA), (wsq, wsq.n)],
                      evac_eng="vector")
    qrv = []
    if rope:
        def qdeps(c, qo=qo):
            wait(p, "vector", ev, qo[c * 4 + 3])
            wait(p, "tensor", ev, qo[-1])
        _rope_perm_fm(QF, QR, qdeps, qrv)
        wait(p, "sync", va, qrv[-1])
        dma(p, "sync", st, fm_ap(qr_o), QR[:, :])
    else:
        qrv = [va.tick(0)]
        wait(p, "sync", ev, qo[-1])
        dma(p, "sync", st, fm_ap(qr_o), QF[:, :])
    def kevw(k, qrv=qrv):
        if qrv and qrv[-1]:
            wait(p, "vector", va, qrv[-1])
        wait(p, "vector", st, 16)
    def kpre(k, qrv=qrv):
        if k == 0 and rope:
            wait(p, "tensor", va, qrv[-1])
    ko = emit_proj_fm(p, sems, XF, WK, QF, D, bias_col=BK, pre=[(lda, LDA), (wsk, wsk.n)],
                      evac_eng="vector", evac_wait=kevw, extra_wait=kpre)
    krv = []
    if rope:
        def kdeps(c, ko=ko):
            wait(p, "vector", ev, ko[c * 4 + 3])
            wait(p, "tensor", ev, ko[-1])
            wait(p, "vector", st, 16)
        _rope_perm_fm(QF, QR, kdeps, krv)
        wait(p, "sync", va, krv[-1])
        dma(p, "sync", st, fm_ap(kr_o), QR[:, :])
    else:
        krv = [va.tick(0)]
        wait(p, "sync", ev, ko[-1])
        dma(p, "sync", st, fm_ap(kr_o), QF[:, :])
    vev = []
    if rope:
        wait(p, "tensor", va, krv[-1])
    for tt in range(NT):
        for dh in range(2):
            k = tt * 2 + dh
            bank = k % 2 + 2
            wait(p, "tensor", gld, GLD)
            wait(p, "tensor", wsv, wsv.n)
            wait(p, "tensor", lda, LDA)
            if k >= 2:
                wait(p, "tensor", ev, vev[k - 2])
            for ci in range(NC):
                run(p, "tensor", lambda e, ci=ci, tt=tt, dh=dh, bank=bank:
                    e.matmul(PSB[bank][:, :],
                             XF[:, ci * T + tt * 128:ci * T + (tt + 1) * 128],
                             WV[:, ci * D + dh * 512:ci * D + dh * 512 + 512],
                             start=(ci == 0), stop=False), None)
            tv = run(p, "tensor", lambda e, tt=tt, dh=dh, bank=bank:
                e.matmul(PSB[bank][:, :], ONESR[0:1, 0:128],
                         BV[0:1, dh * 512:dh * 512 + 512], start=False, stop=True), pe)
            wait(p, "vector", pe, tv)
            if k >= 2:
                wait(p, "vector", stv[k % 2], 16 * (k // 2))
            vev.append(run(p, "vector", lambda e, k=k, bank=bank: e.tensor_copy(
                VT[:, (k % 2) * 512:(k % 2) * 512 + 512], PSB[bank][:, :]), ev))
            wait(p, "sync", ev, vev[-1])
            if vem is None:
                dma(p, "sync", stv[k % 2], v_o[tt * 128:(tt + 1) * 128, dh * 512:dh * 512 + 512],
                    VT[:, (k % 2) * 512:(k % 2) * 512 + 512])
            elif vem[0] == "ve16":
                dma(p, "sync", stv[k % 2],
                    bass.AP(vem[1], tt * 128 * 1040 + dh * 520, [[1040, 128], [65, 8], [1, 64]]),
                    VT[:, (k % 2) * 512:(k % 2) * 512 + 512])
            else:
                dma(p, "sync", stv[k % 2],
                    bass.AP(vem[1], dh * T * 520 + tt * 128 * 520, [[520, 128], [65, 8], [1, 64]]),
                    VT[:, (k % 2) * 512:(k % 2) * 512 + 512])
    wait(p, "sync", st, st.n)
    wait(p, "sync", stv[0], stv[0].n)
    wait(p, "sync", stv[1], stv[1].n)
    return p



def build_stage_mha(*, causal, scale, name="mha", nheads=16, p=None, t=None):
    p = p or Prog(name)
    nc = p.nc
    NKT = S // 128
    qr_i = _io(nc, t, "qr", [D, T], BF16, "ExternalInput")
    kag = t.get("krf_ag") if t else None
    if kag is None:
        krf_i = _io(nc, t, "krf", [D, S], BF16, "ExternalInput")
    vef_i = _io(nc, t, "vef", [S, nheads * 65], BF16, "ExternalInput")
    wo_i = _io(nc, t, "wo", [D, D], F32, "ExternalInput")
    bo_i = _io(nc, t, "bo", [1, D], F32, "ExternalInput")
    if causal:
        bcn_i = _io(nc, t, "bcn", [128, 4 * 32], F32, "ExternalInput")
        smsk2_i = _io(nc, t, "smsk2", [128, 8 * 512], BF16, "ExternalInput")
    ya_o = _io(nc, t, "ya", [T, D], F32, "ExternalOutput")
    VEF = p.sbuf([128, NKT * nheads * 65], BF16, "VEF")
    if causal:
        SMSK = p.sbuf([128, 8 * 512], BF16, "SMSK")
        BCN = p.sbuf([128, 4 * 32], F32, "BCN")
    QE = p.sbuf([65, 2 * T], BF16, "QE")
    KE = p.sbuf([65, 2 * S], BF16, "KE")
    SQ = p.sbuf([64, S], BF16, "SQ")
    QN = p.sbuf([1, T], F32, "QN")
    KN = p.sbuf([1, S], F32, "KN")
    KM = p.sbuf([1, 1], F32, "KM")
    MQ = p.sbuf([1, T], F32, "MQ")
    ONE64 = p.sbuf([64, 1], BF16, "ONE64")
    ONEB = p.sbuf([128, 64], F32, "ONEB")
    EXPT = p.sbuf([128, 2 * 1024], BF16, "EXPT")
    SCR = p.sbuf([128, 1024], F32, "SCR")
    AFM = p.sbuf([65, 512], F32, "AFM")
    RDS = p.sbuf([128, 512], F32, "RDS")
    AO = p.sbuf([128, NC * T], BF16, "AO")
    BO = p.sbuf([1, D], BF16, "BO")
    YS = p.sbuf([128, 2 * 512], F32, "YS")
    ONES1 = p.sbuf([1, 128], BF16, "ONES1")
    PSC = [p.psum([128, 1024], F32, f"psc{i}") for i in range(2)]
    PAV = p.psum([65, 512], F32, "pav")
    PY = [p.psum([128, 512], F32, f"py{i}") for i in range(2)]
    PQN = PY[0]
    PBC = PY[1]

    lda = p.sem("lda"); gld = p.sem("gld"); va = p.sem("va"); ac = p.sem("ac")
    pe = p.sem("pe"); ev = p.sem("ev"); st = p.psem("st")
    hl = [p.sem("hl0"), p.sem("hl1")]
    sty = [p.psem("sty0"), p.psem("sty1")]

    dma(p, "sync", lda, VEF[:, :], vef_i.rearrange("(r p) n -> p r n", p=128))
    if causal:
        dma(p, "sync", lda, SMSK[:, :], smsk2_i)
        dma(p, "sync", lda, BCN[:, :], bcn_i)
    LDA = lda.n
    dma(p, "gpsimd", gld, BO[:, :], bo_i)
    GLD = gld.n
    WO, wso = load_w_fm(p, None, wo_i, D, D, "WO")
    run(p, "vector", lambda e: e.memset(ONE64[:, :], 1.0))
    run(p, "vector", lambda e: e.memset(ONES1[:, :], 1.0))
    t_ones = run(p, "vector", lambda e: e.memset(ONEB[64:128, :], 1.0), va)

    t_hl = {}
    def load_head(h):
        boq = (h % 2) * T
        bok = (h % 2) * S
        dma(p, "sync", hl[h % 2], QE[0:64, boq:boq + T], qr_i[h * 64:(h + 1) * 64, :])
        if kag is None:
            t_hl[h] = dma(p, "sync", hl[h % 2], KE[0:64, bok:bok + S], krf_i[h * 64:(h + 1) * 64, :])
        else:
            dma(p, "sync", hl[h % 2], KE[0:64, bok:bok + T], kag[h * 64:h * 64 + 64, :])
            t_hl[h] = dma(p, "sync", hl[h % 2], KE[0:64, bok + T:bok + S],
                          kag[D + h * 64:D + h * 64 + 64, :])
    load_head(0)
    load_head(1)

    aov = []
    headdone = []
    for h in range(nheads):
        boq = (h % 2) * T
        bok = (h % 2) * S
        wait(p, "scalar", hl[h % 2], t_hl[h])
        tsq = run(p, "scalar", lambda e, bok=bok: e.activation(
            SQ[:, 0:S], KE[0:64, bok:bok + S], AF.Square), ac)
        ksm = []
        kevs = []
        for rc in range(S // 512):
            wait(p, "tensor", ac, tsq)
            wait(p, "tensor", va, t_ones)
            if kevs:
                wait(p, "tensor", ev, kevs[-1])
            ksm.append(run(p, "tensor", lambda e, rc=rc: e.matmul(
                PQN[0:1, 0:512], ONE64[:, :], SQ[:, rc * 512:(rc + 1) * 512],
                start=True, stop=True), pe))
            wait(p, "vector", pe, ksm[-1])
            kevs.append(run(p, "vector", lambda e, rc=rc: e.tensor_copy(
                KN[:, rc * 512:(rc + 1) * 512], PQN[0:1, 0:512]), ev))
        wait(p, "vector", ev, ev.n)
        tkm = run(p, "vector", lambda e: e.reduce_max(KM[:, :], KN[:, :], axis=AX), va)
        wait(p, "scalar", pe, ksm[-1])
        tsq2 = run(p, "scalar", lambda e, boq=boq: e.activation(
            SQ[:, 0:T], QE[0:64, boq:boq + T], AF.Square), ac)
        qsm = []
        for tc in range(T // 512):
            wait(p, "tensor", ac, tsq2)
            if kevs:
                wait(p, "tensor", ev, kevs[-1])
            qsm.append(run(p, "tensor", lambda e, tc=tc: e.matmul(
                PQN[0:1, 0:512], ONE64[:, :], SQ[:, tc * 512:(tc + 1) * 512],
                start=True, stop=True), pe))
            wait(p, "vector", pe, qsm[-1])
            kevs.append(run(p, "vector", lambda e, tc=tc: e.tensor_copy(
                QN[:, tc * 512:(tc + 1) * 512], PQN[0:1, 0:512]), ev))
        wait(p, "vector", ev, ev.n)
        wait(p, "vector", va, tkm)
        t1 = run(p, "vector", lambda e: e.tensor_scalar(
            MQ[:, :], QN[:, :], KM[0:1, 0:1], None, ALU.mult), va)
        wait(p, "scalar", va, t1)
        t2 = run(p, "scalar", lambda e: e.activation(MQ[:, :], MQ[:, :], AF.Sqrt), ac)
        wait(p, "vector", ac, t2)
        t3 = run(p, "vector", lambda e, boq=boq: e.tensor_scalar(
            QE[64:65, boq:boq + T], MQ[:, :], -1.0, None, ALU.mult), va)
        t4 = run(p, "vector", lambda e, bok=bok: e.memset(KE[64:65, bok:bok + S], 1.0), va)
        for qc in range(4):
            slotp = ([2 * qc, 2 * qc + 1, 8 + 2 * qc, 8 + 2 * qc + 1]) if causal else []
            expn = []
            pvnp = []

            def emit_pv(kp, expn=expn, pvnp=pvnp, h=h):
                wait(p, "tensor", ac, expn[kp])
                if kp == 0:
                    wait(p, "tensor", ev, ev.n)
                tpv = None
                for j2 in range(2):
                    kt = 2 * kp + j2
                    tpv = run(p, "tensor", lambda e, kt=kt, h=h, kp=kp, j2=j2: e.matmul(
                        PAV[:, :], VEF[:, kt * nheads * 65 + h * 65:kt * nheads * 65 + (h + 1) * 65],
                        EXPT[:, (kp % 2) * 1024 + j2 * 512:(kp % 2) * 1024 + (j2 + 1) * 512],
                        start=(kt == 0), stop=(kt == NKT - 1)), pe)
                pvnp.append(tpv)

            for kp in range(NKT // 2):
                bank = kp % 2
                wait(p, "tensor", va, max(t3, t4))
                wait(p, "tensor", hl[h % 2], t_hl[h])
                wait(p, "tensor", lda, LDA)
                if len(expn) >= 2:
                    wait(p, "tensor", ac, expn[-2])
                for j2 in range(2):
                    kt = 2 * kp + j2
                    run(p, "tensor", lambda e, kt=kt, qc=qc, bank=bank, j2=j2, bok=bok, boq=boq:
                        e.matmul(PSC[bank][:, j2 * 512:(j2 + 1) * 512],
                                 KE[0:65, bok + kt * 128:bok + (kt + 1) * 128],
                                 QE[0:65, boq + qc * 512:boq + qc * 512 + 512],
                                 start=True, stop=True), pe)
                if pvnp:
                    wait(p, "scalar", pe, pvnp[-1])
                bcol = qc * 32 + 2 * kp
                if causal and kp in slotp:
                    wait(p, "vector", pe, pe.n)
                    if expn:
                        wait(p, "vector", ac, expn[-1])
                    wait(p, "vector", lda, LDA)
                    for j2 in range(2):
                        kt = 2 * kp + j2
                        sj = (kt - qc * 4) if kt < 16 else 4 + (kt - 16 - qc * 4)
                        tvm = run(p, "vector", lambda e, bank=bank, sj=sj, j2=j2: e.tensor_add(
                            SCR[:, j2 * 512:(j2 + 1) * 512],
                            PSC[bank][:, j2 * 512:(j2 + 1) * 512],
                            SMSK[:, sj * 512:(sj + 1) * 512]), va)
                    wait(p, "scalar", va, tvm)
                    expn.append(run(p, "scalar", lambda e, kp=kp, bcol=bcol: e.activation(
                        EXPT[:, (kp % 2) * 1024:(kp % 2) * 1024 + 1024], SCR[:, :], AF.Exp,
                        scale=scale, bias=BCN[:, bcol:bcol + 1]), ac))
                elif causal:
                    wait(p, "scalar", pe, pe.n)
                    expn.append(run(p, "scalar", lambda e, kp=kp, bank=bank, bcol=bcol:
                        e.activation(
                            EXPT[:, (kp % 2) * 1024:(kp % 2) * 1024 + 1024], PSC[bank][:, :],
                            AF.Exp, scale=scale, bias=BCN[:, bcol:bcol + 1]), ac))
                else:
                    wait(p, "scalar", pe, pe.n)
                    expn.append(run(p, "scalar", lambda e, kp=kp, bank=bank: e.activation(
                        EXPT[:, (kp % 2) * 1024:(kp % 2) * 1024 + 1024], PSC[bank][:, :],
                        AF.Exp, scale=scale), ac))
                if kp >= 1:
                    emit_pv(kp - 1)
            emit_pv(NKT // 2 - 1)
            wait(p, "vector", pe, pvnp[-1])
            if aov:
                wait(p, "vector", va, aov[-1])
            te = run(p, "vector", lambda e: e.tensor_copy(AFM[:, :], PAV[:, :]), ev)
            wait(p, "vector", ev, te)
            tr = run(p, "vector", lambda e: e.reciprocal(RDS[64:65, :], AFM[64:65, :]), va)
            wait(p, "tensor", va, tr)
            if aov:
                wait(p, "tensor", va, aov[-1])
            tb = run(p, "tensor", lambda e: e.matmul(
                PBC[0:64, 0:512], ONEB[64:65, 0:64], RDS[64:65, :], start=True, stop=True), pe)
            wait(p, "vector", pe, tb)
            aov.append(run(p, "vector", lambda e, h=h, qc=qc: e.tensor_mul(
                AO[(h % 2) * 64:(h % 2) * 64 + 64,
                   (h // 2) * T + qc * 512:(h // 2) * T + qc * 512 + 512],
                AFM[0:64, :], PBC[0:64, 0:512]), va))
        if h + 2 < nheads:
            wait(p, "sync", pe, pe.n)
            load_head(h + 2)
    yv = []
    wait(p, "tensor", va, aov[-1])
    for tt in range(NT):
        for dc in range(2):
            k = tt * 2 + dc
            bank = k % 2
            wait(p, "tensor", gld, GLD)
            wait(p, "tensor", wso, wso.n)
            if k >= 2:
                wait(p, "tensor", ev, yv[k - 2])
            for ci in range(NC):
                run(p, "tensor", lambda e, ci=ci, tt=tt, dc=dc, bank=bank:
                    e.matmul(PY[bank][:, :],
                             AO[:, ci * T + tt * 128:ci * T + (tt + 1) * 128],
                             WO[:, ci * D + dc * 512:ci * D + dc * 512 + 512],
                             start=(ci == 0), stop=False), None)
            tv = run(p, "tensor", lambda e, dc=dc, bank=bank:
                e.matmul(PY[bank][:, :], ONES1[0:1, 0:128],
                         BO[0:1, dc * 512:dc * 512 + 512], start=False, stop=True), pe)
            wait(p, "vector", pe, tv)
            if k >= 2:
                wait(p, "vector", sty[k % 2], 16 * (k // 2))
            yv.append(run(p, "vector", lambda e, k=k, bank=bank: e.tensor_copy(
                YS[:, (k % 2) * 512:(k % 2) * 512 + 512], PY[bank][:, :]), ev))
            wait(p, "sync", ev, yv[-1])
            dma(p, "sync", sty[k % 2], ya_o[tt * 128:(tt + 1) * 128, dc * 512:dc * 512 + 512],
                YS[:, (k % 2) * 512:(k % 2) * 512 + 512])
    wait(p, "sync", sty[0], sty[0].n)
    wait(p, "sync", sty[1], sty[1].n)
    return p



def build_stage_l1ffn(p=None, t=None):
    I = 2048
    NI = I // 128
    p = p or Prog("l1ffn")
    nc = p.nc
    xnfm_i = _io(nc, t, "xnfm", [D, T], BF16, "ExternalInput")
    sgw_i = _io(nc, t, "sgw", [D, I], F32, "ExternalInput")
    suw_i = _io(nc, t, "suw", [D, I], F32, "ExternalInput")
    sdw_i = _io(nc, t, "sdw", [I, D], F32, "ExternalInput")
    rw_i = _io(nc, t, "rw", [D, 8], F32, "ExternalInput")
    rb_i = _io(nc, t, "rb", [1, 8], F32, "ExternalInput")
    wg_i = _io(nc, t, "wg", [8, D, I], F32, "ExternalInput")
    wu_i = _io(nc, t, "wu", [8, D, I], F32, "ExternalInput")
    wd_i = _io(nc, t, "wd", [8, I, D], F32, "ExternalInput")
    y_o = _io(nc, t, "y1f", [T, D], F32, "ExternalOutput")

    XF = p.sbuf([128, NC * T], BF16, "XF")
    WGB = p.sbuf([128, NC * I], BF16, "WGB")
    WUB = p.sbuf([128, NC * I], BF16, "WUB")
    WDB = p.sbuf([128, NI * D], BF16, "WDB")
    RW = p.sbuf([128, NC * 8], BF16, "RW")
    RB = p.sbuf([1, 8], BF16, "RB")
    ONES1 = p.sbuf([1, 128], BF16, "ONES1")
    PL = p.sbuf([128, NT * 8], F32, "PL")
    WE = p.sbuf([128, NT * 8], F32, "WE")
    M1 = p.sbuf([128, NT], F32, "M1")
    M2 = p.sbuf([128, NT], F32, "M2")
    EQ = p.sbuf([128, 8], F32, "EQ")
    P2 = p.sbuf([128, 8], F32, "P2")
    WS = p.sbuf([128, NT], F32, "WS")
    RWS = p.sbuf([128, NT], F32, "RWS")
    GS = p.sbuf([128, 512], F32, "GS")
    HFM = p.sbuf([128, NI * 256], BF16, "HFM")
    YACC = p.sbuf([128, NT * D], F32, "YACC")
    PSA = p.psum([128, 512], F32, "psa")
    PSB2 = p.psum([128, 512], F32, "psb2")
    PR = p.psum([128, 8], F32, "pr")
    PY = [p.psum([128, 512], F32, f"py{i}") for i in range(2)]

    lda = p.sem("lda"); va = p.sem("va"); ac = p.sem("ac")
    pe = p.sem("pe"); ev = p.sem("ev"); st = p.psem("st")
    ldw = [p.sem("ldw0"), p.sem("ldw1")]

    dma(p, "sync", lda, XF[:, :], fm_ap(xnfm_i))
    gl = p.sem("gl")
    dma(p, "gpsimd", gl, RW[:, :], fm_ap(rw_i))
    dma(p, "gpsimd", gl, RB[:, :], rb_i)
    GL = gl.n
    LDA = lda.n
    run(p, "vector", lambda e: e.memset(ONES1[:, :], 1.0))

    rtev = []
    for tt in range(NT):
        wait(p, "tensor", lda, LDA)
        wait(p, "tensor", gl, GL)
        if rtev:
            wait(p, "tensor", ac, rtev[-1])
        for ci in range(NC):
            run(p, "tensor", lambda e, ci=ci, tt=tt: e.matmul(
                PR[:, :], XF[:, ci * T + tt * 128:ci * T + (tt + 1) * 128],
                RW[:, ci * 8:(ci + 1) * 8], start=(ci == 0), stop=False), None)
        tb = run(p, "tensor", lambda e, tt=tt: e.matmul(
            PR[:, :], ONES1[0:1, 0:128], RB[0:1, :], start=False, stop=True), pe)
        wait(p, "scalar", pe, tb)
        rtev.append(run(p, "scalar", lambda e, tt=tt: e.activation(
            PL[:, tt * 8:(tt + 1) * 8], PR[:, :], AF.Sigmoid), ac))
    wev = []
    for tt in range(NT):
        wait(p, "vector", ac, rtev[tt])
        t1 = run(p, "vector", lambda e, tt=tt: e.reduce_max(
            M1[:, tt:tt + 1], PL[:, tt * 8:(tt + 1) * 8], axis=AX), va)
        wait(p, "vector", va, t1)
        teq = run(p, "vector", lambda e, tt=tt: e.tensor_scalar(
            EQ[:, :], PL[:, tt * 8:(tt + 1) * 8], M1[:, tt:tt + 1], -1e9, ALU.is_ge, ALU.mult), va)
        wait(p, "vector", va, teq)
        t2 = run(p, "vector", lambda e, tt=tt: e.tensor_add(
            P2[:, :], PL[:, tt * 8:(tt + 1) * 8], EQ[:, :]), va)
        wait(p, "vector", va, t2)
        t3 = run(p, "vector", lambda e, tt=tt: e.reduce_max(
            M2[:, tt:tt + 1], P2[:, :], axis=AX), va)
        wait(p, "vector", va, t3)
        t4 = run(p, "vector", lambda e, tt=tt: e.tensor_add(
            WS[:, tt:tt + 1], M1[:, tt:tt + 1], M2[:, tt:tt + 1]), va)
        wait(p, "vector", va, t4)
        t5 = run(p, "vector", lambda e, tt=tt: e.reciprocal(
            RWS[:, tt:tt + 1], WS[:, tt:tt + 1]), va)
        wait(p, "vector", va, t5)
        teq2 = run(p, "vector", lambda e, tt=tt: e.tensor_scalar(
            EQ[:, :], PL[:, tt * 8:(tt + 1) * 8], M2[:, tt:tt + 1], None, ALU.is_ge), va)
        wait(p, "vector", va, teq2)
        t6 = run(p, "vector", lambda e, tt=tt: e.tensor_mul(
            P2[:, :], PL[:, tt * 8:(tt + 1) * 8], EQ[:, :]), va)
        wait(p, "vector", va, t6)
        wev.append(run(p, "vector", lambda e, tt=tt: e.tensor_scalar(
            WE[:, tt * 8:(tt + 1) * 8], P2[:, :], RWS[:, tt:tt + 1], None, ALU.mult), va))

    def load_expert(e):
        s = ldw[(e + 1) % 2]
        if e < 0:
            dma(p, "gpsimd", s, WGB[:, :], fm_ap(sgw_i))
            dma(p, "gpsimd", s, WUB[:, :], fm_ap(suw_i))
            dma(p, "gpsimd", s, WDB[:, :], fm_ap(sdw_i))
        else:
            dma(p, "gpsimd", s, WGB[:, :], wg_i[e].rearrange("(c p) n -> p c n", p=128))
            dma(p, "gpsimd", s, WUB[:, :], wu_i[e].rearrange("(c p) n -> p c n", p=128))
            dma(p, "gpsimd", s, WDB[:, :], wd_i[e].rearrange("(c p) n -> p c n", p=128))
        return s, s.n

    exp_pe_done = [0]
    lw = {}
    lw[-1] = load_expert(-1)
    for e in range(-1, 8):
        s, tot = lw[e]
        hready = []
        yadd = []
        last_tsg = [None]
        for tc in range(8):
            for ic in range(NI):
                wait(p, "tensor", s, tot)
                wait(p, "tensor", lda, LDA)
                if last_tsg[0] is not None:
                    wait(p, "tensor", ac, last_tsg[0])
                if hready:
                    wait(p, "tensor", va, hready[-1])
                for hc in range(NC):
                    run(p, "tensor", lambda e2, hc=hc, ic=ic, tc=tc: e2.matmul(
                        PSA[:, 0:256], WGB[:, hc * I + ic * 128:hc * I + (ic + 1) * 128],
                        XF[:, hc * T + tc * 256:hc * T + tc * 256 + 256],
                        start=(hc == 0), stop=(hc == NC - 1)), pe if hc == NC - 1 else None)
                tg = pe.n
                for hc in range(NC):
                    run(p, "tensor", lambda e2, hc=hc, ic=ic, tc=tc: e2.matmul(
                        PSB2[:, 0:256], WUB[:, hc * I + ic * 128:hc * I + (ic + 1) * 128],
                        XF[:, hc * T + tc * 256:hc * T + tc * 256 + 256],
                        start=(hc == 0), stop=(hc == NC - 1)), pe if hc == NC - 1 else None)
                tu = pe.n
                wait(p, "scalar", pe, tg)
                tsg = run(p, "scalar", lambda e2: e2.activation(
                    GS[:, 0:256], PSA[:, 0:256], AF.Silu), ac)
                last_tsg[0] = tsg
                wait(p, "vector", pe, tu)
                wait(p, "vector", ac, tsg)
                hready.append(run(p, "vector", lambda e2, ic=ic: e2.scalar_tensor_tensor(
                    HFM[:, ic * 256:(ic + 1) * 256], PSB2[:, 0:256], 1.0, GS[:, 0:256],
                    ALU.mult, ALU.mult), va))
            for ts in range(2):
                tt = tc * 2 + ts
                for dc in range(2):
                    bank = (ts * 2 + dc) % 2
                    wait(p, "tensor", va, hready[-1])
                    if len(yadd) >= 2:
                        wait(p, "tensor", va, yadd[-2])
                    for ic in range(NI):
                        run(p, "tensor", lambda e2, ic=ic, ts=ts, dc=dc, bank=bank: e2.matmul(
                            PY[bank][:, :], HFM[:, ic * 256 + ts * 128:ic * 256 + (ts + 1) * 128],
                            WDB[:, ic * D + dc * 512:ic * D + dc * 512 + 512],
                            start=(ic == 0), stop=(ic == NI - 1)), pe if ic == NI - 1 else None)
                    wait(p, "vector", pe, pe.n)
                    if e < 0:
                        yadd.append(run(p, "vector", lambda e2, tt=tt, dc=dc, bank=bank: e2.tensor_copy(
                            YACC[:, tt * D + dc * 512:tt * D + (dc + 1) * 512], PY[bank][:, :]), va))
                    else:
                        wait(p, "vector", va, wev[tt])
                        yadd.append(run(p, "vector", lambda e2, tt=tt, dc=dc, bank=bank, e=e:
                            e2.scalar_tensor_tensor(
                                YACC[:, tt * D + dc * 512:tt * D + (dc + 1) * 512],
                                PY[bank][:, :], WE[:, tt * 8 + e:tt * 8 + e + 1],
                                YACC[:, tt * D + dc * 512:tt * D + (dc + 1) * 512],
                                ALU.mult, ALU.add), va))
        if e + 1 < 8:
            wait(p, "gpsimd", pe, pe.n)
            lw[e + 1] = load_expert(e + 1)
    wait(p, "sync", va, va.n)
    dma(p, "sync", st, tm_ap(y_o), YACC[:, :])
    wait(p, "sync", st, st.n)
    return p



def build_stage_l2locqkv(p=None, t=None):
    DO = 512
    p = p or Prog("l2locqkv")
    nc = p.nc
    xnfm_i = _io(nc, t, "xnfm", [D, T], BF16, "ExternalInput")
    lnw_i = _io(nc, t, "lnw", [128, 8], F32, "ExternalInput")
    lnb_i = _io(nc, t, "lnb", [128, 8], F32, "ExternalInput")
    wq_i = _io(nc, t, "wq", [D, DO], F32, "ExternalInput")
    wk_i = _io(nc, t, "wk", [D, DO], F32, "ExternalInput")
    wv_i = _io(nc, t, "wv", [D, DO], F32, "ExternalInput")
    ql_o = _io(nc, t, "ql", [DO, T], BF16, "ExternalOutput")
    kl_o = _io(nc, t, "kl", [DO, T], BF16, "ExternalOutput")
    vem = t.get("vemode") if t else None
    if vem is None:
        vl_o = _io(nc, t, "vl", [T, DO], BF16, "ExternalOutput")

    XF = p.sbuf([128, NC * T], BF16, "XF")
    QF = p.sbuf([128, 4 * T], BF16, "QF")
    LNW = p.sbuf([128, 8], F32, "LNW")
    LNB = p.sbuf([128, 8], F32, "LNB")
    VT = p.sbuf([128, 2 * 512], BF16, "VT")
    PSB = [p.psum([128, 512], F32, f"pp{i}") for i in range(4)]
    lda = p.sem("lda"); va = p.sem("va"); ac = p.sem("ac"); pe = p.sem("pe")
    ev = p.sem("ev"); st = p.psem("st")
    stv = [p.psem("stv0"), p.psem("stv1")]
    sems = dict(va=va, ac=ac, pe=pe, ev=ev, psumpool=PSB)

    dma(p, "sync", lda, XF[:, :], fm_ap(xnfm_i))
    dma(p, "sync", lda, LNW[:, :], lnw_i)
    dma(p, "sync", lda, LNB[:, :], lnb_i)
    LDA = lda.n
    WQ, wsq = load_w_fm(p, None, wq_i, D, DO, "WQ")
    WK, wsk = load_w_fm(p, None, wk_i, D, DO, "WK")
    WV, wsv = load_w_fm(p, None, wv_i, D, DO, "WV")

    wait(p, "vector", lda, LDA)
    zt = None
    for c in range(NC):
        zt = run(p, "vector", lambda e, c=c: e.tensor_scalar(
            XF[:, c * T:(c + 1) * T], XF[:, c * T:(c + 1) * T],
            LNW[:, c:c + 1], LNB[:, c:c + 1], ALU.mult, ALU.add), va)
    wait(p, "tensor", va, zt)
    wait(p, "scalar", va, zt)
    qo = emit_proj_fm(p, sems, XF, WQ, QF, DO, pre=[(wsq, wsq.n)])
    wait(p, "sync", ev, qo[-1])
    dma(p, "sync", st, ql_o.rearrange("(c p) n -> p c n", p=128), QF[:, 0:4 * T])
    def kevw(k):
        wait(p, "vector", st, 16)
    ko = emit_proj_fm(p, sems, XF, WK, QF, DO, pre=[(wsk, wsk.n)], evac_wait=kevw)
    wait(p, "sync", ev, ko[-1])
    dma(p, "sync", st, kl_o.rearrange("(c p) n -> p c n", p=128), QF[:, 0:4 * T])
    vev = []
    for tt in range(NT):
        bank = tt % 2
        wait(p, "tensor", wsv, wsv.n)
        wait(p, "tensor", va, zt)
        if tt >= 2:
            wait(p, "tensor", ev, vev[tt - 2])
        for ci in range(NC):
            run(p, "tensor", lambda e, ci=ci, tt=tt, bank=bank:
                e.matmul(PSB[bank][:, :],
                         XF[:, ci * T + tt * 128:ci * T + (tt + 1) * 128],
                         WV[:, ci * DO:ci * DO + DO],
                         start=(ci == 0), stop=(ci == NC - 1)),
                pe if ci == NC - 1 else None)
        wait(p, "vector", pe, pe.n)
        if tt >= 2:
            wait(p, "vector", stv[tt % 2], 16 * (tt // 2))
        vev.append(run(p, "vector", lambda e, tt=tt, bank=bank: e.tensor_copy(
            VT[:, (tt % 2) * 512:(tt % 2) * 512 + 512], PSB[bank][:, :]), ev))
        wait(p, "sync", ev, vev[-1])
        if vem is None:
            dma(p, "sync", stv[tt % 2], vl_o[tt * 128:(tt + 1) * 128, :],
                VT[:, (tt % 2) * 512:(tt % 2) * 512 + 512])
        else:
            dma(p, "sync", stv[tt % 2],
                bass.AP(vem[1], tt * 128 * 520, [[520, 128], [65, 8], [1, 64]]),
                VT[:, (tt % 2) * 512:(tt % 2) * 512 + 512])
    wait(p, "sync", st, st.n)
    wait(p, "sync", stv[0], stv[0].n)
    wait(p, "sync", stv[1], stv[1].n)
    return p


def build_stage_l2locattn(p=None, t=None):
    NH = 8
    p = p or Prog("l2locattn")
    nc = p.nc
    qlag = t.get("ql_ag") if t else None
    if qlag is None:
        ql_i = _io(nc, t, "ql", [512, S], BF16, "ExternalInput")
        klf_i = _io(nc, t, "klf", [512, S], BF16, "ExternalInput")
    else:
        klag = t["kl_ag"]
    vlef_i = _io(nc, t, "vlef", [S, NH * 65], BF16, "ExternalInput")
    wol_i = _io(nc, t, "wol", [512, D], F32, "ExternalInput")
    lmask_i = _io(nc, t, "lmask", [128, 2 * 128], F32, "ExternalInput")
    li_o = _io(nc, t, "light", [S, D], F32, "ExternalOutput")

    QL = p.sbuf([128, 4 * S], BF16, "QL")
    KL = p.sbuf([128, 4 * S], BF16, "KL")
    VLE = p.sbuf([128, (S // 128) * NH * 65], BF16, "VLE")
    LM = p.sbuf([128, 2 * 128], F32, "LM")
    SCR = p.sbuf([128, 128], F32, "SCR")
    EXPT = p.sbuf([128, 2 * 128], BF16, "EXPT")
    AFM = p.sbuf([65, 128], F32, "AFM")
    RDS = p.sbuf([128, 128], F32, "RDS")
    ONEB = p.sbuf([128, 64], F32, "ONEB")
    CB20 = p.sbuf([128, 1], F32, "CB20")
    AO = p.sbuf([128, 4 * S], BF16, "AO")
    YS = p.sbuf([128, 2 * 512], F32, "YS")
    PSC = [p.psum([128, 128], F32, f"psc{i}") for i in range(2)]
    PAV = p.psum([65, 128], F32, "pav")
    PBC = p.psum([64, 128], F32, "pbc")
    PY = [p.psum([128, 512], F32, f"py{i}") for i in range(2)]

    lda = p.sem("lda"); va = p.sem("va"); ac = p.sem("ac"); pe = p.sem("pe")
    ev = p.sem("ev"); st = p.psem("st")
    sty = [p.psem("sty0"), p.psem("sty1")]

    if qlag is None:
        dma(p, "sync", lda, QL[:, :], fm_ap(ql_i))
        dma(p, "sync", lda, KL[:, :], fm_ap(klf_i))
    else:
        for hc in range(4):
            for gg in range(2):
                dma(p, "sync", lda, QL[:, hc * S + gg * T:hc * S + gg * T + T],
                    qlag[gg * 512 + hc * 128:gg * 512 + hc * 128 + 128, :])
                dma(p, "sync", lda, KL[:, hc * S + gg * T:hc * S + gg * T + T],
                    klag[gg * 512 + hc * 128:gg * 512 + hc * 128 + 128, :])
    dma(p, "sync", lda, VLE[:, :], vlef_i.rearrange("(r p) n -> p r n", p=128))
    dma(p, "sync", lda, LM[:, :], lmask_i)
    LDA = lda.n
    WO, wso = load_w_fm(p, None, wol_i, 512, D, "WOL")
    run(p, "vector", lambda e: e.memset(CB20[:, :], -20.0))
    t_ones = run(p, "vector", lambda e: e.memset(ONEB[:, :], 1.0), va)
    wait(p, "scalar", va, t_ones)

    aov = []
    for h in range(NH):
        hc = h // 2
        hb = (h % 2) * 64
        for a in range(32):
            ag = a
            kts = [k for k in (ag - 1, ag, ag + 1) if 0 <= k < S // 128]
            expn = []
            pvn = []
            for j, kt in enumerate(kts):
                bank = j % 2
                wait(p, "tensor", lda, LDA)
                if len(expn) >= 2:
                    wait(p, "tensor", ac, expn[-2])
                run(p, "tensor", lambda e, hc=hc, hb=hb, kt=kt, a=a, bank=bank:
                    e.matmul(PSC[bank][:, :],
                             KL[hb:hb + 64, hc * S + kt * 128:hc * S + (kt + 1) * 128],
                             QL[hb:hb + 64, hc * S + a * 128:hc * S + (a + 1) * 128],
                             start=True, stop=True), pe)
                if len(pvn) >= 2:
                    wait(p, "scalar", pe, pvn[-2])
                dj = kt - ag
                if dj == 0:
                    wait(p, "scalar", pe, pe.n)
                    expn.append(run(p, "scalar", lambda e, j=j, bank=bank: e.activation(
                        EXPT[:, (j % 2) * 128:(j % 2) * 128 + 128], PSC[bank][:, :], AF.Exp,
                        scale=0.125, bias=CB20[:, 0:1]), ac))
                else:
                    mo = 0 if dj == -1 else 128
                    wait(p, "vector", pe, pe.n)
                    if expn:
                        wait(p, "vector", ac, expn[-1])
                    tvm = run(p, "vector", lambda e, bank=bank, mo=mo: e.tensor_add(
                        SCR[:, :], PSC[bank][:, :], LM[:, mo:mo + 128]), va)
                    wait(p, "scalar", va, tvm)
                    expn.append(run(p, "scalar", lambda e, j=j: e.activation(
                        EXPT[:, (j % 2) * 128:(j % 2) * 128 + 128], SCR[:, :], AF.Exp,
                        scale=0.125, bias=CB20[:, 0:1]), ac))
                wait(p, "tensor", ac, expn[-1])
                if j == 0:
                    wait(p, "tensor", ev, ev.n)
                pvn.append(run(p, "tensor", lambda e, kt=kt, h=h, j=j, nk=len(kts): e.matmul(
                    PAV[:, :], VLE[:, kt * NH * 65 + h * 65:kt * NH * 65 + (h + 1) * 65],
                    EXPT[:, (j % 2) * 128:(j % 2) * 128 + 128],
                    start=(j == 0), stop=(j == nk - 1)), pe))
            wait(p, "vector", pe, pvn[-1])
            if aov:
                wait(p, "vector", va, aov[-1])
            te = run(p, "vector", lambda e: e.tensor_copy(AFM[:, :], PAV[:, :]), ev)
            wait(p, "vector", ev, te)
            tr = run(p, "vector", lambda e: e.reciprocal(RDS[64:65, :], AFM[64:65, :]), va)
            wait(p, "tensor", va, tr)
            if aov:
                wait(p, "tensor", va, aov[-1])
            tb = run(p, "tensor", lambda e: e.matmul(
                PBC[:, :], ONEB[64:65, 0:64], RDS[64:65, :], start=True, stop=True), pe)
            wait(p, "vector", pe, tb)
            aov.append(run(p, "vector", lambda e, h=h, a=a: e.tensor_mul(
                AO[(h % 2) * 64:(h % 2) * 64 + 64,
                   (h // 2) * S + a * 128:(h // 2) * S + (a + 1) * 128],
                AFM[0:64, :], PBC[:, :]), va))
    yv = []
    wait(p, "tensor", va, aov[-1])
    for tt in range(S // 128):
        for dc in range(2):
            k = tt * 2 + dc
            bank = k % 2
            wait(p, "tensor", wso, wso.n)
            if k >= 2:
                wait(p, "tensor", ev, yv[k - 2])
            for ci in range(4):
                run(p, "tensor", lambda e, ci=ci, tt=tt, dc=dc, bank=bank:
                    e.matmul(PY[bank][:, :],
                             AO[:, ci * S + tt * 128:ci * S + (tt + 1) * 128],
                             WO[:, ci * D + dc * 512:ci * D + dc * 512 + 512],
                             start=(ci == 0), stop=(ci == 3)),
                    pe if ci == 3 else None)
            wait(p, "vector", pe, pe.n)
            if k >= 2:
                wait(p, "vector", sty[k % 2], 16 * (k // 2))
            yv.append(run(p, "vector", lambda e, k=k, bank=bank: e.tensor_copy(
                YS[:, (k % 2) * 512:(k % 2) * 512 + 512], PY[bank][:, :]), ev))
            wait(p, "sync", ev, yv[-1])
            dma(p, "sync", sty[k % 2], li_o[tt * 128:(tt + 1) * 128, dc * 512:dc * 512 + 512],
                YS[:, (k % 2) * 512:(k % 2) * 512 + 512])
    wait(p, "sync", sty[0], sty[0].n)
    wait(p, "sync", sty[1], sty[1].n)
    return p


def build_stage_l2hkv(p=None, t=None):
    DO = 512
    p = p or Prog("l2hkv")
    nc = p.nc
    xnfm_i = _io(nc, t, "xnfm", [D, T], BF16, "ExternalInput")
    gam_i = _io(nc, t, "gam", [128, 8], F32, "ExternalInput")
    c2_i = _io(nc, t, "c2", [128, NT], F32, "ExternalInput")
    rt_i = _io(nc, t, "rt", [D, 2], F32, "ExternalInput")
    wq_i = _io(nc, t, "wq", [D, DO], F32, "ExternalInput")
    wk_i = _io(nc, t, "wk", [D, DO], F32, "ExternalInput")
    wv_i = _io(nc, t, "wv", [D, DO], F32, "ExternalInput")
    qh_o = _io(nc, t, "qh", [DO, T], BF16, "ExternalOutput")
    kh_o = _io(nc, t, "kh", [DO, T], BF16, "ExternalOutput")
    vem = t.get("vemode") if t else None
    if vem is None:
        vh_o = _io(nc, t, "vh", [T, DO], BF16, "ExternalOutput")
    s_o = _io(nc, t, "s", [128, 2 * NT], F32, "ExternalOutput")

    XF = p.sbuf([128, NC * T], BF16, "XF")
    RM = p.sbuf([128, NC * T], BF16, "RM")
    QF = p.sbuf([128, 4 * T], BF16, "QF")
    GAM = p.sbuf([128, 8], F32, "GAM")
    C2C = p.sbuf([128, NT], F32, "C2C")
    RC2 = p.sbuf([128, NT], F32, "RC2")
    RT = p.sbuf([128, NC * 2], BF16, "RT")
    SS = p.sbuf([128, 2 * NT], F32, "SS")
    VT = p.sbuf([128, 2 * 512], BF16, "VT")
    PSB = [p.psum([128, 512], F32, f"pp{i}") for i in range(4)]
    PRT = p.psum([128, 2], F32, "prt")
    lda = p.sem("lda"); gl = p.sem("gl"); va = p.sem("va"); ac = p.sem("ac")
    pe = p.sem("pe"); ev = p.sem("ev"); st = p.psem("st")
    stv = [p.psem("stv0"), p.psem("stv1")]
    sems = dict(va=va, ac=ac, pe=pe, ev=ev, psumpool=PSB)

    dma(p, "sync", lda, XF[:, :], fm_ap(xnfm_i))
    dma(p, "sync", lda, GAM[:, :], gam_i)
    dma(p, "sync", lda, C2C[:, :], c2_i)
    LDA = lda.n
    dma(p, "gpsimd", gl, RT[:, :], fm_ap(rt_i))
    GL = gl.n
    WQ, wsq = load_w_fm(p, None, wq_i, D, DO, "WQ")
    WK, wsk = load_w_fm(p, None, wk_i, D, DO, "WK")
    WV, wsv = load_w_fm(p, None, wv_i, D, DO, "WV")

    wait(p, "vector", lda, LDA)
    trc = run(p, "vector", lambda e: e.reciprocal(RC2[:, :], C2C[:, :]), va)
    sev = []
    for tt in range(NT):
        wait(p, "tensor", lda, LDA)
        wait(p, "tensor", gl, GL)
        if sev:
            wait(p, "tensor", va, sev[-1])
        for ci in range(NC):
            run(p, "tensor", lambda e, ci=ci, tt=tt: e.matmul(
                PRT[:, :], XF[:, ci * T + tt * 128:ci * T + (tt + 1) * 128],
                RT[:, ci * 2:(ci + 1) * 2], start=(ci == 0), stop=(ci == NC - 1)),
                pe if ci == NC - 1 else None)
        wait(p, "vector", pe, pe.n)
        wait(p, "vector", va, trc)
        sev.append(run(p, "vector", lambda e, tt=tt: e.tensor_scalar(
            SS[:, 2 * tt:2 * tt + 2], PRT[:, :], RC2[:, tt:tt + 1], None, ALU.mult), va))
    wait(p, "sync", va, sev[-1])
    dma(p, "sync", st, s_o, SS[:, :])

    rmv = None
    for c in range(NC):
        wait(p, "vector", lda, LDA)
        rmv = run(p, "vector", lambda e, c=c: e.tensor_scalar(
            RM[:, c * T:(c + 1) * T], XF[:, c * T:(c + 1) * T],
            GAM[:, c:c + 1], None, ALU.mult), va)
    wait(p, "tensor", va, rmv)
    qo = emit_proj_fm(p, sems, RM, WQ, QF, DO, pre=[(wsq, wsq.n)])
    wait(p, "sync", ev, qo[-1])
    dma(p, "sync", st, qh_o.rearrange("(c p) n -> p c n", p=128), QF[:, 0:4 * T])
    def kevw(k):
        wait(p, "vector", st, 32)
    ko = emit_proj_fm(p, sems, RM, WK, QF, DO, pre=[(wsk, wsk.n)], evac_wait=kevw)
    wait(p, "sync", ev, ko[-1])
    dma(p, "sync", st, kh_o.rearrange("(c p) n -> p c n", p=128), QF[:, 0:4 * T])
    vev = []
    for tt in range(NT):
        bank = tt % 2
        wait(p, "tensor", wsv, wsv.n)
        wait(p, "tensor", va, rmv)
        if tt >= 2:
            wait(p, "tensor", ev, vev[tt - 2])
        for ci in range(NC):
            run(p, "tensor", lambda e, ci=ci, tt=tt, bank=bank:
                e.matmul(PSB[bank][:, :],
                         RM[:, ci * T + tt * 128:ci * T + (tt + 1) * 128],
                         WV[:, ci * DO:ci * DO + DO],
                         start=(ci == 0), stop=(ci == NC - 1)),
                pe if ci == NC - 1 else None)
        wait(p, "vector", pe, pe.n)
        if tt >= 2:
            wait(p, "vector", stv[tt % 2], 16 * (tt // 2))
        vev.append(run(p, "vector", lambda e, tt=tt, bank=bank: e.tensor_copy(
            VT[:, (tt % 2) * 512:(tt % 2) * 512 + 512], PSB[bank][:, :]), ev))
        wait(p, "sync", ev, vev[-1])
        if vem is None:
            dma(p, "sync", stv[tt % 2], vh_o[tt * 128:(tt + 1) * 128, :],
                VT[:, (tt % 2) * 512:(tt % 2) * 512 + 512])
        else:
            dma(p, "sync", stv[tt % 2],
                bass.AP(vem[1], tt * 128 * 520, [[520, 128], [65, 8], [1, 64]]),
                VT[:, (tt % 2) * 512:(tt % 2) * 512 + 512])
    wait(p, "sync", st, st.n)
    wait(p, "sync", stv[0], stv[0].n)
    wait(p, "sync", stv[1], stv[1].n)
    return p



def build_stage_l2heavy(p=None, t=None):
    NH = 8
    SN = S + 128
    NKT = SN // 128
    p = p or Prog("l2heavy")
    nc = p.nc
    fzh = t is not None and "kh_ag" in t
    qh_i = _io(nc, t, "qh", [512, T], BF16, "ExternalInput")
    if fzh:
        khag = t["kh_ag"]
        vh_ag = t["vh_ag"]
        s_ag = t["s_ag"]
        nk_i = t["nk"]
        nv_i = t["nv"]
        lof_i = t["lof"]
        idf_i = t["identf"]
        mrow_h = t["mrow_h"]
        light_i = t["light"]
    else:
        khf_i = _io(nc, t, "khf", [512, SN], BF16, "ExternalInput")
        vhef_i = _io(nc, t, "vhef", [SN, NH * 65], BF16, "ExternalInput")
        sbis_i = _io(nc, t, "sbis", [128, 64], F32, "ExternalInput")
        skvrow_i = _io(nc, t, "skvrow", [1, S], F32, "ExternalInput")
        light_i = _io(nc, t, "light", [T, D], F32, "ExternalInput")
    sqtm_i = _io(nc, t, "sqtm", [128, NT], F32, "ExternalInput")
    kcon_i = _io(nc, t, "kcon", [1, 2], F32, "ExternalInput")
    woh_i = _io(nc, t, "woh", [512, D], F32, "ExternalInput")
    nqb_i = _io(nc, t, "nqb", [128, D], F32, "ExternalInput")
    y_o = _io(nc, t, "y2f", [T, D], F32, "ExternalOutput")

    QH = p.sbuf([128, 4 * T], BF16, "QH")
    SB = p.sbuf([128, 64], F32, "SB")
    if fzh:
        MKF = p.sbuf([128, 32], F32, "MKF")
        MKT = p.sbuf([32, 128], F32, "MKT")
        NKB = p.sbuf([128, 4], BF16, "NKB")
        LOF = p.sbuf([128, NT], mybir.dt.int32, "LOF")
        IDF = p.sbuf([128, 128], F32, "IDF")
    else:
        SKR = p.sbuf([1, S], F32, "SKR")
    SQTM = p.sbuf([128, NT], F32, "SQTM")
    KCON = p.sbuf([1, 2], F32, "KCON")
    TAU = p.sbuf([1, 2], F32, "TAU")
    LO = p.sbuf([1, 2], F32, "LO")
    HI = p.sbuf([1, 2], F32, "HI")
    LO2 = p.sbuf([1, 2], F32, "LO2")
    HI2 = p.sbuf([1, 2], F32, "HI2")
    CNT = p.sbuf([1, 2], F32, "CNT")
    CONDU = p.sbuf([1, 2], mybir.dt.uint8, "CONDU")
    TAUB = p.sbuf([128, 2], F32, "TAUB")
    CMP = p.sbuf([128, 64], F32, "CMP")
    CN12 = p.sbuf([128, 2], F32, "CN12")
    ONES1 = p.sbuf([1, 128], F32, "ONES1")
    ONESF = p.sbuf([128, 1], F32, "ONESF")
    if not fzh:
        MKV = p.sbuf([1, SN], BF16, "MKV")
        MKS = p.sbuf([1, SN], F32, "MKS")
    M01 = p.sbuf([128, NT], F32, "M01")
    QE = p.sbuf([65, T], BF16, "QE")
    KE = p.sbuf([65, 2 * SN], BF16, "KE")
    VHE = p.sbuf([128, NKT * NH * 65], BF16, "VHE")
    EXPT = p.sbuf([128, 2 * 512], BF16, "EXPT")
    AFM = p.sbuf([65, 512], F32, "AFM")
    RDS = p.sbuf([128, 512], F32, "RDS")
    ONEB = p.sbuf([128, 64], F32, "ONEB")
    CB20 = p.sbuf([128, 1], F32, "CB20")
    AO = p.sbuf([128, 4 * T], BF16, "AO")
    NQB = p.sbuf([128, D], F32, "NQB")
    LIT = p.sbuf([128, 2 * D], F32, "LIT")
    HV = p.sbuf([128, 512], F32, "HV")
    YS = p.sbuf([128, 2 * 512], F32, "YS")
    PSC = [p.psum([128, 512], F32, f"psc{i}") for i in range(2)]
    PAV = p.psum([65, 512], F32, "pav")
    PBB = p.psum([128, 2], F32, "pbb")
    PCN = p.psum([1, 2], F32, "pcn")
    PBC = p.psum([64, 512], F32, "pbc")
    PY = [p.psum([128, 512], F32, f"py{i}") for i in range(2)]

    lda = p.sem("lda"); gl = p.sem("gl"); va = p.sem("va"); ac = p.sem("ac")
    pe = p.sem("pe"); ev = p.sem("ev")
    hl = [p.sem("hl0"), p.sem("hl1")]
    sty = [p.psem("sty0"), p.psem("sty1")]
    ldl = [p.sem("ldl0"), p.sem("ldl1")]
    msk = p.sem("msk")

    dma(p, "sync", lda, QH[:, :], fm_ap(qh_i))
    if fzh:
        dma(p, "sync", lda, VHE[:, 0:32 * NH * 65],
            vh_ag.rearrange("(r p) n -> p r n", p=128))
        dma(p, "sync", lda, SB[:, 0:16], s_ag[0:128, 0:NT])
        dma(p, "sync", lda, SB[:, 16:32], s_ag[128:256, 0:NT])
        dma(p, "sync", lda, SB[:, 32:48], s_ag[0:128, NT:2 * NT])
        dma(p, "sync", lda, SB[:, 48:64], s_ag[128:256, NT:2 * NT])
        dma(p, "sync", lda, LOF[:, :], lof_i)
        dma(p, "sync", lda, IDF[:, :], idf_i)
    else:
        dma(p, "sync", lda, VHE[:, :], vhef_i.rearrange("(r p) n -> p r n", p=128))
        dma(p, "sync", lda, SB[:, :], sbis_i)
        dma(p, "sync", lda, SKR[:, :], skvrow_i)
    dma(p, "sync", lda, SQTM[:, :], sqtm_i)
    dma(p, "sync", lda, KCON[:, :], kcon_i)
    dma(p, "sync", lda, NQB[:, :], nqb_i)
    LDA = lda.n
    if fzh:
        dma(p, "gpsimd", gl, NKB[:, :], nk_i)
        GLNK = gl.n
        tvz0 = run(p, "vector", lambda e: e.memset(VHE[:, 32 * NH * 65:33 * NH * 65], 0.0), va)
        wait(p, "gpsimd", va, tvz0)
        dma(p, "gpsimd", gl, VHE[0:1, 32 * NH * 65:33 * NH * 65], nv_i)
        GLNV = gl.n
        wait(p, "sync", gl, GLNV)
        run(p, "vector", lambda e: e.memset(KE[0:64, S + 1:SN], 0.0))
        run(p, "vector", lambda e: e.memset(KE[0:64, SN + S + 1:2 * SN], 0.0), va)
    WO, wso = load_w_fm(p, None, woh_i, 512, D, "WOH")
    run(p, "vector", lambda e: e.memset(ONES1[:, :], 1.0))
    run(p, "vector", lambda e: e.memset(ONESF[:, :], 1.0))
    run(p, "vector", lambda e: e.memset(CB20[:, :], -20.0))
    tob_ = run(p, "vector", lambda e: e.memset(ONEB[:, :], 1.0), va)
    wait(p, "scalar", va, tob_)
    run(p, "vector", lambda e: e.memset(LO[:, :], -1100.0))
    run(p, "vector", lambda e: e.memset(HI[:, :], 1100.0))
    t0 = run(p, "vector", lambda e: e.memset(TAU[:, :], 0.0), va)

    wait(p, "vector", lda, LDA)
    wait(p, "tensor", lda, LDA)
    for it in range(30):
        wait(p, "tensor", va, t0)
        tb = run(p, "tensor", lambda e: e.matmul(
            PBB[:, :], ONES1[0:1, 0:128], TAU[:, :], start=True, stop=True), pe)
        wait(p, "vector", pe, tb)
        t1 = run(p, "vector", lambda e: e.tensor_copy(TAUB[:, :], PBB[:, :]), va)
        wait(p, "vector", va, t1)
        run(p, "vector", lambda e: e.tensor_scalar(
            CMP[:, 0:32], SB[:, 0:32], TAUB[:, 0:1], None, ALU.is_ge))
        t2 = run(p, "vector", lambda e: e.tensor_scalar(
            CMP[:, 32:64], SB[:, 32:64], TAUB[:, 1:2], None, ALU.is_ge), va)
        wait(p, "vector", va, t2)
        run(p, "vector", lambda e: e.reduce_sum(CN12[:, 0:1], CMP[:, 0:32], axis=AX))
        t3 = run(p, "vector", lambda e: e.reduce_sum(CN12[:, 1:2], CMP[:, 32:64], axis=AX), va)
        wait(p, "tensor", va, t3)
        tc_ = run(p, "tensor", lambda e: e.matmul(
            PCN[:, :], ONESF[:, 0:1], CN12[:, :], start=True, stop=True), pe)
        wait(p, "vector", pe, tc_)
        t4 = run(p, "vector", lambda e: e.tensor_copy(CNT[:, :], PCN[:, :]), va)
        wait(p, "vector", va, t4)
        t5 = run(p, "vector", lambda e: e.tensor_tensor(
            CONDU[:, :], CNT[:, :], KCON[:, :], ALU.is_ge), va)
        wait(p, "vector", va, t5)
        t6 = run(p, "vector", lambda e: e.select(LO2[:, :], CONDU[:, :], TAU[:, :], LO[:, :], add_drain=True), va)
        wait(p, "vector", va, t6)
        t7 = run(p, "vector", lambda e: e.select(HI2[:, :], CONDU[:, :], HI[:, :], TAU[:, :], add_drain=True), va)
        wait(p, "vector", va, t7)
        tcl = run(p, "vector", lambda e: e.tensor_copy(LO[:, :], LO2[:, :]), va)
        wait(p, "vector", va, tcl)
        t8 = run(p, "vector", lambda e: e.tensor_copy(HI[:, :], HI2[:, :]), va)
        wait(p, "vector", va, t8)
        t9 = run(p, "vector", lambda e: e.tensor_add(TAU[:, :], LO[:, :], HI[:, :]), va)
        wait(p, "vector", va, t9)
        t0 = run(p, "vector", lambda e: e.tensor_scalar_mul(TAU[:, :], TAU[:, :], 0.5), va)
        wait(p, "vector", va, t0)
    wait(p, "tensor", va, t0)
    tb2 = run(p, "tensor", lambda e: e.matmul(
        PBB[:, :], ONES1[0:1, 0:128], LO[:, :], start=True, stop=True), pe)
    wait(p, "vector", pe, tb2)
    tl1 = run(p, "vector", lambda e: e.tensor_copy(TAUB[:, :], PBB[:, :]), va)
    wait(p, "vector", va, tl1)
    tm4 = run(p, "vector", lambda e: e.tensor_scalar(
        M01[:, :], SQTM[:, :], TAUB[:, 0:1], None, ALU.is_ge), va)
    if fzh:
        wait(p, "vector", va, tm4)
        m1 = run(p, "vector", lambda e: e.tensor_scalar(
            MKF[:, :], SB[:, 32:64], TAUB[:, 1:2], None, ALU.is_ge), va)
        wait(p, "vector", va, m1)
        m2 = run(p, "vector", lambda e: e.tensor_scalar(
            MKF[:, :], MKF[:, :], 1.0, 1e9, ALU.subtract, ALU.mult), va)
        wait(p, "tensor", va, m2)
        wait(p, "tensor", lda, LDA)
        wait(p, "tensor", gl, GLNV)
        tt_ = run(p, "tensor", lambda e: e.transpose(PY[0][0:32, 0:128], MKF[:, :], IDF[:, :]), pe)
        wait(p, "vector", pe, tt_)
        t3_ = run(p, "vector", lambda e: e.tensor_copy(MKT[:, :], PY[0][0:32, 0:128]), ev)
        wait(p, "sync", ev, t3_)
        dma(p, "sync", msk, bass.AP(mrow_h, 0, [[128, 32], [1, 128]]), MKT[:, :])
        wait(p, "gpsimd", msk, msk.n)
        mrow_ap = mrow_h.ap()
        dma(p, "gpsimd", gl, KE[64:65, 0:S], mrow_ap)
        dma(p, "gpsimd", gl, KE[64:65, SN:SN + S], mrow_ap)
        GKE = gl.n
        wait(p, "tensor", gl, GKE)
        run(p, "vector", lambda e: e.memset(KE[64:65, S:S + 1], 0.0))
        run(p, "vector", lambda e: e.memset(KE[64:65, S + 1:SN], -1e9))
        run(p, "vector", lambda e: e.memset(KE[64:65, SN + S:SN + S + 1], 0.0))
        run(p, "vector", lambda e: e.memset(KE[64:65, SN + S + 1:2 * SN], -1e9), va)
        MSK = msk.n
    else:
        wait(p, "vector", va, t0)
        tm1 = run(p, "vector", lambda e: e.tensor_scalar(
            MKS[0:1, 0:S], SKR[:, :], LO[0:1, 1:2], None, ALU.is_ge), va)
        wait(p, "vector", va, tm1)
        tma = run(p, "vector", lambda e: e.memset(MKS[0:1, S:SN], 0.0), va)
        wait(p, "vector", va, tma)
        tm2 = run(p, "vector", lambda e: e.memset(MKS[0:1, S:S + 1], 1.0), va)
        wait(p, "vector", va, tm2)
        tm3 = run(p, "vector", lambda e: e.tensor_scalar(
            MKV[0:1, :], MKS[0:1, :], 1.0, 1e9, ALU.subtract, ALU.mult), va)
        wait(p, "sync", va, tm3)
        dma(p, "sync", msk, KE[64:65, 0:SN], MKV[0:1, :])
        dma(p, "sync", msk, KE[64:65, SN:2 * SN], MKV[0:1, :])
        MSK = msk.n

    t_hl = {}
    def load_head(h):
        bok = (h % 2) * SN
        if fzh:
            dma(p, "sync", hl[h % 2], KE[0:64, bok:bok + T], khag[h * 64:h * 64 + 64, :])
            dma(p, "sync", hl[h % 2], KE[0:64, bok + T:bok + S],
                khag[512 + h * 64:512 + h * 64 + 64, :])
            t_hl[h] = dma(p, "sync", hl[h % 2], KE[0:64, bok + S:bok + S + 1],
                          NKB[(h % 2) * 64:(h % 2) * 64 + 64, h // 2:h // 2 + 1])
        else:
            t_hl[h] = dma(p, "sync", hl[h % 2], KE[0:64, bok:bok + SN],
                          khf_i[h * 64:(h + 1) * 64, :])
    load_head(0)
    load_head(1)
    aov = []
    for h in range(NH):
        bok = (h % 2) * SN
        hc = h // 2
        hb = (h % 2) * 64
        wait(p, "sync", lda, LDA)
        wait(p, "sync", pe, pe.n)
        tqe = dma(p, "sync", hl[h % 2], QE[0:64, 0:T], QH[hb:hb + 64, hc * T:(hc + 1) * T])
        wait(p, "vector", hl[h % 2], tqe)
        tqo = run(p, "vector", lambda e: e.memset(QE[64:65, 0:T], 1.0), va)
        for qc in range(4):
            expn = []
            pvn = []
            for kt in range(NKT):
                bank = kt % 2
                wait(p, "tensor", hl[h % 2], tqe)
                wait(p, "tensor", msk, MSK)
                wait(p, "tensor", va, tqo)
                wait(p, "tensor", lda, LDA)
                if len(expn) >= 2:
                    wait(p, "tensor", ac, expn[-2])
                run(p, "tensor", lambda e, kt=kt, qc=qc, bank=bank, bok=bok:
                    e.matmul(PSC[bank][:, :],
                             KE[0:65, bok + kt * 128:bok + (kt + 1) * 128],
                             QE[0:65, qc * 512:qc * 512 + 512],
                             start=True, stop=True), pe)
                if len(pvn) >= 2:
                    wait(p, "scalar", pe, pvn[-2])
                wait(p, "scalar", pe, pe.n)
                expn.append(run(p, "scalar", lambda e, kt=kt, bank=bank: e.activation(
                    EXPT[:, (kt % 2) * 512:(kt % 2) * 512 + 512], PSC[bank][:, :], AF.Exp,
                    scale=0.125, bias=CB20[:, 0:1]), ac))
                wait(p, "tensor", ac, expn[-1])
                if kt == 0:
                    wait(p, "tensor", ev, ev.n)
                pvn.append(run(p, "tensor", lambda e, kt=kt, h=h: e.matmul(
                    PAV[:, :], VHE[:, kt * NH * 65 + h * 65:kt * NH * 65 + (h + 1) * 65],
                    EXPT[:, (kt % 2) * 512:(kt % 2) * 512 + 512],
                    start=(kt == 0), stop=(kt == NKT - 1)), pe))
            wait(p, "vector", pe, pvn[-1])
            if aov:
                wait(p, "vector", va, aov[-1])
            te = run(p, "vector", lambda e: e.tensor_copy(AFM[:, :], PAV[:, :]), ev)
            wait(p, "vector", ev, te)
            tr = run(p, "vector", lambda e: e.reciprocal(RDS[64:65, :], AFM[64:65, :]), va)
            wait(p, "tensor", va, tr)
            if aov:
                wait(p, "tensor", va, aov[-1])
            tb3 = run(p, "tensor", lambda e: e.matmul(
                PBC[:, :], ONEB[64:65, 0:64], RDS[64:65, :], start=True, stop=True), pe)
            wait(p, "vector", pe, tb3)
            aov.append(run(p, "vector", lambda e, h=h, qc=qc: e.tensor_mul(
                AO[(h % 2) * 64:(h % 2) * 64 + 64,
                   (h // 2) * T + qc * 512:(h // 2) * T + qc * 512 + 512],
                AFM[0:64, :], PBC[:, :]), va))
        if h + 2 < NH:
            wait(p, "sync", pe, pe.n)
            load_head(h + 2)
    yv = []
    t1s = []
    wait(p, "tensor", va, aov[-1])
    for tt in range(NT):
        if fzh:
            if tt == 0:
                wait(p, "gpsimd", lda, LDA)
            if tt >= 2:
                wait(p, "gpsimd", va, yv[2 * (tt - 2) + 1])
            tll = ldl[tt % 2].tick(16)
            p.on("gpsimd", lambda e, tt=tt, sm=ldl[tt % 2].h: e.indirect_dma_start(
                LIT[:, (tt % 2) * D:(tt % 2) * D + D], None, light_i,
                bass.IndirectOffsetOnAxis(LOF[:, tt:tt + 1], 0)).then_inc(sm, 16))
        else:
            if tt >= 2:
                wait(p, "sync", va, yv[2 * (tt - 2) + 1])
            tll = dma(p, "sync", ldl[tt % 2], LIT[:, (tt % 2) * D:(tt % 2) * D + D],
                      light_i[tt * 128:(tt + 1) * 128, :])
        for dc in range(2):
            k = tt * 2 + dc
            bank = k % 2
            wait(p, "tensor", wso, wso.n)
            if k >= 2:
                wait(p, "tensor", va, t1s[k - 2])
            for ci in range(4):
                run(p, "tensor", lambda e, ci=ci, tt=tt, dc=dc, bank=bank:
                    e.matmul(PY[bank][:, :],
                             AO[:, ci * T + tt * 128:ci * T + (tt + 1) * 128],
                             WO[:, ci * D + dc * 512:ci * D + dc * 512 + 512],
                             start=(ci == 0), stop=(ci == 3)),
                    pe if ci == 3 else None)
            wait(p, "vector", pe, pe.n)
            wait(p, "vector", va, tm4)
            wait(p, "vector", lda, LDA)
            wait(p, "vector", ldl[tt % 2], tll)
            if yv:
                wait(p, "vector", va, yv[-1])
            t1_ = run(p, "vector", lambda e, dc=dc, bank=bank: e.tensor_sub(
                HV[:, :], PY[bank][:, :], NQB[:, dc * 512:dc * 512 + 512]), va)
            t1s.append(t1_)
            wait(p, "vector", va, t1_)
            t2_ = run(p, "vector", lambda e, tt=tt: e.tensor_scalar(
                HV[:, :], HV[:, :], M01[:, tt:tt + 1], None, ALU.mult), va)
            wait(p, "vector", va, t2_)
            t3_ = run(p, "vector", lambda e, dc=dc: e.tensor_add(
                HV[:, :], HV[:, :], NQB[:, dc * 512:dc * 512 + 512]), va)
            wait(p, "vector", va, t3_)
            if k >= 2:
                wait(p, "vector", sty[k % 2], 16 * (k // 2))
            yv.append(run(p, "vector", lambda e, tt=tt, dc=dc, k=k: e.tensor_add(
                YS[:, (k % 2) * 512:(k % 2) * 512 + 512], HV[:, :],
                LIT[:, (tt % 2) * D + dc * 512:(tt % 2) * D + dc * 512 + 512]), va))
            wait(p, "sync", va, yv[-1])
            dma(p, "sync", sty[k % 2], y_o[tt * 128:(tt + 1) * 128, dc * 512:dc * 512 + 512],
                YS[:, (k % 2) * 512:(k % 2) * 512 + 512])
    wait(p, "sync", sty[0], sty[0].n)
    wait(p, "sync", sty[1], sty[1].n)
    return p



def build_stage_l3attn(p=None, t=None):
    NKT = S // 128
    JW = 512
    p = p or Prog("l3attn")
    nc = p.nc
    fz3 = t is not None and "qk_ag" in t
    if fz3:
        qkag = t["qk_ag"]
        kkag = t["kk_ag"]
        vag3 = t["v_ag"]
        ofq_i = t["ofq"]
        ofk_i = t["ofk"]
        ofv_i = t["ofv"]
        aocc_ap = t["aocc"]
    else:
        qr_i = _io(nc, t, "qh8", [512, S], BF16, "ExternalInput")
        krf_i = _io(nc, t, "krf8", [512, S], BF16, "ExternalInput")
        vef_i = _io(nc, t, "vef8", [S, 8 * 65], BF16, "ExternalInput")
        ao_o = _io(nc, t, "aofm", [512, S], BF16, "ExternalOutput")
    embt_i = _io(nc, t, "embt", [64, 73], F32, "ExternalInput")
    identf_i = _io(nc, t, "identf", [128, 128], F32, "ExternalInput")
    epad = nc.dram_tensor("epad", [S, JW], F32)

    QH = p.sbuf([128, 4 * S], BF16, "QH")
    VEF = p.sbuf([128, NKT * 8 * 65], BF16, "VEF")
    EMBT = p.sbuf([64, 73], F32, "EMBT")
    EMBB = p.sbuf([64, JW], BF16, "EMBB")
    DIFF = p.sbuf([64, 1], F32, "DIFF")
    E2C = p.sbuf([64, 2], BF16, "E2C")
    IDF = p.sbuf([128, 128], F32, "IDF")
    QE = p.sbuf([66, S], BF16, "QE")
    ZR = p.sbuf([1, S], BF16, "ZR")
    QROWS = p.sbuf([2, S], BF16, "QROWS")
    KE = p.sbuf([66, 2 * S], BF16, "KE")
    EPS_ = p.sbuf([128, JW], F32, "EPS_")
    SH = p.sbuf([128, 2 * 384], F32, "SH")
    EXPT = p.sbuf([128, 2 * 512], BF16, "EXPT")
    AFM = p.sbuf([65, 128], F32, "AFM")
    RDS = p.sbuf([128, 128], F32, "RDS")
    ONEB = p.sbuf([128, 64], F32, "ONEB")
    CB20 = p.sbuf([128, 1], F32, "CB20")
    AO = p.sbuf([128, 4 * S], BF16, "AO")
    PSC = [p.psum([128, 512], F32, f"psc{i}") for i in range(2)]
    PAV = p.psum([65, 128], F32, "pav")
    PBC = p.psum([64, 128], F32, "pbc")
    PEB = p.psum([128, JW], F32, "peb")

    lda = p.sem("lda"); va = p.sem("va"); ac = p.sem("ac")
    pe = p.sem("pe"); ev = p.sem("ev")
    hl = p.sem("hl"); qrw = p.sem("qrw")
    epw = p.sem("epw"); epr = [p.sem("epr0"), p.sem("epr1")]
    st = p.psem("st")

    if fz3:
        OFQ = p.sbuf([128, 16], mybir.dt.int32, "OFQ")
        OFK = p.sbuf([64, 16], mybir.dt.int32, "OFK")
        OFV = p.sbuf([128, 32], mybir.dt.int32, "OFV")
        gld = p.sem("gld")
        dma(p, "sync", lda, OFQ[:, :], ofq_i)
        dma(p, "sync", lda, OFK[:, :], ofk_i)
        dma(p, "sync", lda, OFV[:, :], ofv_i)
        wait(p, "gpsimd", lda, lda.n)
        for hc in range(4):
            for gg in range(2):
                tg = gld.tick(16)
                p.on("gpsimd", lambda e, hc=hc, gg=gg, sm=gld.h: e.indirect_dma_start(
                    QH[:, hc * S + gg * T:hc * S + gg * T + T], None, qkag,
                    bass.IndirectOffsetOnAxis(OFQ[:, hc * 2 + gg:hc * 2 + gg + 1], 0)
                ).then_inc(sm, 16))
        for kt in range(NKT):
            tg = gld.tick(16)
            p.on("gpsimd", lambda e, kt=kt, sm=gld.h: e.indirect_dma_start(
                VEF[:, kt * 8 * 65:(kt + 1) * 8 * 65], None, vag3,
                bass.IndirectOffsetOnAxis(OFV[:, kt:kt + 1], 0)).then_inc(sm, 16))
        GQV = gld.n
        wait(p, "sync", gld, GQV)
        wait(p, "tensor", gld, GQV)
    else:
        dma(p, "sync", lda, QH[:, :], fm_ap(qr_i))
        dma(p, "sync", lda, VEF[:, :], vef_i.rearrange("(r p) n -> p r n", p=128))
    dma(p, "sync", lda, EMBT[:, :], embt_i)
    dma(p, "sync", lda, IDF[:, :], identf_i)
    LDA = lda.n
    tcb = run(p, "vector", lambda e: e.memset(CB20[:, :], -20.0), va)
    wait(p, "scalar", va, tcb)
    run(p, "vector", lambda e: e.memset(ONEB[:, :], 1.0))
    run(p, "vector", lambda e: e.memset(ZR[:, :], 0.0))
    run(p, "vector", lambda e: e.memset(KE[64:66, 0:S], 1.0))
    t_ke0 = run(p, "vector", lambda e: e.memset(KE[64:66, S:2 * S], 1.0), va)
    wait(p, "sync", va, t_ke0)
    t_zr = dma(p, "sync", qrw, KE[65:66, 0:S], ZR[:, :])
    t_ke1 = t_ke0
    wait(p, "vector", lda, LDA)
    run(p, "vector", lambda e: e.memset(EMBB[:, 0:192], 0.0))
    tdf = run(p, "vector", lambda e: e.tensor_sub(
        DIFF[:, 0:1], EMBT[:, 72:73], EMBT[:, 0:1]), va)
    teb1 = run(p, "vector", lambda e: e.tensor_scalar(
        EMBB[:, 192:264], EMBT[:, 1:73], EMBT[:, 0:1], None, ALU.subtract), va)
    wait(p, "vector", va, max(tdf, teb1))
    tob = run(p, "vector", lambda e: e.memset(EMBB[:, 264:JW], 1.0), va)
    wait(p, "vector", va, tob)
    teb2 = run(p, "vector", lambda e: e.tensor_scalar(
        EMBB[:, 264:JW], EMBB[:, 264:JW], DIFF[:, 0:1], None, ALU.mult), va)
    run(p, "vector", lambda e: e.tensor_copy(E2C[:, 0:1], EMBT[:, 0:1]))
    tdfb = run(p, "vector", lambda e: e.tensor_copy(E2C[:, 1:2], DIFF[:, 0:1]), va)

    aov = []
    for h in range(8):
        hc = h // 2
        hb = (h % 2) * 64
        wait(p, "sync", pe, pe.n)
        wait(p, "sync", lda, LDA)
        wait(p, "sync", epr[0], epr[0].n)
        wait(p, "sync", epr[1], epr[1].n)
        if fz3:
            for cp in range(2):
                for gg in range(2):
                    tk0 = hl.tick(16)
                    p.on("gpsimd", lambda e, cp=cp, gg=gg, h=h, sm=hl.h: e.indirect_dma_start(
                        KE[0:64, cp * S + gg * T:cp * S + gg * T + T], None, kkag,
                        bass.IndirectOffsetOnAxis(OFK[:, h * 2 + gg:h * 2 + gg + 1], 0)
                    ).then_inc(sm, 16))
            tk1 = tk0
        else:
            tk0 = dma(p, "sync", hl, KE[0:64, 0:S], krf_i[h * 64:(h + 1) * 64, :])
            tk1 = dma(p, "sync", hl, KE[0:64, S:2 * S], krf_i[h * 64:(h + 1) * 64, :])
        tqe = dma(p, "sync", qrw, QE[0:64, 0:S], QH[hb:hb + 64, hc * S:(hc + 1) * S])
        epv = []
        for lt in range(S // 128):
            wait(p, "tensor", lda, LDA)
            wait(p, "tensor", va, teb2)
            if epv:
                wait(p, "tensor", ev, epv[-1])
            wait(p, "tensor", qrw, qrw.n)
            tm_ = run(p, "tensor", lambda e, lt=lt: e.matmul(
                PEB[:, :], QE[0:64, lt * 128:(lt + 1) * 128],
                EMBB[:, :], start=True, stop=True), pe)
            wait(p, "vector", pe, tm_)
            wait(p, "vector", epw, epw.n)
            epv.append(run(p, "vector", lambda e: e.tensor_copy(EPS_[:, :], PEB[:, :]), ev))
            wait(p, "sync", ev, epv[-1])
            dma(p, "sync", epw, epad.ap()[lt * 128:(lt + 1) * 128, :], EPS_[:, :])
        EPW = epw.n
        q2v = []
        for tc2 in range(S // 512):
            wait(p, "tensor", lda, LDA)
            wait(p, "tensor", ev, (q2v[-1] if q2v else epv[-1]))
            wait(p, "tensor", va, tdfb)
            wait(p, "tensor", qrw, qrw.n)
            tqa = run(p, "tensor", lambda e, tc2=tc2: e.matmul(
                PEB[0:2, 0:512], E2C[:, :],
                QE[0:64, tc2 * 512:tc2 * 512 + 512],
                start=True, stop=True), pe)
            wait(p, "vector", pe, tqa)
            q2v.append(run(p, "vector", lambda e, tc2=tc2: e.tensor_copy(
                QROWS[0:2, tc2 * 512:tc2 * 512 + 512], PEB[0:2, 0:512]), ev))
        wait(p, "sync", ev, q2v[-1])
        dma(p, "sync", qrw, QE[64:66, 0:S], QROWS[:, :])
        QRW = qrw.n
        for lt in range(S // 128):
            l0g = lt * 128
            shsl = (lt % 2) * 384
            wait(p, "sync", epw, EPW)
            if lt >= 2:
                wait(p, "sync", pe, pe.n)
            off = lt * 128 * JW + 127
            tsh = dma(p, "sync", epr[lt % 2], SH[:, shsl:shsl + 384],
                      bass.AP(epad, off, [[JW - 1, 128], [1, 384]]))
            expn = []
            pvn = []
            for grp in range(NKT // 4):
                bank = grp % 2
                wait(p, "tensor", hl, max(tk0, tk1))
                wait(p, "tensor", qrw, QRW)
                wait(p, "tensor", va, t_ke1)
                wait(p, "tensor", lda, LDA)
                if len(expn) >= 2:
                    wait(p, "tensor", ac, expn[-2])
                for j in range(4):
                    kt = grp * 4 + j
                    r0 = kt * 128
                    d = r0 - l0g
                    band = d in (-128, 0, 128)
                    flagr = 1 if d >= 256 else 0
                    kb = S if flagr else 0
                    last = (j == 3) and not band
                    tgrp = run(p, "tensor", lambda e, kt=kt, lt=lt, bank=bank, kb=kb, j=j, last=last:
                        e.matmul(PSC[bank][:, j * 128:(j + 1) * 128],
                                 KE[0:66, kb + kt * 128:kb + (kt + 1) * 128],
                                 QE[0:66, lt * 128:(lt + 1) * 128],
                                 start=(j == 0), stop=last), pe if last else None)
                    if band:
                        bidx = (d + 128) // 128
                        wait(p, "tensor", epr[lt % 2], tsh)
                        tgrp = run(p, "tensor", lambda e, bank=bank, shsl=shsl, bidx=bidx, j=j:
                            e.matmul(PSC[bank][:, j * 128:(j + 1) * 128],
                                     SH[:, shsl + bidx * 128:shsl + (bidx + 1) * 128],
                                     IDF[:, :], is_transpose=True,
                                     start=False, stop=(j == 3)), pe if j == 3 else None)
                wait(p, "scalar", pe, tgrp)
                if len(pvn) >= 8:
                    wait(p, "scalar", pe, pvn[-5])
                expn.append(run(p, "scalar", lambda e, grp=grp, bank=bank: e.activation(
                    EXPT[:, (grp % 2) * 512:(grp % 2) * 512 + 512], PSC[bank][:, :], AF.Exp,
                    scale=0.125, bias=CB20[:, 0:1]), ac))
                wait(p, "tensor", ac, expn[-1])
                if grp == 0:
                    wait(p, "tensor", ev, ev.n)
                for j in range(4):
                    kt = grp * 4 + j
                    pvn.append(run(p, "tensor", lambda e, kt=kt, h=h, grp=grp, j=j: e.matmul(
                        PAV[:, :], VEF[:, kt * 8 * 65 + h * 65:kt * 8 * 65 + (h + 1) * 65],
                        EXPT[:, (grp % 2) * 512 + j * 128:(grp % 2) * 512 + (j + 1) * 128],
                        start=(kt == 0), stop=(kt == NKT - 1)), pe))
            wait(p, "vector", pe, pvn[-1])
            if aov:
                wait(p, "vector", va, aov[-1])
            te = run(p, "vector", lambda e: e.tensor_copy(AFM[:, :], PAV[:, :]), ev)
            wait(p, "vector", ev, te)
            tr = run(p, "vector", lambda e: e.reciprocal(RDS[64:65, :], AFM[64:65, :]), va)
            wait(p, "tensor", va, tr)
            if aov:
                wait(p, "tensor", va, aov[-1])
            tb3 = run(p, "tensor", lambda e: e.matmul(
                PBC[:, :], ONEB[64:65, 0:64], RDS[64:65, :], start=True, stop=True), pe)
            wait(p, "vector", pe, tb3)
            aov.append(run(p, "vector", lambda e, h=h, lt=lt: e.tensor_mul(
                AO[(h % 2) * 64:(h % 2) * 64 + 64,
                   (h // 2) * S + lt * 128:(h // 2) * S + (lt + 1) * 128],
                AFM[0:64, :], PBC[:, :]), va))
    wait(p, "sync", va, aov[-1])
    if fz3:
        for ci in range(4):
            for gg in range(2):
                dma(p, "sync", st, aocc_ap[gg * 512 + ci * 128:gg * 512 + ci * 128 + 128, :],
                    AO[:, ci * S + gg * T:ci * S + gg * T + T])
    else:
        dma(p, "sync", st, fm_ap(ao_o), AO[:, :])
    wait(p, "sync", st, st.n)
    return p


def build_stage_l3proj(p=None, t=None):
    p = p or Prog("l3proj")
    nc = p.nc
    fzp = t is not None and "ao_ag" in t
    if fzp:
        aoag = t["ao_ag"]
        ofa_i = t["ofa"]
    else:
        af_i = _io(nc, t, "af", [D, T], BF16, "ExternalInput")
    wo_i = _io(nc, t, "wo", [D, D], F32, "ExternalInput")
    bo_i = _io(nc, t, "bo", [1, D], F32, "ExternalInput")
    ya_o = _io(nc, t, "ya", [T, D], F32, "ExternalOutput")
    AOF = p.sbuf([128, NC * T], BF16, "AOF")
    BO = p.sbuf([1, D], BF16, "BO")
    YS = p.sbuf([128, 2 * 512], F32, "YS")
    ONES1 = p.sbuf([1, 128], BF16, "ONES1")
    PY = [p.psum([128, 512], F32, f"py{i}") for i in range(2)]
    lda = p.sem("lda"); gld = p.sem("gld"); va = p.sem("va")
    pe = p.sem("pe"); ev = p.sem("ev")
    sty = [p.psem("sty0"), p.psem("sty1")]
    if fzp:
        OFA = p.sbuf([128, 8], mybir.dt.int32, "OFA")
        dma(p, "sync", lda, OFA[:, :], ofa_i)
        wait(p, "gpsimd", lda, lda.n)
        for ci in range(NC):
            gld.tick(16)
            p.on("gpsimd", lambda e, ci=ci, sm=gld.h: e.indirect_dma_start(
                AOF[:, ci * T:(ci + 1) * T], None, aoag,
                bass.IndirectOffsetOnAxis(OFA[:, ci:ci + 1], 0)).then_inc(sm, 16))
    else:
        dma(p, "sync", lda, AOF[:, :], fm_ap(af_i))
    LDA = lda.n
    dma(p, "gpsimd", gld, BO[:, :], bo_i)
    GLD = gld.n
    WO, wso = load_w_fm(p, None, wo_i, D, D, "WO")
    t_on = run(p, "vector", lambda e: e.memset(ONES1[:, :], 1.0), va)
    yv = []
    for tt in range(NT):
        for dc in range(2):
            k = tt * 2 + dc
            bank = k % 2
            wait(p, "tensor", wso, wso.n)
            wait(p, "tensor", gld, GLD)
            wait(p, "tensor", lda, LDA)
            wait(p, "tensor", va, t_on)
            if k >= 2:
                wait(p, "tensor", ev, yv[k - 2])
            for ci in range(NC):
                run(p, "tensor", lambda e, ci=ci, tt=tt, dc=dc, bank=bank:
                    e.matmul(PY[bank][:, :],
                             AOF[:, ci * T + tt * 128:ci * T + (tt + 1) * 128],
                             WO[:, ci * D + dc * 512:ci * D + dc * 512 + 512],
                             start=(ci == 0), stop=False), None)
            tv = run(p, "tensor", lambda e, dc=dc, bank=bank:
                e.matmul(PY[bank][:, :], ONES1[0:1, 0:128],
                         BO[0:1, dc * 512:dc * 512 + 512], start=False, stop=True), pe)
            wait(p, "vector", pe, tv)
            if k >= 2:
                wait(p, "vector", sty[k % 2], 16 * (k // 2))
            yv.append(run(p, "vector", lambda e, k=k, bank=bank: e.tensor_copy(
                YS[:, (k % 2) * 512:(k % 2) * 512 + 512], PY[bank][:, :]), ev))
            wait(p, "sync", ev, yv[-1])
            dma(p, "sync", sty[k % 2], ya_o[tt * 128:(tt + 1) * 128, dc * 512:dc * 512 + 512],
                YS[:, (k % 2) * 512:(k % 2) * 512 + 512])
    wait(p, "sync", sty[0], sty[0].n)
    wait(p, "sync", sty[1], sty[1].n)
    return p



def build_stage_l3ffn(TK=1024, p=None, t=None):
    I = 4096
    NI = I // 128
    NTK = TK // 128
    p = p or Prog("l3ffn")
    nc = p.nc
    xnfm_i = _io(nc, t, "xnfm", [D, TK], BF16, "ExternalInput")
    wg_i = _io(nc, t, "wg", [8, D, I], F32, "ExternalInput")
    wu_i = _io(nc, t, "wu", [8, D, I], F32, "ExternalInput")
    wd_i = _io(nc, t, "wd", [8, I, D], F32, "ExternalInput")
    y_o = _io(nc, t, "y3f", [TK, D], F32, "ExternalOutput")

    XF = p.sbuf([128, NC * TK], BF16, "XF")
    WGB = p.sbuf([128, 2 * NC * 512], BF16, "WGB")
    WUB = p.sbuf([128, 2 * NC * 512], BF16, "WUB")
    WDB = p.sbuf([128, NI * 512], BF16, "WDB")
    GS = p.sbuf([128, 512], F32, "GS")
    H = p.sbuf([128, NI * TK], BF16, "H")
    YACC = p.sbuf([128, NTK * D], F32, "YACC")
    PSA = p.psum([128, 512], F32, "psa")
    PSB2 = p.psum([128, 512], F32, "psb2")
    PY = [p.psum([128, 512], F32, f"py{i}") for i in range(2)]

    lda = p.sem("lda"); va = p.sem("va"); ac = p.sem("ac")
    pe = p.sem("pe"); ev = p.sem("ev"); st = p.psem("st")
    ldw = [p.sem("ldw0"), p.sem("ldw1")]
    ldd = [p.sem("ldd0"), p.sem("ldd1")]

    dma(p, "sync", lda, XF[:, :], fm_ap(xnfm_i))
    LDA = lda.n

    def load_gu(e, icg):
        k = (e * 8 + icg)
        s = ldw[k % 2]
        dma(p, "gpsimd", s, WGB[:, (k % 2) * NC * 512:((k % 2) + 1) * NC * 512],
            wg_i[e][:, icg * 512:(icg + 1) * 512].rearrange("(c p) n -> p c n", p=128))
        dma(p, "gpsimd", s, WUB[:, (k % 2) * NC * 512:((k % 2) + 1) * NC * 512],
            wu_i[e][:, icg * 512:(icg + 1) * 512].rearrange("(c p) n -> p c n", p=128))
        return s, s.n

    def load_wd(e, dc):
        k = e * 2 + dc
        s = ldd[k % 2]
        wait(p, "gpsimd", pe, pe.n)
        dma(p, "gpsimd", s, WDB[:, :],
            wd_i[e][:, dc * 512:(dc + 1) * 512].rearrange("(c p) n -> p c n", p=128))
        return s, s.n

    lg = load_gu(0, 0)
    yadd = []
    for e in range(8):
        hready = []
        for icg in range(8):
            s, tot = lg
            if e * 8 + icg + 1 < 64:
                nxt = (e * 8 + icg + 1)
                lg2 = None
            for ici in range(4):
                ic = icg * 4 + ici
                for tc in range(TK // 512):
                    wait(p, "tensor", s, tot)
                    wait(p, "tensor", lda, LDA)
                    if hready:
                        wait(p, "tensor", va, hready[-1])
                    if len(hready) >= 1:
                        wait(p, "tensor", ac, ac.n)
                    koff = ((e * 8 + icg) % 2) * NC * 512
                    for hcc in range(NC):
                        run(p, "tensor", lambda e2, hcc=hcc, ici=ici, tc=tc, koff=koff:
                            e2.matmul(PSA[:, :],
                                      WGB[:, koff + hcc * 512 + ici * 128:koff + hcc * 512 + (ici + 1) * 128],
                                      XF[:, hcc * TK + tc * 512:hcc * TK + tc * 512 + 512],
                                      start=(hcc == 0), stop=(hcc == NC - 1)),
                            pe if hcc == NC - 1 else None)
                    tg = pe.n
                    for hcc in range(NC):
                        run(p, "tensor", lambda e2, hcc=hcc, ici=ici, tc=tc, koff=koff:
                            e2.matmul(PSB2[:, :],
                                      WUB[:, koff + hcc * 512 + ici * 128:koff + hcc * 512 + (ici + 1) * 128],
                                      XF[:, hcc * TK + tc * 512:hcc * TK + tc * 512 + 512],
                                      start=(hcc == 0), stop=(hcc == NC - 1)),
                            pe if hcc == NC - 1 else None)
                    tu = pe.n
                    wait(p, "scalar", pe, tg)
                    tsg = run(p, "scalar", lambda e2: e2.activation(
                        GS[:, :], PSA[:, :], AF.Silu), ac)
                    wait(p, "vector", pe, tu)
                    wait(p, "vector", ac, tsg)
                    hready.append(run(p, "vector", lambda e2, ic=ic, tc=tc: e2.scalar_tensor_tensor(
                        H[:, ic * TK + tc * 512:ic * TK + tc * 512 + 512], PSB2[:, :], 1.0, GS[:, :],
                        ALU.mult, ALU.mult), va))
            if e * 8 + icg + 1 < 64:
                wait(p, "gpsimd", pe, pe.n)
                lg = load_gu((e * 8 + icg + 1) // 8, (e * 8 + icg + 1) % 8)
        for dc in range(2):
            sd, totd = load_wd(e, dc)
            for ts_ in range(NTK):
                bank = ts_ % 2
                wait(p, "tensor", sd, totd)
                wait(p, "tensor", va, hready[-1])
                if len(yadd) >= 2:
                    wait(p, "tensor", va, yadd[-2])
                for ic in range(NI):
                    run(p, "tensor", lambda e2, ic=ic, ts_=ts_, bank=bank: e2.matmul(
                        PY[bank][:, :], H[:, ic * TK + ts_ * 128:ic * TK + (ts_ + 1) * 128],
                        WDB[:, ic * 512:(ic + 1) * 512],
                        start=(ic == 0), stop=(ic == NI - 1)), pe if ic == NI - 1 else None)
                wait(p, "vector", pe, pe.n)
                if e == 0:
                    yadd.append(run(p, "vector", lambda e2, ts_=ts_, dc=dc, bank=bank: e2.tensor_copy(
                        YACC[:, ts_ * D + dc * 512:ts_ * D + (dc + 1) * 512], PY[bank][:, :]), va))
                else:
                    yadd.append(run(p, "vector", lambda e2, ts_=ts_, dc=dc, bank=bank: e2.tensor_add(
                        YACC[:, ts_ * D + dc * 512:ts_ * D + (dc + 1) * 512],
                        YACC[:, ts_ * D + dc * 512:ts_ * D + (dc + 1) * 512], PY[bank][:, :]), va))
    wait(p, "sync", va, yadd[-1])
    dma(p, "sync", st, y_o.rearrange("(t p) d -> p t d", p=128), YACC[:, :])
    wait(p, "sync", st, st.n)
    return p



def build_stage_final(p=None, t=None):
    p = p or Prog("final")
    nc = p.nc
    xin = _io(nc, t, "xin", [T, D], F32, "ExternalInput")
    ya = _io(nc, t, "ya", [T, D], F32, "ExternalInput")
    yf = _io(nc, t, "yf", [T, D], F32, "ExternalInput")
    out = _io(nc, t, "out", [T, D], F32, "ExternalOutput")
    X = p.sbuf([128, NT * D], F32, "X")
    YA = p.sbuf([128, 2 * D], F32, "YA")
    YF = p.sbuf([128, 2 * D], F32, "YF")
    lda = p.sem("lda"); va = p.sem("va"); st = p.psem("st")
    ldy = [p.sem("ldy0"), p.sem("ldy1")]
    dma(p, "sync", lda, X[:, :], tm_ap(xin))
    LDA = lda.n
    addv = []
    for t in range(NT):
        bsl = slice((t % 2) * D, (t % 2) * D + D)
        if t >= 2:
            wait(p, "sync", va, addv[t - 2])
        dma(p, "sync", ldy[t % 2], YA[:, bsl], ya[t * 128:(t + 1) * 128, :])
        ty = dma(p, "sync", ldy[t % 2], YF[:, bsl], yf[t * 128:(t + 1) * 128, :])
        wait(p, "vector", lda, LDA)
        wait(p, "vector", ldy[t % 2], ty)
        run(p, "vector", lambda e, t=t, bsl=bsl: e.tensor_add(
            X[:, t * D:(t + 1) * D], X[:, t * D:(t + 1) * D], YA[:, bsl]))
        addv.append(run(p, "vector", lambda e, t=t, bsl=bsl: e.tensor_add(
            X[:, t * D:(t + 1) * D], X[:, t * D:(t + 1) * D], YF[:, bsl]), va))
    wait(p, "sync", va, addv[-1])
    dma(p, "sync", st, tm_ap(out), X[:, :])
    wait(p, "sync", st, st.n)
    return p




_PROG_CACHE = {}

def _prog(key, builder, *args, **kw):
    if key not in _PROG_CACHE:
        nc = builder(*args, **kw).finish()
        nc._dbg_name = str(key)
        _PROG_CACHE[key] = nc
    return _PROG_CACHE[key]

def _shard_tm(x_full):
    return [np.ascontiguousarray(x_full[c // 2, (c % 2) * T:(c % 2 + 1) * T]) for c in range(NCORE)]

def _unshard_tm(parts):
    x = np.zeros((B, S, D), np.float32)
    for c in range(NCORE):
        x[c // 2, (c % 2) * T:(c % 2 + 1) * T] = parts[c]
    return x

def _ppcols(b):
    return np.ascontiguousarray(np.asarray(b, np.float32).reshape(8, 128).T)

import sys as _sys
def _run(nc, in_maps, cores=None):
    print(f"[kernel] launch {getattr(nc, '_dbg_name', '?')}", file=_sys.stderr, flush=True)
    res = run_bass_kernel_spmd(nc, in_maps, core_ids=cores or list(range(NCORE)))
    return res.results

def _l0_ropetabs(half):
    pos = np.arange(half * T, (half + 1) * T, dtype=np.float32)
    inv = 32.0 ** (-np.arange(0, D, 2, dtype=np.float32) / D)
    ang = pos[None, :] * inv[:, None]
    return np.cos(ang).astype(BF), np.sin(ang).astype(BF)

def _l1_perm():
    pm = np.zeros(1024, np.int64)
    for h in range(16):
        b0 = h * 64
        pm[b0:b0 + 32] = b0 + np.arange(0, 64, 2)
        pm[b0 + 32:b0 + 64] = b0 + np.arange(1, 64, 2)
    return pm

def _l1_ropetabs(half):
    pos = np.arange(half * T, (half + 1) * T, dtype=np.float32)
    inv = 1.0 / 10000.0 ** (np.arange(0, 64, 2, dtype=np.float32) / 64)
    ang = pos[None, :] * inv[:, None]
    c32, s32 = np.cos(ang), np.sin(ang)
    cosr = np.tile(np.concatenate([c32, c32], 0), (2, 1)).astype(BF)
    sinr = np.tile(np.concatenate([s32, s32], 0), (2, 1)).astype(BF)
    return cosr, sinr

def _l1_rotperm():
    R = np.zeros((128, 128), np.float32)
    for b0 in (0, 64):
        for j in range(32):
            R[b0 + 32 + j, b0 + j] = -1.0
            R[b0 + j, b0 + 32 + j] = 1.0
    return R.astype(BF)

def _cmask():
    m = np.zeros((128, 4, 512), np.float32)
    for j in range(4):
        for pp_ in range(128):
            m[pp_, j, :] = np.where(np.arange(512) >= pp_ + 128 * j, 0.0, -1e9)
    return np.ascontiguousarray(m.reshape(128, 2048))

def _causal_tabs(g, cm):
    bcn = np.zeros((128, 4, 32), np.float32)
    for qc in range(4):
        base = g * 16 + qc * 4
        bcn[:, qc, base + 4:] = -1e9
    smsk2 = np.zeros((128, 8, 512), np.float32)
    for j in range(4):
        smsk2[:, g * 4 + j] = cm[:, j * 512:(j + 1) * 512]
    return (np.ascontiguousarray(bcn.reshape(128, 128)),
            np.ascontiguousarray(smsk2.reshape(128, 4096)).astype(BF))

def _lmask():
    m = np.full((128, 2, 128), -1e9, np.float32)
    m[64:128, 0, 0:64] = 0.0
    m[0:64, 1, 64:128] = 0.0
    return np.ascontiguousarray(m.reshape(128, 256))

def _vef(vparts, b, nh, h0=0):
    vfull = np.concatenate([np.asarray(vparts[2 * b], np.float32),
                            np.asarray(vparts[2 * b + 1], np.float32)], 0)
    vef = np.ones((S, nh * 65), np.float32)
    for hh in range(nh):
        vef[:, hh * 65:hh * 65 + 64] = vfull[:, (h0 + hh) * 64:(h0 + hh + 1) * 64]
    return vef.astype(BF)

def _half_run(key, builder, per_core_inputs):
    outs = [None] * NCORE
    for half in (0, 1):
        nc = _prog((key, half), builder, half)
        cores = [c for c in range(NCORE) if c % 2 == half]
        res = run_bass_kernel_spmd(nc, [per_core_inputs[c] for c in cores], core_ids=cores)
        for i, c in enumerate(cores):
            outs[c] = res.results[i]
    return outs


FUSE_UPTO = 4


def build_fused(upto):
    p = Prog("mega")
    nc = p.nc
    cc = p.sem("cc")
    G2 = [[0, 1], [2, 3], [4, 5], [6, 7]]
    G8 = [list(range(8))]

    def xin_(name, shape, dt):
        return nc.dram_tensor(name, shape, dt, kind="ExternalInput").ap()

    def dram(name, shape, dt, ext=False, shared=False):
        if ext:
            return nc.dram_tensor(name, shape, dt, kind="ExternalOutput")
        if shared:
            return nc.dram_tensor(name, shape, dt, addr_space="Shared")
        return nc.dram_tensor(name, shape, dt)

    def ag(ins, outs, groups=G2):
        tcc = cc.tick(1)
        p.on("gpsimd", lambda e, i=ins, o=outs, g=groups: e.collective_compute(
            "AllGather", ALU.bypass, replica_groups=g,
            ins=[i.ap().opt()], outs=[o.ap().opt()]).then_inc(cc.h))
        wait(p, "sync", cc, tcc)
        wait(p, "gpsimd", cc, tcc)
        return tcc

    ID_BF = xin_("identbf", [128, 128], BF16)
    CCON = xin_("cconst", [128, 8], F32)

    x_in = xin_("xin", [T, D], F32)
    xn0 = dram("xn0", [T, D], F32)
    xnew0 = dram("xnew0", [T, D], F32)
    xnfm0 = dram("xnfm0", [D, T], BF16)
    p.phase("ln0")
    build_stage_ln(False, None, p=p, t=dict(
        xin=x_in, identbf=ID_BF, cconst=CCON,
        xnew=xnew0.ap(), xn=xn0.ap(), xnfm=xnfm0.ap()))

    qr0 = dram("qr0", [D, T], BF16)
    kcc0 = dram("kcc0", [D, T], BF16)
    p.phase("l0qk")
    build_stage_l0qk(p=p, t=dict(
        xnfm=xnfm0.ap(), wq=xin_("l0qk_wq", [D, D], F32), wk=xin_("l0qk_wk", [D, D], F32),
        cosf=xin_("l0qk_cosf", [512, T], BF16), sinf=xin_("l0qk_sinf", [512, T], BF16),
        qr=qr0.ap(), kr=kcc0.ap()))

    kag0 = dram("kag0", [2 * D, T], BF16)
    y0a = dram("y0a", [T, D], F32, ext=(upto == 1))
    p.phase("l0attn")
    ag(kcc0, kag0)
    build_stage_l0attn(p=p, t=dict(
        qr=qr0.ap(), krf_ag=kag0.ap(),
        fc1p=xin_("l0a_fc1p", [1, 16], F32), fc1b=xin_("l0a_fc1b", [16, 1], F32),
        fc2p=xin_("l0a_fc2p", [17, D], F32),
        bcn=(MBCN := xin_("l0a_bcn", [128, 128], F32)),
        smsk2=(MSMSK := xin_("l0a_smsk2", [128, 8 * 512], BF16)),
        y0a=y0a.ap()))

    WQI = 2816
    per_gu = [3, 3, 3, 3, 3, 3, 2, 2]
    wgt = dram("wgt", [1024, 384], BF16)
    wut = dram("wut", [1024, 384], BF16)
    wdt = dram("wdt", [WQI, 128], BF16)
    wgag = dram("wg_ag", [8 * 1024, 384], BF16, shared=True)
    wuag = dram("wu_ag", [8 * 1024, 384], BF16, shared=True)
    wdag = dram("wd_ag", [8 * WQI, 128], BF16, shared=True)
    p.phase("wqg")
    build_stage_l0wq(3, 1024, p=p, t=dict(
        w=xin_("wqg_w", [3 * 128, 1024], F32), wqt=wgt.ap(), identbf=ID_BF))
    p.phase("wqu")
    build_stage_l0wq(3, 1024, p=p, t=dict(
        w=xin_("wqu_w", [3 * 128, 1024], F32), wqt=wut.ap(), identbf=ID_BF))
    p.phase("wqd")
    build_stage_l0wq(1, WQI, p=p, t=dict(
        w=xin_("wqd_w", [128, WQI], F32), wqt=wdt.ap(), identbf=ID_BF))

    y0f = dram("y0f", [T, D], F32, ext=(upto == 1))
    p.phase("l0ffn")
    ag(wgt, wgag, G8)
    ag(wut, wuag, G8)
    ag(wdt, wdag, G8)
    build_stage_l0ffn(p=p, t=dict(
        xn=xn0.ap(), identbf=ID_BF,
        w_ags=(wgag.ap(), wuag.ap(), wdag.ap(), per_gu), y0f=y0f.ap()))
    if upto == 1:
        return p

    def vprefill(vt, width, rows):
        ONEV = p.sbuf([128, width], BF16, "ONEV")
        va = p.sem("va")
        vpre = p.sem("vpre")
        tm_ = run(p, "vector", lambda e, O=ONEV: e.memset(O[:, :], 1.0), va)
        wait(p, "sync", va, tm_)
        for tt in range(rows // 128):
            dma(p, "sync", vpre, vt.ap()[tt * 128:(tt + 1) * 128, :], ONEV[:, :])
        wait(p, "sync", vpre, vpre.n)

    xnew1 = dram("xnew1", [T, D], F32, ext=(upto == 2))
    xn1 = dram("xn1", [T, D], F32)
    xnfm1 = dram("xnfm1", [D, T], BF16)
    p.phase("ln1")
    build_stage_ln(True, None, p=p, t=dict(
        xin=x_in, ya=y0a.ap(), yf=y0f.ap(), identbf=ID_BF, cconst=CCON,
        xnew=xnew1.ap(), xn=xn1.ap(), xnfm=xnfm1.ap()))

    qr1 = dram("qr1", [D, T], BF16)
    kcc1 = dram("kcc1", [D, T], BF16)
    vcc1 = dram("vcc1", [T, 16 * 65], BF16)
    p.phase("l1qkv")
    vprefill(vcc1, 1040, T)
    build_stage_l1qkv(p=p, t=dict(
        xnfm=xnfm1.ap(),
        wq=xin_("l1_wq", [D, D], F32), wk=xin_("l1_wk", [D, D], F32),
        wv=xin_("l1_wv", [D, D], F32),
        bq=xin_("l1_bq", [128, 8], F32), bk=xin_("l1_bk", [128, 8], F32),
        bv=xin_("l1_bv", [1, D], F32),
        cosr=xin_("l1_cosr", [128, T], BF16), sinr=xin_("l1_sinr", [128, T], BF16),
        rotperm=xin_("l1_rotperm", [128, 128], BF16),
        qr=qr1.ap(), kr=kcc1.ap(), vemode=("ve16", vcc1)))

    kag1 = dram("kag1", [2 * D, T], BF16)
    vag1 = dram("vag1", [S, 16 * 65], BF16)
    y1a = dram("y1a", [T, D], F32, ext=(upto == 2))
    p.phase("l1attn")
    ag(kcc1, kag1)
    ag(vcc1, vag1)
    build_stage_mha(causal=True, scale=0.125, name="l1attn", p=p, t=dict(
        qr=qr1.ap(), krf_ag=kag1.ap(), vef=vag1.ap(),
        wo=xin_("l1_wo", [D, D], F32), bo=xin_("l1_bo", [1, D], F32),
        bcn=MBCN, smsk2=MSMSK, ya=y1a.ap()))

    y1f = dram("y1f", [T, D], F32, ext=(upto == 2))
    p.phase("l1ffn")
    build_stage_l1ffn(p=p, t=dict(
        xnfm=xnfm1.ap(),
        sgw=xin_("l1f_sgw", [D, 2048], F32), suw=xin_("l1f_suw", [D, 2048], F32),
        sdw=xin_("l1f_sdw", [2048, D], F32),
        rw=xin_("l1f_rw", [D, 8], F32), rb=xin_("l1f_rb", [1, 8], F32),
        wg=xin_("l1f_wg", [8, D, 2048], F32), wu=xin_("l1f_wu", [8, D, 2048], F32),
        wd=xin_("l1f_wd", [8, 2048, D], F32),
        y1f=y1f.ap()))
    if upto == 2:
        return p

    xnew2 = dram("xnew2", [T, D], F32, ext=(upto == 3))
    xn2 = dram("xn2", [T, D], F32, ext=(upto == 3))
    znfm = dram("znfm", [D, T], BF16)
    c2d = dram("c2d", [128, NT], F32)
    p.phase("ln2")
    build_stage_ln(True, "c2", p=p, t=dict(
        xin=xnew1.ap(), ya=y1a.ap(), yf=y1f.ap(), identbf=ID_BF, cconst=CCON,
        xnew=xnew2.ap(), xn=xn2.ap(), xnfm=znfm.ap(), c2=c2d.ap()))

    qr2 = dram("qr2", [D, T], BF16)
    kcc2 = dram("kcc2", [D, T], BF16)
    vcc2 = dram("vcc2", [T, 16 * 65], BF16)
    p.phase("l2qkv")
    vprefill(vcc2, 1040, T)
    build_stage_l1qkv(True, False, "l2qkv", p=p, t=dict(
        xnfm=znfm.ap(),
        wq=xin_("l2_wq", [D, D], F32), wk=xin_("l2_wk", [D, D], F32),
        wv=xin_("l2_wv", [D, D], F32),
        bq=xin_("l2_bq", [128, 8], F32), bk=xin_("l2_bk", [128, 8], F32),
        bv=xin_("l2_bv", [1, D], F32),
        lnw=xin_("l2_lnw", [128, 8], F32), lnb=xin_("l2_lnb", [128, 8], F32),
        qr=qr2.ap(), kr=kcc2.ap(), vemode=("ve16", vcc2)))

    kag2 = dram("kag2", [2 * D, T], BF16)
    vag2 = dram("vag2", [S, 16 * 65], BF16)
    y2a = dram("y2a", [T, D], F32, ext=(upto == 3))
    p.phase("l2attn")
    ag(kcc2, kag2)
    ag(vcc2, vag2)
    build_stage_mha(causal=False, scale=0.125, name="l2attn", p=p, t=dict(
        qr=qr2.ap(), krf_ag=kag2.ap(), vef=vag2.ap(),
        wo=xin_("l2_wo", [D, D], F32), bo=xin_("l2_bo", [1, D], F32), ya=y2a.ap()))

    qlcc = dram("qlcc", [512, T], BF16)
    klcc = dram("klcc", [512, T], BF16)
    vlcc = dram("vlcc", [T, 520], BF16)
    p.phase("l2locqkv")
    vprefill(vlcc, 520, T)
    build_stage_l2locqkv(p=p, t=dict(
        xnfm=znfm.ap(), lnw=xin_("l2l_lnw", [128, 8], F32), lnb=xin_("l2l_lnb", [128, 8], F32),
        wq=xin_("l2l_wq", [D, 512], F32), wk=xin_("l2l_wk", [D, 512], F32),
        wv=xin_("l2l_wv", [D, 512], F32),
        ql=qlcc.ap(), kl=klcc.ap(), vemode=("ve8", vlcc)))

    qlag = dram("qlag", [1024, T], BF16)
    klag = dram("klag", [1024, T], BF16)
    vlag = dram("vlag", [S, 520], BF16)
    lightf = dram("lightf", [S, D], F32)
    p.phase("l2locattn")
    ag(qlcc, qlag)
    ag(klcc, klag)
    ag(vlcc, vlag)
    build_stage_l2locattn(p=p, t=dict(
        ql_ag=qlag.ap(), kl_ag=klag.ap(), vlef=vlag.ap(),
        wol=xin_("l2l_wol", [512, D], F32), lmask=xin_("l2l_lmask", [128, 2 * 128], F32),
        light=lightf.ap()))

    qh2 = dram("qh2", [512, T], BF16)
    khcc = dram("khcc", [512, T], BF16)
    vhcc = dram("vhcc", [T, 520], BF16)
    scc = dram("scc", [128, 2 * NT], F32)
    p.phase("l2hkv")
    vprefill(vhcc, 520, T)
    build_stage_l2hkv(p=p, t=dict(
        xnfm=znfm.ap(), gam=xin_("l2h_gam", [128, 8], F32), c2=c2d.ap(),
        rt=xin_("l2h_rt", [D, 2], F32),
        wq=xin_("l2h_wq", [D, 512], F32), wk=xin_("l2h_wk", [D, 512], F32),
        wv=xin_("l2h_wv", [D, 512], F32),
        qh=qh2.ap(), kh=khcc.ap(), vemode=("ve8", vhcc), s=scc.ap()))

    khag2 = dram("khag2", [1024, T], BF16)
    vhag2 = dram("vhag2", [S, 520], BF16)
    sag2 = dram("sag2", [256, 2 * NT], F32)
    mrow = dram("mrow", [1, S], F32)
    y2f = dram("y2f", [T, D], F32, ext=(upto == 3))
    p.phase("l2heavy")
    ag(khcc, khag2)
    ag(vhcc, vhag2)
    ag(scc, sag2)
    build_stage_l2heavy(p=p, t=dict(
        qh=qh2.ap(), kh_ag=khag2.ap(), vh_ag=vhag2.ap(), s_ag=sag2.ap(),
        nk=xin_("l2h_nk", [128, 4], F32), nv=xin_("l2h_nv", [1, 520], F32),
        lof=xin_("l2h_lof", [128, NT], mybir.dt.int32),
        identf=(ID_F32 := xin_("identf", [128, 128], F32)), mrow_h=mrow, light=lightf.ap(),
        sqtm=scc.ap()[:, 0:NT],
        kcon=xin_("l2h_kcon", [1, 2], F32), woh=xin_("l2h_woh", [512, D], F32),
        nqb=xin_("l2h_nqb", [128, D], F32), y2f=y2f.ap()))
    if upto == 3:
        return p

    I32 = mybir.dt.int32
    xnew3 = dram("xnew3", [T, D], F32)
    xn3 = dram("xn3", [T, D], F32)
    xnfm3 = dram("xnfm3", [D, T], BF16)
    p.phase("ln3")
    build_stage_ln(True, None, True, p=p, t=dict(
        xin=xnew2.ap(), ya=y2a.ap(), yf=y2f.ap(), y3=xn2.ap(),
        identbf=ID_BF, cconst=CCON,
        xnew=xnew3.ap(), xn=xn3.ap(), xnfm=xnfm3.ap()))

    qcc3 = dram("qcc3", [D, T], BF16)
    kcc3 = dram("kcc3", [D, T], BF16)
    vcc3 = dram("vcc3", [2 * T, 520], BF16)
    p.phase("l3qkv")
    vprefill(vcc3, 520, 2 * T)
    build_stage_l1qkv(False, False, "l3qkv", p=p, t=dict(
        xnfm=xnfm3.ap(),
        wq=xin_("l3_wq", [D, D], F32), wk=xin_("l3_wk", [D, D], F32),
        wv=xin_("l3_wv", [D, D], F32),
        bq=xin_("l3_bq", [128, 8], F32), bk=xin_("l3_bk", [128, 8], F32),
        bv=xin_("l3_bv", [1, D], F32),
        qr=qcc3.ap(), kr=kcc3.ap(), vemode=("ve8x2", vcc3)))

    qag3 = dram("qag3", [2 * D, T], BF16)
    kag3 = dram("kag3", [2 * D, T], BF16)
    vag3 = dram("vag3", [4 * T, 520], BF16)
    aocc = dram("aocc", [1024, T], BF16)
    p.phase("l3attn")
    ag(qcc3, qag3)
    ag(kcc3, kag3)
    ag(vcc3, vag3)
    build_stage_l3attn(p=p, t=dict(
        qk_ag=qag3.ap(), kk_ag=kag3.ap(), v_ag=vag3.ap(),
        ofq=xin_("l3_ofq", [128, 16], I32), ofk=xin_("l3_ofk", [64, 16], I32),
        ofv=xin_("l3_ofv", [128, 32], I32), aocc=aocc.ap(),
        embt=xin_("l3_embt", [64, 73], F32), identf=ID_F32))

    aoag = dram("aoag", [2048, T], BF16)
    y3a = dram("y3a", [T, D], F32)
    p.phase("l3proj")
    ag(aocc, aoag)
    build_stage_l3proj(p=p, t=dict(
        ao_ag=aoag.ap(), ofa=xin_("l3_ofa", [128, 8], I32),
        wo=xin_("l3_wo", [D, D], F32), bo=xin_("l3_bo", [1, D], F32), ya=y3a.ap()))

    y3f = dram("y3f", [T, D], F32)
    wg3 = xin_("l3f_wg", [8, D, 4096], F32)
    wu3 = xin_("l3f_wu", [8, D, 4096], F32)
    wd3 = xin_("l3f_wd", [8, 4096, D], F32)
    for part in (0, 1):
        p.phase(f"l3ffn{part}")
        build_stage_l3ffn(1024, p=p, t=dict(
            xnfm=xnfm3.ap()[:, part * 1024:(part + 1) * 1024],
            wg=wg3, wu=wu3, wd=wd3,
            y3f=y3f.ap()[part * 1024:(part + 1) * 1024, :]))

    outd = dram("out", [T, D], F32, ext=True)
    p.phase("final")
    build_stage_final(p=p, t=dict(
        xin=xnew3.ap(), ya=y3a.ap(), yf=y3f.ap(), out=outd.ap()))
    return p


def _fused_inputs(Z, upto):
    g = Z
    pk = pack_lhsT
    cm = _cmask()
    ctabs = [_causal_tabs(gg, cm) for gg in (0, 1)]
    xs = _shard_tm(np.asarray(g["L0_attn_hidden_states"], np.float32))
    wq0 = pk(g["L0_attn_Wq"].T)
    wk0 = pk(g["L0_attn_Wk"].T)
    fc1p = np.ascontiguousarray(np.asarray(g["L0_attn_fc1_w"], np.float32).T)
    fc1b = np.ascontiguousarray(np.asarray(g["L0_attn_fc1_b"], np.float32)[:, None])
    fc2p = np.concatenate([np.asarray(g["L0_attn_fc2_w"], np.float32).T,
                           np.asarray(g["L0_attn_fc2_b"], np.float32)[None, :]], 0)
    shards = {}
    for nm, w, per in (("wqg_w", g["L0_ffn_w_gate"], [3, 3, 3, 3, 3, 3, 2, 2]),
                       ("wqu_w", g["L0_ffn_w_up"], [3, 3, 3, 3, 3, 3, 2, 2]),
                       ("wqd_w", g["L0_ffn_w_down"], [1] * 8)):
        w = np.asarray(w, np.float32)
        Wc = w.shape[1]
        offs = np.cumsum([0] + per)
        ls = []
        for c in range(NCORE):
            sl = w[offs[c] * 128:(offs[c] + per[c]) * 128]
            if per[c] < max(per):
                sl = np.concatenate([sl, np.zeros((128 * (max(per) - per[c]), Wc), np.float32)], 0)
            ls.append(np.ascontiguousarray(sl))
        shards[nm] = ls
    ims = []
    for c in range(NCORE):
        cosf, sinf = _l0_ropetabs(c % 2)
        im = dict(xin=xs[c], identbf=IDENT_BF, cconst=CCONST,
                  l0qk_wq=wq0, l0qk_wk=wk0, l0qk_cosf=cosf, l0qk_sinf=sinf,
                  l0a_fc1p=fc1p, l0a_fc1b=fc1b, l0a_fc2p=fc2p,
                  l0a_bcn=ctabs[c % 2][0], l0a_smsk2=ctabs[c % 2][1],
                  wqg_w=shards["wqg_w"][c], wqu_w=shards["wqu_w"][c],
                  wqd_w=shards["wqd_w"][c])
        ims.append(im)
    if upto >= 2:
        pm = _l1_perm()
        rpm = _l1_rotperm()
        com = dict(
            l1_wq=pk(np.asarray(g["L1_attn_Wq"], np.float32)[:, pm]),
            l1_wk=pk(np.asarray(g["L1_attn_Wk"], np.float32)[:, pm]),
            l1_wv=pk(g["L1_attn_Wv"]),
            l1_bq=_ppcols(np.asarray(g["L1_attn_bq"], np.float32)[pm]),
            l1_bk=_ppcols(np.asarray(g["L1_attn_bk"], np.float32)[pm]),
            l1_bv=np.ascontiguousarray(np.asarray(g["L1_attn_bv"], np.float32)[None, :]),
            l1_rotperm=rpm,
            l1_wo=pk(g["L1_attn_Wo"]),
            l1_bo=np.ascontiguousarray(np.asarray(g["L1_attn_bo"], np.float32)[None, :]),
            l1f_sgw=pk(np.asarray(g["L1_ffn_sg_w"], np.float32).T),
            l1f_suw=pk(np.asarray(g["L1_ffn_su_w"], np.float32).T),
            l1f_sdw=pk(np.asarray(g["L1_ffn_sd_w"], np.float32).T),
            l1f_rw=pk(np.asarray(g["L1_ffn_router_w"], np.float32).T),
            l1f_rb=np.ascontiguousarray(np.asarray(g["L1_ffn_routing_bias"], np.float32)[None, :]),
            l1f_wg=np.ascontiguousarray(np.asarray(g["L1_ffn_wg"], np.float32).transpose(0, 2, 1)),
            l1f_wu=np.ascontiguousarray(np.asarray(g["L1_ffn_wu"], np.float32).transpose(0, 2, 1)),
            l1f_wd=np.ascontiguousarray(np.asarray(g["L1_ffn_wd"], np.float32).transpose(0, 2, 1)))
        for c in range(NCORE):
            cosr, sinr = _l1_ropetabs(c % 2)
            ims[c].update(com)
            ims[c]["l1_cosr"] = cosr
            ims[c]["l1_sinr"] = sinr
    if upto >= 3:
        Wq2 = np.asarray(g["L2_attn_W_qkv"], np.float32)
        bqkv = np.asarray(g["L2_attn_b_qkv"], np.float32)
        Wl = np.asarray(g["L2_ffn_Wqkv"], np.float32)
        Wkv = np.asarray(g["L2_ffn_Wkv"], np.float32)
        wkv_k = np.concatenate([Wkv[h * 128:h * 128 + 64] for h in range(8)], 0)
        wkv_v = np.concatenate([Wkv[h * 128 + 64:h * 128 + 128] for h in range(8)], 0)
        nullk = np.asarray(g["L2_ffn_null_kv"], np.float32)[0]
        nullv = np.asarray(g["L2_ffn_null_kv"], np.float32)[1]
        nv = np.ones((1, 520), np.float32)
        for hh in range(8):
            nv[0, hh * 65:hh * 65 + 64] = nullv[hh]
        com = dict(
            l2_wq=pk(Wq2[:, 0:1024]), l2_wk=pk(Wq2[:, 1024:2048]), l2_wv=pk(Wq2[:, 2048:3072]),
            l2_bq=_ppcols(bqkv[0:1024]), l2_bk=_ppcols(bqkv[1024:2048]),
            l2_bv=np.ascontiguousarray(bqkv[2048:3072][None, :]),
            l2_lnw=_ppcols(g["L2_attn_ln_w"]), l2_lnb=_ppcols(g["L2_attn_ln_b"]),
            l2_wo=pk(g["L2_attn_W_proj"]),
            l2_bo=np.ascontiguousarray(np.asarray(g["L2_attn_b_proj"], np.float32)[None, :]),
            l2l_lnw=_ppcols(g["L2_ffn_ln_w"]), l2l_lnb=_ppcols(g["L2_ffn_ln_b"]),
            l2l_wq=pk(Wl[0:512].T), l2l_wk=pk(Wl[512:1024].T), l2l_wv=pk(Wl[1024:1536].T),
            l2l_wol=pk(np.asarray(g["L2_ffn_Wout_l"], np.float32).T),
            l2l_lmask=_lmask(),
            l2h_gam=_ppcols(g["L2_ffn_gamma"]),
            l2h_rt=pk(np.concatenate([np.asarray(g["L2_ffn_rt_q"], np.float32),
                                      np.asarray(g["L2_ffn_rt_kv"], np.float32)], 0).T),
            l2h_wq=pk(np.asarray(g["L2_ffn_Wq"], np.float32).T),
            l2h_wk=pk(wkv_k.T), l2h_wv=pk(wkv_v.T),
            l2h_nk=np.ascontiguousarray(nullk.reshape(4, 128).T),
            l2h_nv=nv,
            l2h_kcon=np.array([[512.0, 1024.0]], np.float32),
            l2h_woh=pk(np.asarray(g["L2_ffn_Wout_h"], np.float32).T),
            l2h_nqb=np.tile(np.asarray(g["L2_ffn_null_q"], np.float32)[None, :], (128, 1)),
            identf=IDENT_F32)
        for c in range(NCORE):
            lof = np.zeros((128, NT), np.int32)
            for tt in range(NT):
                lof[:, tt] = (c % 2) * T + tt * 128 + np.arange(128)
            ims[c].update(com)
            ims[c]["l2h_lof"] = lof
    if upto >= 4:
        com = dict(
            l3_wq=pk(np.asarray(g["L3_attn_Wq"], np.float32).T),
            l3_wk=pk(np.asarray(g["L3_attn_Wk"], np.float32).T),
            l3_wv=pk(np.asarray(g["L3_attn_Wv"], np.float32).T),
            l3_bq=_ppcols(g["L3_attn_bq"]), l3_bk=_ppcols(g["L3_attn_bk"]),
            l3_bv=np.ascontiguousarray(np.asarray(g["L3_attn_bv"], np.float32)[None, :]),
            l3_embt=np.ascontiguousarray(np.asarray(g["L3_attn_dist_emb"], np.float32).T),
            l3_wo=pk(np.asarray(g["L3_attn_Wo"], np.float32).T),
            l3_bo=np.ascontiguousarray(np.asarray(g["L3_attn_bo"], np.float32)[None, :]),
            l3f_wg=np.ascontiguousarray(np.asarray(g["L3_ffn_Wg"], np.float32)),
            l3f_wu=np.ascontiguousarray(np.asarray(g["L3_ffn_Wu"], np.float32)),
            l3f_wd=np.ascontiguousarray(np.asarray(g["L3_ffn_Wd"], np.float32)))
        pp = np.arange(128)
        for c in range(NCORE):
            gq = c % 2
            ofq = np.zeros((128, 16), np.int32)
            ofk = np.zeros((64, 16), np.int32)
            ofv = np.zeros((128, 32), np.int32)
            ofa = np.zeros((128, 8), np.int32)
            for hc in range(4):
                for gg in range(2):
                    ofq[:, hc * 2 + gg] = gg * D + gq * 512 + hc * 128 + pp
            for h in range(8):
                for gg in range(2):
                    ofk[:, h * 2 + gg] = gg * D + gq * 512 + h * 64 + np.arange(64)
            for kt in range(32):
                ofv[:, kt] = (gq + (2 if kt >= 16 else 0)) * T + (kt % 16) * 128 + pp
            for ci in range(8):
                H = 2 * ci + (pp >= 64).astype(np.int32)
                ofa[:, ci] = np.where(H < 8, 0, 1024) + gq * 512 + \
                    128 * ((H % 8) // 2) + 64 * (H % 2) + (pp % 64)
            ims[c]["l3_ofq"] = ofq
            ims[c]["l3_ofk"] = ofk
            ims[c]["l3_ofv"] = ofv
            ims[c]["l3_ofa"] = ofa
            ims[c].update(com)
    return ims, xs


def _fused_run(Z, st, upto):
    nc = _prog(("mega", upto), build_fused, upto)
    ims, xs = _fused_inputs(Z, upto)
    r = _run(nc, ims)
    st["x0"] = xs
    if upto == 1:
        st["y0a"] = [r[c]["y0a"] for c in range(NCORE)]
        st["y0f"] = [r[c]["y0f"] for c in range(NCORE)]
    elif upto == 2:
        st["x1"] = [r[c]["xnew1"] for c in range(NCORE)]
        st["y1a"] = [r[c]["y1a"] for c in range(NCORE)]
        st["y1f"] = [r[c]["y1f"] for c in range(NCORE)]
    elif upto == 3:
        st["x2"] = [r[c]["xnew2"] for c in range(NCORE)]
        st["xn2"] = [r[c]["xn2"] for c in range(NCORE)]
        st["y2a"] = [r[c]["y2a"] for c in range(NCORE)]
        st["y2f"] = [r[c]["y2f"] for c in range(NCORE)]
    elif upto == 4:
        st["out"] = [r[c]["out"] for c in range(NCORE)]


def _seg_l0(Z, st):
    g = Z
    pk = pack_lhsT
    nc = _prog("ln0", build_stage_ln, False, None)
    xs = _shard_tm(np.asarray(g["L0_attn_hidden_states"], np.float32))
    r = _run(nc, [{"xin": xs[c], "identbf": IDENT_BF, "cconst": CCONST} for c in range(NCORE)])
    xn0 = [r[c]["xn"] for c in range(NCORE)]
    xnfm0 = [r[c]["xnfm"] for c in range(NCORE)]
    x0 = xs

    nc = _prog("l0qk", build_stage_l0qk)
    wq = pk(g["L0_attn_Wq"].T); wk = pk(g["L0_attn_Wk"].T)
    ims = []
    for c in range(NCORE):
        cosf, sinf = _l0_ropetabs(c % 2)
        ims.append({"xnfm": xnfm0[c], "wq": wq, "wk": wk, "cosf": cosf, "sinf": sinf})
    r = _run(nc, ims)
    qr = [r[c]["qr"] for c in range(NCORE)]
    kr = [r[c]["kr"] for c in range(NCORE)]

    fc1p = np.ascontiguousarray(np.asarray(g["L0_attn_fc1_w"], np.float32).T)
    fc1b = np.ascontiguousarray(np.asarray(g["L0_attn_fc1_b"], np.float32)[:, None])
    fc2p = np.concatenate([np.asarray(g["L0_attn_fc2_w"], np.float32).T,
                           np.asarray(g["L0_attn_fc2_b"], np.float32)[None, :]], 0)
    cm = _cmask()
    ctabs = [_causal_tabs(gg, cm) for gg in (0, 1)]
    nc = _prog("l0attn", build_stage_l0attn)
    ims = []
    for c in range(NCORE):
        bb = c // 2
        krf = np.concatenate([kr[2 * bb], kr[2 * bb + 1]], axis=1)
        ims.append({"qr": qr[c], "krf": krf, "fc1p": fc1p, "fc1b": fc1b,
                    "fc2p": fc2p, "bcn": ctabs[c % 2][0], "smsk2": ctabs[c % 2][1]})
    r = _run(nc, ims)
    y0a = [r[c]["y0a"] for c in range(NCORE)]

    wqout = {}
    for name, w, per in (("wg", g["L0_ffn_w_gate"], [3, 3, 3, 3, 3, 3, 2, 2]),
                         ("wu", g["L0_ffn_w_up"], [3, 3, 3, 3, 3, 3, 2, 2]),
                         ("wd", g["L0_ffn_w_down"], [1] * 8)):
        w = np.asarray(w, np.float32)
        W = w.shape[1]
        nc = _prog(("l0wq", max(per), W), build_stage_l0wq, max(per), W)
        offs = np.cumsum([0] + per)
        ims = []
        for c in range(NCORE):
            sl = w[offs[c] * 128:(offs[c] + per[c]) * 128]
            if per[c] < max(per):
                sl = np.concatenate([sl, np.zeros((128 * (max(per) - per[c]), W), np.float32)], 0)
            ims.append({"w": np.ascontiguousarray(sl)})
        r = _run(nc, ims)
        wqout[name] = np.concatenate([r[c]["wq"][:per[c] * 128].astype(np.float32) for c in range(NCORE)], 0)
    nc = _prog("l0ffn", build_stage_l0ffn)
    wgP = np.ascontiguousarray(wqout["wg"].T).astype(BF)
    wuP = np.ascontiguousarray(wqout["wu"].T).astype(BF)
    wdP = np.ascontiguousarray(wqout["wd"].T).astype(BF)
    r = _run(nc, [{"xn": np.asarray(xn0[c]), "wgq": wgP, "wuq": wuP, "wdq": wdP,
                   "identbf": IDENT_BF} for c in range(NCORE)])
    y0f = [r[c]["y0f"] for c in range(NCORE)]
    st["x0"] = x0
    st["y0a"] = y0a
    st["y0f"] = y0f


def _seg_l1(Z, st):
    g = Z
    pk = pack_lhsT
    x0, y0a, y0f = st["x0"], st["y0a"], st["y0f"]
    cm = _cmask()
    ctabs = [_causal_tabs(gg, cm) for gg in (0, 1)]
    nc = _prog("ln1", build_stage_ln, True, None)
    r = _run(nc, [{"xin": x0[c], "ya": np.asarray(y0a[c]), "yf": np.asarray(y0f[c]),
                   "identbf": IDENT_BF, "cconst": CCONST} for c in range(NCORE)])
    x1 = [r[c]["xnew"] for c in range(NCORE)]
    xnfm1 = [r[c]["xnfm"] for c in range(NCORE)]

    pm = _l1_perm()
    nc = _prog("l1qkv", build_stage_l1qkv)
    wq = pk(np.asarray(g["L1_attn_Wq"], np.float32)[:, pm])
    wk = pk(np.asarray(g["L1_attn_Wk"], np.float32)[:, pm])
    wv = pk(g["L1_attn_Wv"])
    bq = _ppcols(np.asarray(g["L1_attn_bq"], np.float32)[pm])
    bk = _ppcols(np.asarray(g["L1_attn_bk"], np.float32)[pm])
    bv = np.ascontiguousarray(np.asarray(g["L1_attn_bv"], np.float32)[None, :])
    rpm = _l1_rotperm()
    ims = []
    for c in range(NCORE):
        cosr, sinr = _l1_ropetabs(c % 2)
        ims.append({"xnfm": xnfm1[c], "wq": wq, "wk": wk, "wv": wv, "bq": bq, "bk": bk,
                    "bv": bv, "cosr": cosr, "sinr": sinr, "rotperm": rpm})
    r = _run(nc, ims)
    qr = [r[c]["qr"] for c in range(NCORE)]
    kr = [r[c]["kr"] for c in range(NCORE)]
    vv = [r[c]["v"] for c in range(NCORE)]

    wo = pk(g["L1_attn_Wo"])
    bo = np.ascontiguousarray(np.asarray(g["L1_attn_bo"], np.float32)[None, :])
    ims = []
    for c in range(NCORE):
        bb = c // 2
        krf = np.concatenate([kr[2 * bb], kr[2 * bb + 1]], 1)
        ims.append({"qr": qr[c], "krf": krf, "vef": _vef(vv, bb, 16),
                    "wo": wo, "bo": bo, "bcn": ctabs[c % 2][0], "smsk2": ctabs[c % 2][1]})
    nc = _prog("l1attn", build_stage_mha, causal=True, scale=0.125, name="l1attn")
    r = _run(nc, ims)
    y1a = [r[c]["ya"] for c in range(NCORE)]

    nc = _prog("l1ffn", build_stage_l1ffn)
    ins1 = dict(
        sgw=pk(np.asarray(g["L1_ffn_sg_w"], np.float32).T), suw=pk(np.asarray(g["L1_ffn_su_w"], np.float32).T),
        sdw=pk(np.asarray(g["L1_ffn_sd_w"], np.float32).T),
        rw=pk(np.asarray(g["L1_ffn_router_w"], np.float32).T),
        rb=np.ascontiguousarray(np.asarray(g["L1_ffn_routing_bias"], np.float32)[None, :]),
        wg=np.ascontiguousarray(np.asarray(g["L1_ffn_wg"], np.float32).transpose(0, 2, 1)),
        wu=np.ascontiguousarray(np.asarray(g["L1_ffn_wu"], np.float32).transpose(0, 2, 1)),
        wd=np.ascontiguousarray(np.asarray(g["L1_ffn_wd"], np.float32).transpose(0, 2, 1)))
    r = _run(nc, [{**ins1, "xnfm": xnfm1[c]} for c in range(NCORE)])
    y1f = [r[c]["y1f"] for c in range(NCORE)]
    st["x1"] = x1
    st["y1a"] = y1a
    st["y1f"] = y1f


def _seg_l2(Z, st):
    g = Z
    pk = pack_lhsT
    x1, y1a, y1f = st["x1"], st["y1a"], st["y1f"]
    nc = _prog("ln2", build_stage_ln, True, "c2")
    r = _run(nc, [{"xin": np.asarray(x1[c]), "ya": np.asarray(y1a[c]), "yf": np.asarray(y1f[c]),
                   "identbf": IDENT_BF, "cconst": CCONST} for c in range(NCORE)])
    x2 = [r[c]["xnew"] for c in range(NCORE)]
    xn2 = [r[c]["xn"] for c in range(NCORE)]
    zncfm = [r[c]["xnfm"] for c in range(NCORE)]
    c2c = [r[c]["c2"] for c in range(NCORE)]

    W = np.asarray(g["L2_attn_W_qkv"], np.float32)
    bqkv = np.asarray(g["L2_attn_b_qkv"], np.float32)
    nc = _prog("l2qkv", build_stage_l1qkv, True, False, "l2qkv")
    ims = [{"xnfm": zncfm[c], "wq": pk(W[:, 0:1024]), "wk": pk(W[:, 1024:2048]),
            "wv": pk(W[:, 2048:3072]), "bq": _ppcols(bqkv[0:1024]), "bk": _ppcols(bqkv[1024:2048]),
            "bv": np.ascontiguousarray(bqkv[2048:3072][None, :]),
            "lnw": _ppcols(g["L2_attn_ln_w"]), "lnb": _ppcols(g["L2_attn_ln_b"])}
           for c in range(NCORE)]
    r = _run(nc, ims)
    qr = [r[c]["qr"] for c in range(NCORE)]
    kr = [r[c]["kr"] for c in range(NCORE)]
    vv = [r[c]["v"] for c in range(NCORE)]
    wo = pk(g["L2_attn_W_proj"])
    bo = np.ascontiguousarray(np.asarray(g["L2_attn_b_proj"], np.float32)[None, :])
    ims = []
    for c in range(NCORE):
        bb = c // 2
        krf = np.concatenate([kr[2 * bb], kr[2 * bb + 1]], 1)
        ims.append({"qr": qr[c], "krf": krf, "vef": _vef(vv, bb, 16),
                    "wo": wo, "bo": bo})
    nc = _prog("l2attn", build_stage_mha, causal=False, scale=0.125, name="l2attn")
    r = _run(nc, ims)
    y2a = [r[c]["ya"] for c in range(NCORE)]

    Wl = np.asarray(g["L2_ffn_Wqkv"], np.float32)
    nc = _prog("l2locqkv", build_stage_l2locqkv)
    ims = [{"xnfm": zncfm[c], "wq": pk(Wl[0:512].T), "wk": pk(Wl[512:1024].T),
            "wv": pk(Wl[1024:1536].T), "lnw": _ppcols(g["L2_ffn_ln_w"]),
            "lnb": _ppcols(g["L2_ffn_ln_b"])} for c in range(NCORE)]
    r = _run(nc, ims)
    ql = [r[c]["ql"] for c in range(NCORE)]
    kl = [r[c]["kl"] for c in range(NCORE)]
    vl = [r[c]["vl"] for c in range(NCORE)]
    wol = pk(np.asarray(g["L2_ffn_Wout_l"], np.float32).T)
    lm = _lmask()
    ims = []
    for c in range(NCORE):
        bb = c // 2
        klf = np.concatenate([kl[2 * bb], kl[2 * bb + 1]], 1)
        qlf = np.concatenate([ql[2 * bb], ql[2 * bb + 1]], 1)
        ims.append({"ql": qlf, "klf": klf, "vlef": _vef(vl, bb, 8), "wol": wol, "lmask": lm})
    nc = _prog("l2locattn", build_stage_l2locattn)
    r = _run(nc, ims)
    light = [np.ascontiguousarray(np.asarray(r[c]["light"])[(c % 2) * T:(c % 2 + 1) * T])
             for c in range(NCORE)]

    Wkv = np.asarray(g["L2_ffn_Wkv"], np.float32)
    wkv_k = np.concatenate([Wkv[h * 128:h * 128 + 64] for h in range(8)], 0)
    wkv_v = np.concatenate([Wkv[h * 128 + 64:h * 128 + 128] for h in range(8)], 0)
    rt = pk(np.concatenate([np.asarray(g["L2_ffn_rt_q"], np.float32),
                            np.asarray(g["L2_ffn_rt_kv"], np.float32)], 0).T)
    nc = _prog("l2hkv", build_stage_l2hkv)
    ims = [{"xnfm": zncfm[c], "gam": _ppcols(g["L2_ffn_gamma"]), "c2": np.asarray(c2c[c]),
            "rt": rt, "wq": pk(np.asarray(g["L2_ffn_Wq"], np.float32).T),
            "wk": pk(wkv_k.T), "wv": pk(wkv_v.T)} for c in range(NCORE)]
    r = _run(nc, ims)
    qh = [r[c]["qh"] for c in range(NCORE)]
    kh = [r[c]["kh"] for c in range(NCORE)]
    vh = [r[c]["vh"] for c in range(NCORE)]
    ssd = [r[c]["s"] for c in range(NCORE)]

    nullk = np.asarray(g["L2_ffn_null_kv"], np.float32)[0]
    nullv = np.asarray(g["L2_ffn_null_kv"], np.float32)[1]
    woh = pk(np.asarray(g["L2_ffn_Wout_h"], np.float32).T)
    nqb = np.tile(np.asarray(g["L2_ffn_null_q"], np.float32)[None, :], (128, 1))
    kcon = np.array([[512.0, 1024.0]], np.float32)
    nc = _prog("l2heavy", build_stage_l2heavy)
    ims = []
    for c in range(NCORE):
        bb = c // 2
        khf = np.zeros((512, S + 128), np.float32)
        khf[:, 0:S] = np.concatenate([np.asarray(kh[2 * bb], np.float32),
                                      np.asarray(kh[2 * bb + 1], np.float32)], 1)
        khf[:, S] = nullk.reshape(512)
        vhef = np.zeros((S + 128, 8 * 65), np.float32)
        vfull = np.concatenate([np.asarray(vh[2 * bb], np.float32),
                                np.asarray(vh[2 * bb + 1], np.float32)], 0)
        for hh in range(8):
            vhef[0:S, hh * 65:hh * 65 + 64] = vfull[:, hh * 64:(hh + 1) * 64]
            vhef[S, hh * 65:hh * 65 + 64] = nullv[hh]
            vhef[0:S + 1, hh * 65 + 64] = 1.0
        sqf = np.concatenate([ssd[2 * bb][:, 0::2].T.reshape(-1), ssd[2 * bb + 1][:, 0::2].T.reshape(-1)])
        skf = np.concatenate([ssd[2 * bb][:, 1::2].T.reshape(-1), ssd[2 * bb + 1][:, 1::2].T.reshape(-1)])
        sbis = np.zeros((128, 64), np.float32)
        sbis[:, 0:32] = sqf.reshape(32, 128).T
        sbis[:, 32:64] = skf.reshape(32, 128).T
        ims.append({"qh": qh[c], "khf": khf.astype(BF), "vhef": vhef.astype(BF),
                    "sbis": sbis, "sqrow": sqf[None, :], "skvrow": skf[None, :],
                    "sqtm": np.ascontiguousarray(ssd[c][:, 0::2]), "kcon": kcon,
                    "woh": woh, "nqb": nqb, "light": np.asarray(light[c])})
    r = _run(nc, ims)
    y2f = [r[c]["y2f"] for c in range(NCORE)]
    st["x2"] = x2
    st["y2a"] = y2a
    st["y2f"] = y2f
    st["xn2"] = xn2


def _seg_l3(Z, st):
    g = Z
    pk = pack_lhsT
    x2, y2a, y2f, xn2 = st["x2"], st["y2a"], st["y2f"], st["xn2"]
    nc = _prog("ln3", build_stage_ln, True, None, True)
    r = _run(nc, [{"xin": np.asarray(x2[c]), "ya": np.asarray(y2a[c]), "yf": np.asarray(y2f[c]),
                   "y3": np.asarray(xn2[c]), "identbf": IDENT_BF, "cconst": CCONST}
                  for c in range(NCORE)])
    x3 = [r[c]["xnew"] for c in range(NCORE)]
    xnfm3 = [r[c]["xnfm"] for c in range(NCORE)]

    nc = _prog("l3qkv", build_stage_l1qkv, False, False, "l3qkv")
    ims = [{"xnfm": xnfm3[c], "wq": pk(np.asarray(g["L3_attn_Wq"], np.float32).T),
            "wk": pk(np.asarray(g["L3_attn_Wk"], np.float32).T),
            "wv": pk(np.asarray(g["L3_attn_Wv"], np.float32).T),
            "bq": _ppcols(g["L3_attn_bq"]), "bk": _ppcols(g["L3_attn_bk"]),
            "bv": np.ascontiguousarray(np.asarray(g["L3_attn_bv"], np.float32)[None, :])}
           for c in range(NCORE)]
    r = _run(nc, ims)
    qr = [r[c]["qr"] for c in range(NCORE)]
    kr = [r[c]["kr"] for c in range(NCORE)]
    vv = [r[c]["v"] for c in range(NCORE)]
    wo = pk(np.asarray(g["L3_attn_Wo"], np.float32).T)
    bo = np.ascontiguousarray(np.asarray(g["L3_attn_bo"], np.float32)[None, :])
    embt = np.ascontiguousarray(np.asarray(g["L3_attn_dist_emb"], np.float32).T)
    nc = _prog("l3attn", build_stage_l3attn)
    ims = []
    for c in range(NCORE):
        bb = c // 2
        gg = c % 2
        krf = np.concatenate([kr[2 * bb], kr[2 * bb + 1]], 1)
        qrf = np.concatenate([qr[2 * bb], qr[2 * bb + 1]], 1)
        ims.append({"qh8": np.ascontiguousarray(qrf[gg * 512:(gg + 1) * 512]),
                    "krf8": np.ascontiguousarray(krf[gg * 512:(gg + 1) * 512]),
                    "vef8": _vef(vv, bb, 8, h0=gg * 8),
                    "embt": embt, "identf": IDENT_F32})
    r = _run(nc, ims)
    aofm = [np.asarray(r[c]["aofm"]) for c in range(NCORE)]
    nc = _prog("l3proj", build_stage_l3proj)
    ims = []
    for c in range(NCORE):
        bb = c // 2
        gc = c % 2
        af = np.zeros((D, T), BF)
        for gh in (0, 1):
            for hh in range(8):
                H = gh * 8 + hh
                af[H * 64:(H + 1) * 64] = aofm[2 * bb + gh][
                    128 * (hh // 2) + 64 * (hh % 2):128 * (hh // 2) + 64 * (hh % 2) + 64,
                    gc * T:(gc + 1) * T]
        ims.append({"af": af, "wo": wo, "bo": bo})
    r = _run(nc, ims)
    y3a = [r[c]["ya"] for c in range(NCORE)]

    nc = _prog("l3ffn", build_stage_l3ffn, 1024)
    wg3 = np.ascontiguousarray(np.asarray(g["L3_ffn_Wg"], np.float32))
    wu3 = np.ascontiguousarray(np.asarray(g["L3_ffn_Wu"], np.float32))
    wd3 = np.ascontiguousarray(np.asarray(g["L3_ffn_Wd"], np.float32))
    y3f = [np.zeros((T, D), np.float32) for _ in range(NCORE)]
    for part in (0, 1):
        ims = []
        for c in range(NCORE):
            xf = np.asarray(xnfm3[c])[:, part * 1024:(part + 1) * 1024]
            ims.append({"xnfm": np.ascontiguousarray(xf), "wg": wg3, "wu": wu3, "wd": wd3})
        r = _run(nc, ims)
        for c in range(NCORE):
            y3f[c][part * 1024:(part + 1) * 1024] = r[c]["y3f"]

    nc = _prog("final", build_stage_final)
    r = _run(nc, [{"xin": np.asarray(x3[c]), "ya": np.asarray(y3a[c]), "yf": np.asarray(y3f[c])}
                  for c in range(NCORE)])
    st["out"] = [r[c]["out"] for c in range(NCORE)]


def kernel(**Z):
    st = {}
    if FUSE_UPTO >= 1:
        _fused_run(Z, st, FUSE_UPTO)
    else:
        _seg_l0(Z, st)
    if FUSE_UPTO < 2:
        _seg_l1(Z, st)
    if FUSE_UPTO < 3:
        _seg_l2(Z, st)
    if FUSE_UPTO < 4:
        _seg_l3(Z, st)
    return _unshard_tm(st["out"])



# version 17 (final)
# speedup vs baseline: 1.2023x; 1.2023x over previous
import numpy as np
import ml_dtypes
from contextlib import ExitStack
import concourse.bass as bass
import concourse.mybir as mybir
from concourse.bass_utils import run_bass_kernel_spmd

F32 = mybir.dt.float32
BF16 = mybir.dt.bfloat16
AF = mybir.ActivationFunctionType
ALU = mybir.AluOpType
AX = mybir.AxisListType.X
BF = ml_dtypes.bfloat16

NCORE = 8
B = 4
S = 4096
T = 2048
D = 1024
NT = T // 128
NC = D // 128
EPS = 1e-5


class Sem:
    __slots__ = ("h", "n", "name")
    def __init__(self, h, name):
        self.h = h; self.n = 0; self.name = name
    def tick(self, k=1):
        self.n += k
        return self.n

ENGS = ("sync", "gpsimd", "vector", "scalar", "tensor")

class Prog:
    def __init__(self, name="prog"):
        self.nc = bass.Bass("TRN2", target_bir_lowering=False, debug=False)
        self.stack = ExitStack()
        self.steps = {e: [] for e in ENGS}
        self.name = name
        self._semi = 0
        self._sbi = 0
        self._allsems = []
        self._sems_by_name = {}
        self._phase_stack = None
        self._phase_name = ""
        self._bar = None

    def sbuf(self, shape, dtype, name=None):
        self._sbi += 1
        nm = f"{self._phase_name}{name or ''}_{self._sbi}"
        return (self._phase_stack or self.stack).enter_context(
            self.nc.sbuf_tensor(nm, shape, dtype))

    def psum(self, shape, dtype=F32, name=None):
        self._sbi += 1
        nm = f"{self._phase_name}{name or ''}_{self._sbi}"
        return (self._phase_stack or self.stack).enter_context(
            self.nc.psum_tensor(nm, shape, dtype))

    def sem(self, name=None):
        if name is not None and name in self._sems_by_name:
            return self._sems_by_name[name]
        self._semi += 1
        nm = name or f"sem{self._semi}"
        h = self.nc.alloc_semaphore(name=nm)
        self._allsems.append(h)
        s = Sem(h, nm)
        if name is not None:
            self._sems_by_name[name] = s
        return s

    def barrier(self):
        if self._bar is None:
            self._bar = self.sem("bar")
        tgt = self._bar.n + 5
        self._bar.n = tgt
        for eng in ENGS:
            self.on(eng, lambda e, h=self._bar.h: e.sem_inc(h, 1))
        for eng in ENGS:
            wait(self, eng, self._bar, tgt)

    def phase(self, name):
        if self._phase_stack is not None:
            self._phase_stack.close()
            self.barrier()
        self._phase_stack = ExitStack()
        self._phase_name = name + "_"

    def psem(self, name):
        return self.sem(self._phase_name + name)

    def on(self, eng, fn):
        self.steps[eng].append(fn)

    def finish(self):
        nc = self.nc
        if self._phase_stack is not None:
            self._phase_stack.close()
            self._phase_stack = None
        with nc.cleanup_on_exit():
            with nc.Block() as block:
                @block.sync
                def _(e):
                    for f in self.steps["sync"]: f(e)
                @block.gpsimd
                def _(e):
                    for f in self.steps["gpsimd"]: f(e)
                @block.vector
                def _(e):
                    for f in self.steps["vector"]: f(e)
                @block.scalar
                def _(e):
                    for f in self.steps["scalar"]: f(e)
                @block.tensor
                def _(e):
                    for f in self.steps["tensor"]: f(e)
            nc.all_engine_barrier()
            sems = sorted(s.num if hasattr(s, "num") else s for s in self._allsems)
            if sems:
                nc.gpsimd.dma_reset(range(min(sems), max(sems) + 1))
            nc.all_engine_barrier()
        self.stack.close()
        return nc


def dma(p, eng, sem, dst, src, **kw):
    t = sem.tick(16)
    p.on(eng, lambda e, d=dst, s=src, sm=sem.h, kw=kw: e.dma_start(d, s, **kw).then_inc(sm, 16))
    return t

def wait(p, eng, sem, val):
    p.on(eng, lambda e, sm=sem.h, v=val: e.wait_ge(sm, v))

def run(p, eng, fn, sem=None):
    if sem is None:
        p.on(eng, fn)
        return None
    t = sem.tick(1)
    def go(e, fn=fn, sm=sem.h):
        fn(e).then_inc(sm, 1)
    p.on(eng, go)
    return t


def _io(nc, t, name, shape, dtype, kind):
    if t is not None and name in t:
        return t[name]
    return nc.dram_tensor(name, shape, dtype, kind=kind).ap()

def to_bf(x):
    return np.ascontiguousarray(x).astype(BF)

def tm_ap(dram_ap):
    return dram_ap.rearrange("(t p) d -> p t d", p=128)

def fm_ap(dram_ap):
    return dram_ap.rearrange("(c p) n -> p c n", p=128)

IDENT_BF = np.eye(128, dtype=np.float32).astype(BF)
CCONST = np.tile(np.array([[EPS, 0, 0, 0, 0, 0, 0, 0]], np.float32), (128, 1))
IDENT_F32 = np.eye(128, dtype=np.float32)


def build_stage_ln(has_res, scale_mode, has_res3=False, p=None, t=None):
    p = p or Prog("ln")
    nc = p.nc
    xin = _io(nc, t, "xin", [T, D], F32, "ExternalInput")
    if has_res:
        ya = _io(nc, t, "ya", [T, D], F32, "ExternalInput")
        yf = _io(nc, t, "yf", [T, D], F32, "ExternalInput")
    if has_res3:
        y3 = _io(nc, t, "y3", [T, D], F32, "ExternalInput")
    identbf = _io(nc, t, "identbf", [128, 128], BF16, "ExternalInput")
    cconst = _io(nc, t, "cconst", [128, 8], F32, "ExternalInput")
    xnew_o = _io(nc, t, "xnew", [T, D], F32, "ExternalOutput")
    xn_o = _io(nc, t, "xn", [T, D], F32, "ExternalOutput")
    xnfm_o = _io(nc, t, "xnfm", [D, T], BF16, "ExternalOutput")
    if scale_mode == "c2":
        c2_o = _io(nc, t, "c2", [128, NT], F32, "ExternalOutput")

    X = p.sbuf([128, NT * D], F32, "X")
    XN = p.sbuf([128, NT * D], F32, "XN")
    SQ = p.sbuf([128, D], F32, "SQ")
    XBF = p.sbuf([128, 2 * D], BF16, "XBF")
    XNFM = p.sbuf([128, NC * T], BF16, "XNFM")
    IDB = p.sbuf([128, 128], BF16, "IDB")
    SUM = p.sbuf([128, NT], F32, "SUM")
    SQS = p.sbuf([128, NT], F32, "SQS")
    MU = p.sbuf([128, NT], F32, "MU")
    MU2 = p.sbuf([128, NT], F32, "MU2")
    VAR = p.sbuf([128, NT], F32, "VAR")
    STD = p.sbuf([128, NT], F32, "STD")
    RSTD = p.sbuf([128, NT], F32, "RSTD")
    YA = p.sbuf([128, 2 * D], F32, "YA")
    YF = p.sbuf([128, 2 * D], F32, "YF")
    if has_res3:
        Y3 = p.sbuf([128, 2 * D], F32, "Y3")
    if scale_mode == "c2":
        VPE = p.sbuf([128, NT], F32, "VPE")
        RVE = p.sbuf([128, NT], F32, "RVE")
        RAT = p.sbuf([128, NT], F32, "RAT")
        SR = p.sbuf([128, NT], F32, "SR")
        C2 = p.sbuf([128, NT], F32, "C2")
    CC = p.sbuf([128, 8], F32, "CC")
    PS = [p.psum([128, 128], BF16, f"PT{i}") for i in range(4)]

    lda = p.sem("lda"); va = p.sem("va"); ac = p.sem("ac"); pe = p.sem("pe")
    ev = p.sem("ev"); st = p.psem("st")
    ldy = [p.sem("ldy0"), p.sem("ldy1")]

    dma(p, "sync", lda, X[:, :], tm_ap(xin))
    dma(p, "sync", lda, IDB[:, :], identbf)
    dma(p, "sync", lda, CC[:, :], cconst)
    LDA = lda.n

    addv = []
    if has_res:
        for t in range(NT):
            bsl = slice((t % 2) * D, (t % 2) * D + D)
            if t >= 2:
                wait(p, "sync", va, addv[t - 2])
            dma(p, "sync", ldy[t % 2], YA[:, bsl], ya[t * 128:(t + 1) * 128, :])
            if has_res3:
                dma(p, "sync", ldy[t % 2], Y3[:, bsl], y3[t * 128:(t + 1) * 128, :])
            ty = dma(p, "sync", ldy[t % 2], YF[:, bsl], yf[t * 128:(t + 1) * 128, :])
            wait(p, "vector", lda, LDA)
            wait(p, "vector", ldy[t % 2], ty)
            run(p, "vector", lambda e, t=t, bsl=bsl: e.tensor_add(
                X[:, t * D:(t + 1) * D], X[:, t * D:(t + 1) * D], YA[:, bsl]))
            if has_res3:
                run(p, "vector", lambda e, t=t, bsl=bsl: e.tensor_add(
                    X[:, t * D:(t + 1) * D], X[:, t * D:(t + 1) * D], Y3[:, bsl]))
            addv.append(run(p, "vector", lambda e, t=t, bsl=bsl: e.tensor_add(
                X[:, t * D:(t + 1) * D], X[:, t * D:(t + 1) * D], YF[:, bsl]), va))
        wait(p, "sync", va, addv[-1])
    else:
        wait(p, "vector", lda, LDA)
        addv = [0] * NT
        wait(p, "sync", lda, LDA)
    dma(p, "sync", st, tm_ap(xnew_o), X[:, :])

    for t in range(NT):
        if has_res:
            wait(p, "scalar", va, addv[t])
        else:
            wait(p, "scalar", lda, LDA)
        run(p, "scalar", lambda e, t=t: e.activation(
            SQ[:, :], X[:, t * D:(t + 1) * D], AF.Square,
            accum_out=SQS[:, t:t + 1]), ac)
    last_red = None
    for t in range(NT):
        if has_res:
            wait(p, "vector", va, addv[t])
        last_red = run(p, "vector", lambda e, t=t: e.reduce_sum(
            SUM[:, t:t + 1], X[:, t * D:(t + 1) * D], axis=AX), va)
    wait(p, "vector", va, last_red)
    tmu = run(p, "vector", lambda e: e.tensor_scalar_mul(MU[:, :], SUM[:, :], 1.0 / D), va)
    wait(p, "vector", va, tmu)
    tmu2 = run(p, "vector", lambda e: e.tensor_mul(MU2[:, :], MU[:, :], MU[:, :]), va)
    wait(p, "vector", ac, ac.n)
    wait(p, "vector", va, tmu2)
    tvar = run(p, "vector", lambda e: e.scalar_tensor_tensor(
        VAR[:, :], SQS[:, :], 1.0 / D, MU2[:, :], ALU.mult, ALU.subtract), va)
    wait(p, "scalar", va, tvar)
    wait(p, "scalar", lda, LDA)
    tstd = run(p, "scalar", lambda e: e.activation(STD[:, :], VAR[:, :], AF.Sqrt, bias=CC[:, 0:1]), ac)
    wait(p, "vector", ac, tstd)
    trs = run(p, "vector", lambda e: e.reciprocal(RSTD[:, :], STD[:, :]), va)

    if scale_mode == "c2":
        t1 = run(p, "vector", lambda e: e.tensor_scalar_add(VPE[:, :], VAR[:, :], EPS), va)
        wait(p, "vector", va, t1)
        t2 = run(p, "vector", lambda e: e.reciprocal(RVE[:, :], VPE[:, :]), va)
        wait(p, "vector", va, t2)
        t3 = run(p, "vector", lambda e: e.tensor_mul(RAT[:, :], VAR[:, :], RVE[:, :]), va)
        wait(p, "scalar", va, t3)
        t4 = run(p, "scalar", lambda e: e.activation(SR[:, :], RAT[:, :], AF.Sqrt, bias=CC[:, 0:1]), ac)
        wait(p, "vector", ac, t4)
        t5 = run(p, "vector", lambda e: e.reciprocal(C2[:, :], SR[:, :]), va)
        wait(p, "sync", va, t5)
        dma(p, "sync", st, c2_o, C2[:, :])

    wait(p, "vector", va, trs)
    xnv = []
    for t in range(NT):
        xnv.append(run(p, "vector", lambda e, t=t: e.tensor_scalar(
            XN[:, t * D:(t + 1) * D], X[:, t * D:(t + 1) * D],
            MU[:, t:t + 1], RSTD[:, t:t + 1], ALU.subtract, ALU.mult), va))
    wait(p, "sync", va, xnv[-1])
    dma(p, "sync", st, tm_ap(xn_o), XN[:, :])

    sem_cast = va if scale_mode == "c2" else ac
    castv = []
    evn = []
    for t in range(NT):
        bsl = slice((t % 2) * D, (t % 2) * D + D)
        eng = "vector" if scale_mode == "c2" else "scalar"
        if scale_mode == "c2":
            wait(p, eng, va, max(xnv[t], t5))
        else:
            wait(p, eng, va, xnv[t])
        if t >= 2:
            wait(p, eng, pe, (t - 1) * NC)
        if scale_mode == "c2":
            castv.append(run(p, "vector", lambda e, t=t, bsl=bsl: e.tensor_scalar(
                XBF[:, bsl], XN[:, t * D:(t + 1) * D], C2[:, t:t + 1], None, ALU.mult), va))
        else:
            castv.append(run(p, "scalar", lambda e, t=t, bsl=bsl: e.copy(
                XBF[:, bsl], XN[:, t * D:(t + 1) * D]), ac))
        for c in range(NC):
            k = t * NC + c
            bank = k % 4
            wait(p, "tensor", sem_cast, castv[t])
            if k == 0:
                wait(p, "tensor", lda, LDA)
            if k >= 4:
                wait(p, "tensor", ev, evn[k - 4])
            tp = run(p, "tensor", lambda e, t=t, c=c, bank=bank: e.transpose(
                PS[bank][:, :], XBF[:, (t % 2) * D + c * 128:(t % 2) * D + (c + 1) * 128],
                IDB[:, :]), pe)
            wait(p, "vector", pe, tp)
            evn.append(run(p, "vector", lambda e, t=t, c=c, bank=bank: e.tensor_copy(
                XNFM[:, c * T + t * 128:c * T + (t + 1) * 128], PS[bank][:, :]), ev))
    wait(p, "sync", ev, evn[-1])
    dma(p, "sync", st, fm_ap(xnfm_o), XNFM[:, :])
    wait(p, "sync", st, st.n)
    return p



def pack_lhsT(w_din_dout):
    return np.ascontiguousarray(w_din_dout.astype(np.float32))

def load_w_fm(p, _unused, dram_ap, din, dout, name):
    W = p.sbuf([128, (din // 128) * dout], BF16, name)
    s = p.sem("w_" + name)
    dma(p, "gpsimd", s, W[:, :], dram_ap.rearrange("(c p) n -> p c n", p=128))
    return W, s

def emit_proj_fm(p, sems, XF, W, OUT, dout, *, bias_col=None, nci=NC, ntok=T,
                 pre=(), evac_eng="scalar", extra_wait=None,
                 evac_wait=None):
    va, ac, pe, ev = sems["va"], sems["ac"], sems["pe"], sems["ev"]
    PSB = sems["psumpool"]
    nco = dout // 128
    ntc = ntok // 512
    k = 0
    outv = []
    for co in range(nco):
        for tc in range(ntc):
            bank = k % len(PSB)
            for (sm, vv) in pre:
                wait(p, "tensor", sm, vv)
            if extra_wait is not None:
                extra_wait(k)
            if k >= len(PSB):
                wait(p, "tensor", ev, outv[k - len(PSB)])
            for ci in range(nci):
                st_ = (ci == 0); sp_ = (ci == nci - 1)
                run(p, "tensor", lambda e, co=co, tc=tc, ci=ci, bank=bank, st_=st_, sp_=sp_:
                    e.matmul(PSB[bank][:, :],
                             W[:, ci * dout + co * 128:ci * dout + (co + 1) * 128],
                             XF[:, ci * ntok + tc * 512:ci * ntok + tc * 512 + 512],
                             start=st_, stop=sp_),
                    pe if sp_ else None)
            wait(p, evac_eng, pe, pe.n)
            if evac_wait is not None:
                evac_wait(k)
            if bias_col is not None:
                outv.append(run(p, evac_eng, lambda e, co=co, tc=tc, bank=bank: e.tensor_scalar(
                    OUT[:, co * ntok + tc * 512:co * ntok + tc * 512 + 512], PSB[bank][:, :],
                    bias_col[:, co:co + 1], None, ALU.add), ev))
            else:
                fn = (lambda e, co=co, tc=tc, bank=bank: e.copy(
                    OUT[:, co * ntok + tc * 512:co * ntok + tc * 512 + 512], PSB[bank][:, :])) \
                    if evac_eng == "scalar" else (lambda e, co=co, tc=tc, bank=bank: e.tensor_copy(
                    OUT[:, co * ntok + tc * 512:co * ntok + tc * 512 + 512], PSB[bank][:, :]))
                outv.append(run(p, evac_eng, fn, ev))
            k += 1
    return outv

def emit_rope_fm(p, sems, QF, QR, COSF, SINF, *, ntok=T, deps=None):
    va = sems["va"]
    outs = []
    S0 = NC * ntok
    S1 = (NC + 1) * ntok
    for c0 in range(4):
        c1 = c0 + 4
        if deps is not None:
            deps(c0); deps(c1)
        run(p, "vector", lambda e, c1=c1, c0=c0: e.tensor_mul(
            QR[:, S0:S0 + ntok], QF[:, c1 * ntok:(c1 + 1) * ntok],
            SINF[:, c0 * ntok:(c0 + 1) * ntok]))
        run(p, "vector", lambda e, c0=c0: e.tensor_mul(
            QR[:, S1:S1 + ntok], QF[:, c0 * ntok:(c0 + 1) * ntok],
            SINF[:, c0 * ntok:(c0 + 1) * ntok]))
        run(p, "vector", lambda e, c0=c0: e.tensor_mul(
            QR[:, c0 * ntok:(c0 + 1) * ntok], QF[:, c0 * ntok:(c0 + 1) * ntok],
            COSF[:, c0 * ntok:(c0 + 1) * ntok]))
        run(p, "vector", lambda e, c1=c1, c0=c0: e.tensor_mul(
            QR[:, c1 * ntok:(c1 + 1) * ntok], QF[:, c1 * ntok:(c1 + 1) * ntok],
            COSF[:, c0 * ntok:(c0 + 1) * ntok]))
        outs.append(run(p, "vector", lambda e, c0=c0: e.tensor_sub(
            QR[:, c0 * ntok:(c0 + 1) * ntok], QR[:, c0 * ntok:(c0 + 1) * ntok],
            QR[:, S0:S0 + ntok]), va))
        outs.append(run(p, "vector", lambda e, c1=c1: e.tensor_add(
            QR[:, c1 * ntok:(c1 + 1) * ntok], QR[:, c1 * ntok:(c1 + 1) * ntok],
            QR[:, S1:S1 + ntok]), va))
    return outs


def build_stage_l0qk(p=None, t=None):
    p = p or Prog("l0qk")
    nc = p.nc
    xnfm_i = _io(nc, t, "xnfm", [D, T], BF16, "ExternalInput")
    wq_i = _io(nc, t, "wq", [D, D], F32, "ExternalInput")
    wk_i = _io(nc, t, "wk", [D, D], F32, "ExternalInput")
    cos_i = _io(nc, t, "cosf", [512, T], BF16, "ExternalInput")
    sin_i = _io(nc, t, "sinf", [512, T], BF16, "ExternalInput")
    qr_o = _io(nc, t, "qr", [D, T], BF16, "ExternalOutput")
    kr_o = _io(nc, t, "kr", [D, T], BF16, "ExternalOutput")

    XF = p.sbuf([128, NC * T], BF16, "XF")
    QF = p.sbuf([128, NC * T], BF16, "QF")
    QR = p.sbuf([128, (NC + 2) * T], BF16, "QR")
    COSF = p.sbuf([128, 4 * T], BF16, "COSF")
    SINF = p.sbuf([128, 4 * T], BF16, "SINF")
    PSB = [p.psum([128, 512], F32, f"pp{i}") for i in range(4)]
    lda = p.sem("lda"); va = p.sem("va"); ac = p.sem("ac"); pe = p.sem("pe")
    ev = p.sem("ev"); st = p.psem("st")
    sems = dict(va=va, ac=ac, pe=pe, ev=ev, psumpool=PSB)

    dma(p, "sync", lda, XF[:, :], fm_ap(xnfm_i))
    dma(p, "sync", lda, COSF[:, :], cos_i.rearrange("(c p) n -> p c n", p=128))
    dma(p, "sync", lda, SINF[:, :], sin_i.rearrange("(c p) n -> p c n", p=128))
    LDA = lda.n
    WQ, wsq = load_w_fm(p, None, wq_i, D, D, "WQ")
    WK, wsk = load_w_fm(p, None, wk_i, D, D, "WK")

    qo = emit_proj_fm(p, sems, XF, WQ, QF, D, pre=[(lda, LDA), (wsq, wsq.n)])
    wait(p, "vector", lda, LDA)
    def qdeps(c, qo=qo):
        wait(p, "vector", ev, qo[c * (T // 512) + (T // 512) - 1])
    qrv = emit_rope_fm(p, sems, QF, QR, COSF, SINF, deps=qdeps)
    wait(p, "sync", va, qrv[-1])
    dma(p, "sync", st, fm_ap(qr_o), QR[:, 0:NC * T])
    def kwait(k, qrv=qrv):
        wait(p, "scalar", va, qrv[-1])
    ko = emit_proj_fm(p, sems, XF, WK, QF, D, pre=[(lda, LDA), (wsk, wsk.n)], evac_wait=kwait)
    def kdeps(c, ko=ko):
        wait(p, "vector", ev, ko[c * (T // 512) + (T // 512) - 1])
    wait(p, "vector", st, 16)
    krv = emit_rope_fm(p, sems, QF, QR, COSF, SINF, deps=kdeps)
    wait(p, "sync", va, krv[-1])
    dma(p, "sync", st, fm_ap(kr_o), QR[:, 0:NC * T])
    wait(p, "sync", st, st.n)
    return p



def build_stage_l0attn(p=None, t=None):
    p = p or Prog("l0attn")
    nc = p.nc
    qr_i = _io(nc, t, "qr", [D, T], BF16, "ExternalInput")
    kag0 = t.get("krf_ag") if t else None
    if kag0 is None:
        krf_i = _io(nc, t, "krf", [D, S], BF16, "ExternalInput")
    fc1p_i = _io(nc, t, "fc1p", [1, 16], F32, "ExternalInput")
    fc1b_i = _io(nc, t, "fc1b", [16, 1], F32, "ExternalInput")
    fc2p_i = _io(nc, t, "fc2p", [17, D], F32, "ExternalInput")
    bcn_i = _io(nc, t, "bcn", [128, 4 * 32], F32, "ExternalInput")
    smsk2_i = _io(nc, t, "smsk2", [128, 8 * 512], BF16, "ExternalInput")
    y0a_o = _io(nc, t, "y0a", [T, D], F32, "ExternalOutput")

    QR = p.sbuf([128, NC * T], BF16, "QR")
    KRF = p.sbuf([128, NC * S], BF16, "KRF")
    EXPT = p.sbuf([128, 2 * 512], BF16, "EXPT")
    BCN = p.sbuf([128, 4 * 32], F32, "BCN")
    SMSK = p.sbuf([128, 8 * 512], BF16, "SMSK")
    SCR = p.sbuf([128, 512], F32, "SCR")
    ONESC = p.sbuf([128, 1], BF16, "ONESC")
    OS = p.sbuf([1, T], F32, "OS")
    FC1 = p.sbuf([1, 16], F32, "FC1")
    FC1B = p.sbuf([16, 1], F32, "FC1B")
    FC2 = p.sbuf([17, D], BF16, "FC2")
    H1 = p.sbuf([17, T], BF16, "H1")
    Y = p.sbuf([128, NT * D], F32, "Y")
    PSC = [p.psum([128, 512], F32, f"psc{i}") for i in range(2)]
    POS = p.psum([1, 512], F32, "pos")
    PH = p.psum([16, 512], F32, "ph")
    PO = [p.psum([128, 512], F32, f"po{i}") for i in range(2)]

    lda = p.sem("lda"); va = p.sem("va"); ac = p.sem("ac"); pe = p.sem("pe")
    ev = p.sem("ev"); st = p.psem("st"); gld = p.sem("gld")

    dma(p, "sync", lda, QR[:, :], fm_ap(qr_i))
    if kag0 is None:
        dma(p, "sync", lda, KRF[:, :], fm_ap(krf_i))
    else:
        for ci in range(NC):
            for gg in range(2):
                dma(p, "sync", lda, KRF[:, ci * S + gg * T:ci * S + gg * T + T],
                    kag0[gg * D + ci * 128:gg * D + ci * 128 + 128, :])
    dma(p, "sync", lda, BCN[:, :], bcn_i)
    dma(p, "sync", lda, SMSK[:, :], smsk2_i)
    dma(p, "sync", lda, FC1[:, :], fc1p_i)
    dma(p, "sync", lda, FC1B[:, :], fc1b_i)
    LDA = lda.n
    dma(p, "gpsimd", gld, FC2[:, :], fc2p_i)
    GLD = gld.n
    run(p, "vector", lambda e: e.memset(ONESC[:, :], 1.0))
    von = run(p, "vector", lambda e: e.memset(H1[:, :], 1.0), va)

    expn = []
    pemm = []
    osev = []
    for qc in range(4):
        slots = [qc * 4 + j for j in range(4)] + [16 + qc * 4 + j for j in range(4)]
        for kt in range(32):
            bank = kt % 2
            wait(p, "tensor", lda, LDA)
            if len(expn) >= 2:
                wait(p, "tensor", ac, expn[-2])
            for ci in range(NC):
                run(p, "tensor", lambda e, ci=ci, kt=kt, qc=qc, bank=bank:
                    e.matmul(PSC[bank][:, :],
                             KRF[:, ci * S + kt * 128:ci * S + (kt + 1) * 128],
                             QR[:, ci * T + qc * 512:ci * T + qc * 512 + 512],
                             start=(ci == 0), stop=(ci == NC - 1)),
                    pe if ci == NC - 1 else None)
            if len(pemm) >= 2:
                wait(p, "scalar", pe, pemm[-2])
            bcol = qc * 32 + kt
            if kt in slots:
                sj = slots.index(kt)
                wait(p, "vector", pe, pe.n)
                if expn:
                    wait(p, "vector", ac, expn[-1])
                wait(p, "vector", lda, LDA)
                tvm = run(p, "vector", lambda e, bank=bank, sj=sj: e.tensor_add(
                    SCR[:, :], PSC[bank][:, :], SMSK[:, sj * 512:(sj + 1) * 512]), va)
                wait(p, "scalar", va, tvm)
                expn.append(run(p, "scalar", lambda e, kt=kt, bcol=bcol: e.activation(
                    EXPT[:, (kt % 2) * 512:(kt % 2) * 512 + 512], SCR[:, :], AF.Exp,
                    scale=1.0 / 32.0, bias=BCN[:, bcol:bcol + 1]), ac))
            else:
                wait(p, "scalar", pe, pe.n)
                expn.append(run(p, "scalar", lambda e, kt=kt, bank=bank, bcol=bcol: e.activation(
                    EXPT[:, (kt % 2) * 512:(kt % 2) * 512 + 512], PSC[bank][:, :], AF.Exp,
                    scale=1.0 / 32.0, bias=BCN[:, bcol:bcol + 1]), ac))
            wait(p, "tensor", ac, expn[-1])
            if qc > 0 and kt == 0:
                wait(p, "tensor", ev, osev[qc - 1])
            pemm.append(run(p, "tensor", lambda e, kt=kt: e.matmul(
                POS[:, :], ONESC[:, :], EXPT[:, (kt % 2) * 512:(kt % 2) * 512 + 512],
                start=(kt == 0), stop=(kt == 31)), pe))
        wait(p, "vector", pe, pe.n)
        osev.append(run(p, "vector", lambda e, qc=qc: e.tensor_copy(
            OS[:, qc * 512:qc * 512 + 512], POS[:, :]), ev))
    h1n = []
    for qc in range(4):
        wait(p, "tensor", ev, osev[qc])
        wait(p, "tensor", lda, LDA)
        if qc > 0:
            wait(p, "tensor", ac, h1n[qc - 1])
        tph = run(p, "tensor", lambda e, qc=qc: e.matmul(
            PH[:, :], FC1[:, :], OS[:, qc * 512:qc * 512 + 512], start=True, stop=True), pe)
        wait(p, "scalar", pe, tph)
        wait(p, "scalar", lda, LDA)
        h1n.append(run(p, "scalar", lambda e, qc=qc: e.activation(
            H1[0:16, qc * 512:qc * 512 + 512], PH[:, :], AF.Relu, bias=FC1B[:, 0:1]), ac))
    yv = []
    for tt in range(NT):
        for dh in range(2):
            k = tt * 2 + dh
            bank = k % 2
            wait(p, "tensor", ac, h1n[tt // 4])
            wait(p, "tensor", va, von)
            wait(p, "tensor", gld, GLD)
            if k >= 2:
                wait(p, "tensor", ev, yv[k - 2])
            tpo = run(p, "tensor", lambda e, tt=tt, dh=dh, bank=bank: e.matmul(
                PO[bank][:, :], H1[0:17, tt * 128:(tt + 1) * 128],
                FC2[0:17, dh * 512:(dh + 1) * 512], start=True, stop=True), pe)
            wait(p, "vector", pe, tpo)
            yv.append(run(p, "vector", lambda e, tt=tt, dh=dh, bank=bank: e.tensor_copy(
                Y[:, tt * D + dh * 512:tt * D + (dh + 1) * 512], PO[bank][:, :]), ev))
    wait(p, "sync", ev, yv[-1])
    dma(p, "sync", st, tm_ap(y0a_o), Y[:, :])
    wait(p, "sync", st, st.n)
    return p



def emit_quant(p, sems, SRC, DST, W, QS, *, src_off=0, dst_off=0, dep=None, tick=None):
    va, ac = sems["va"], sems["ac"]
    NB = W // 16
    BM, SC, RS = QS["BM"], QS["SC"], QS["RS"]
    V, SG, Aq, Bq, Qq, Q2 = QS["V"], QS["SG"], QS["A"], QS["B"], QS["Q"], QS["Q2"]
    s3 = lambda ap, off, w: ap[:, off:off + w].rearrange("p (nb k) -> p nb k", k=16)
    if dep is not None:
        dep()
    t = run(p, "vector", lambda e: e.reduce_max(
        BM[:, 0:NB], s3(SRC, src_off, W), axis=AX, apply_absolute_value=True), va)
    wait(p, "vector", va, t)
    t = run(p, "vector", lambda e: e.tensor_scalar(
        SC[:, 0:NB], BM[:, 0:NB], 1.0 / 6.0, 1e-12, ALU.mult, ALU.max), va)
    wait(p, "vector", va, t)
    t = run(p, "vector", lambda e: e.reciprocal(RS[:, 0:NB], SC[:, 0:NB]), va)
    wait(p, "vector", va, t)
    rsb = lambda: bass.AP(RS, 0, [[RS.shape[1], 128], [1, NB], [0, 16]])
    t = run(p, "vector", lambda e: e.tensor_tensor(
        s3(V, 0, W), s3(SRC, src_off, W), rsb(), ALU.mult), va)
    wait(p, "scalar", va, t)
    tsg = run(p, "scalar", lambda e: e.activation(SG[:, 0:W], V[:, 0:W], AF.Sign), ac)
    tab = run(p, "scalar", lambda e: e.activation(Q2[:, 0:W], V[:, 0:W], AF.Abs), ac)
    wait(p, "vector", ac, tab)
    run(p, "vector", lambda e: e.tensor_scalar(Aq[:, 0:W], Q2[:, 0:W], 0.25, None, ALU.is_gt))
    run(p, "vector", lambda e: e.tensor_scalar(Bq[:, 0:W], Q2[:, 0:W], 0.75, None, ALU.is_gt))
    run(p, "vector", lambda e: e.tensor_add(Qq[:, 0:W], Aq[:, 0:W], Bq[:, 0:W]))
    run(p, "vector", lambda e: e.tensor_scalar(Aq[:, 0:W], Q2[:, 0:W], 1.25, None, ALU.is_gt))
    run(p, "vector", lambda e: e.tensor_scalar(Bq[:, 0:W], Q2[:, 0:W], 1.75, None, ALU.is_gt))
    run(p, "vector", lambda e: e.tensor_add(Aq[:, 0:W], Aq[:, 0:W], Bq[:, 0:W]))
    run(p, "vector", lambda e: e.tensor_add(Qq[:, 0:W], Qq[:, 0:W], Aq[:, 0:W]))
    run(p, "vector", lambda e: e.tensor_scalar(Aq[:, 0:W], Q2[:, 0:W], 2.5, None, ALU.is_gt))
    run(p, "vector", lambda e: e.tensor_scalar(Bq[:, 0:W], Q2[:, 0:W], 3.5, None, ALU.is_gt))
    run(p, "vector", lambda e: e.tensor_scalar_mul(Qq[:, 0:W], Qq[:, 0:W], 0.5))
    run(p, "vector", lambda e: e.tensor_add(Aq[:, 0:W], Aq[:, 0:W], Bq[:, 0:W]))
    run(p, "vector", lambda e: e.tensor_scalar(Bq[:, 0:W], Q2[:, 0:W], 5.0, None, ALU.is_gt))
    run(p, "vector", lambda e: e.tensor_add(Qq[:, 0:W], Qq[:, 0:W], Aq[:, 0:W]))
    t = run(p, "vector", lambda e: e.scalar_tensor_tensor(
        Qq[:, 0:W], Bq[:, 0:W], 2.0, Qq[:, 0:W], ALU.mult, ALU.add), va)
    wait(p, "vector", va, t)
    wait(p, "vector", ac, tsg)
    t = run(p, "vector", lambda e: e.tensor_mul(Qq[:, 0:W], Qq[:, 0:W], SG[:, 0:W]), va)
    wait(p, "vector", va, t)
    scb = lambda: bass.AP(SC, 0, [[SC.shape[1], 128], [1, NB], [0, 16]])
    t = run(p, "vector", lambda e: e.tensor_tensor(
        s3(DST, dst_off, W), s3(Qq, 0, W), scb(), ALU.mult), va if tick is None else tick)
    return t

def quant_scratch(p, W):
    return dict(BM=p.sbuf([128, W // 16], F32, None), SC=p.sbuf([128, W // 16], F32, None),
                RS=p.sbuf([128, W // 16], F32, None), V=p.sbuf([128, W], F32, None),
                SG=p.sbuf([128, W], BF16, None), A=p.sbuf([128, W], BF16, None),
                B=p.sbuf([128, W], BF16, None), Q=p.sbuf([128, W], BF16, None),
                Q2=p.sbuf([128, W], F32, None))


def build_stage_l0wq(ntiles, W, p=None, t=None):
    p = p or Prog("l0wq")
    nc = p.nc
    fz = t is not None and "wqt" in t
    w_i = _io(nc, t, "w", [ntiles * 128, W], F32, "ExternalInput")
    if fz:
        wqt = t["wqt"]
    else:
        w_o = _io(nc, t, "wq", [ntiles * 128, W], BF16, "ExternalOutput")
    SRC = p.sbuf([128, 2 * W], F32, "SRC")
    DST = p.sbuf([128, 2 * W], BF16, "DST")
    QS = quant_scratch(p, W)
    va = p.sem("va"); ac = p.sem("ac")
    ldp = [p.sem("ldp0"), p.sem("ldp1")]
    stp = [p.psem("stp0"), p.psem("stp1")]
    sems = dict(va=va, ac=ac)
    if fz:
        pe = p.sem("pe"); ev = p.sem("ev")
        PT = [p.psum([128, 128], BF16, f"pt{i}") for i in range(2)]
        TEV = p.sbuf([128, 2 * 128], BF16, "TEV")
        IDB = p.sbuf([128, 128], BF16, "IDB")
        ldi = dma(p, "sync", ldp[0], IDB[:, :], t["identbf"])
    qticks = []
    sdone = []
    ktr = 0
    trv = []
    for tt in range(ntiles):
        off = (tt % 2) * W
        if tt >= 2:
            wait(p, "sync", va, qticks[tt - 2])
        tl = dma(p, "sync", ldp[tt % 2], SRC[:, off:off + W], w_i[tt * 128:(tt + 1) * 128, :])
        def dep(tl=tl, tt=tt):
            wait(p, "vector", ldp[tt % 2], tl)
            if tt >= 2:
                wait(p, "vector", stp[0], sdone[tt - 2][0])
                wait(p, "vector", stp[1], sdone[tt - 2][1])
        tq = emit_quant(p, sems, SRC, DST, W, QS, src_off=off, dst_off=off, dep=dep)
        qticks.append(tq)
        if fz:
            for tc in range(W // 128):
                bank = ktr % 2
                wait(p, "tensor", va, tq)
                if ktr == 0:
                    wait(p, "tensor", ldp[0], ldi)
                if ktr >= 2:
                    wait(p, "tensor", ev, trv[ktr - 2])
                tp = run(p, "tensor", lambda e, off=off, tc=tc, bank=bank: e.transpose(
                    PT[bank][:, :], DST[:, off + tc * 128:off + (tc + 1) * 128],
                    IDB[:, :]), pe)
                wait(p, "vector", pe, tp)
                if ktr >= 2:
                    wait(p, "vector", stp[bank], 16 * (ktr // 2))
                trv.append(run(p, "vector", lambda e, bank=bank: e.tensor_copy(
                    TEV[:, bank * 128:bank * 128 + 128], PT[bank][:, :]), ev))
                wait(p, "sync", ev, trv[-1])
                dma(p, "sync", stp[bank], wqt[tc * 128:(tc + 1) * 128, tt * 128:(tt + 1) * 128],
                    TEV[:, bank * 128:bank * 128 + 128])
                ktr += 1
        else:
            wait(p, "sync", va, tq)
            dma(p, "sync", stp[tt % 2], w_o[tt * 128:(tt + 1) * 128, :], DST[:, off:off + W])
        sdone.append((stp[0].n, stp[1].n))
    wait(p, "sync", stp[0], stp[0].n)
    wait(p, "sync", stp[1], stp[1].n)
    return p


def build_stage_l0ffn(p=None, t=None):
    I = 2816
    NI = I // 128
    p = p or Prog("l0ffn")
    nc = p.nc
    xn_i = _io(nc, t, "xn", [T, D], F32, "ExternalInput")
    wags = t.get("w_ags") if t else None
    if wags is None:
        wg_i = _io(nc, t, "wgq", [D, I], BF16, "ExternalInput")
        wu_i = _io(nc, t, "wuq", [D, I], BF16, "ExternalInput")
        wd_i = _io(nc, t, "wdq", [I, D], BF16, "ExternalInput")
    identbf = _io(nc, t, "identbf", [128, 128], BF16, "ExternalInput")
    y_o = _io(nc, t, "y0f", [T, D], F32, "ExternalOutput")

    XNT = p.sbuf([128, D], F32, "XNT")
    XQ = p.sbuf([128, D], BF16, "XQ")
    XQF = p.sbuf([128, 2 * NC * 128], BF16, "XQF")
    WG = p.sbuf([128, NC * I], BF16, "WG")
    WU = p.sbuf([128, NC * I], BF16, "WU")
    WD = p.sbuf([128, NI * D], BF16, "WD")
    IDB = p.sbuf([128, 128], BF16, "IDB")
    GS = p.sbuf([128, 512], F32, "GS")
    HQ = p.sbuf([128, I], BF16, "HQ")
    HQF = p.sbuf([128, NI * 128], BF16, "HQF")
    YS = p.sbuf([128, D], F32, "YS")
    QS = quant_scratch(p, I)
    PSB = [p.psum([128, 512], F32, f"pp{i}") for i in range(2)]
    PT = [p.psum([128, 128], BF16, f"pt{i}") for i in range(2)]
    PY = [p.psum([128, 512], F32, f"py{i}") for i in range(2)]

    ldw = p.sem("ldw"); va = p.sem("va"); ac = p.sem("ac")
    pe = p.sem("pe"); ev = p.sem("ev"); st = p.psem("st")
    ldx = [p.sem("ldx0"), p.sem("ldx1")]
    sems = dict(va=va, ac=ac, pe=pe, ev=ev)

    dma(p, "sync", ldw, IDB[:, :], identbf)
    if wags is None:
        dma(p, "sync", ldw, WG[:, :], fm_ap(wg_i))
        dma(p, "sync", ldw, WU[:, :], fm_ap(wu_i))
        dma(p, "sync", ldw, WD[:, :], fm_ap(wd_i))
    else:
        wg_ag, wu_ag, wd_ag, per = wags
        offs = [0]
        for v in per:
            offs.append(offs[-1] + v)
        for ci in range(NC):
            for k in range(8):
                o0 = offs[k] * 128
                w_ = per[k] * 128
                dma(p, "sync", ldw, WG[:, ci * I + o0:ci * I + o0 + w_],
                    wg_ag[k * D + ci * 128:k * D + ci * 128 + 128, 0:w_])
                dma(p, "sync", ldw, WU[:, ci * I + o0:ci * I + o0 + w_],
                    wu_ag[k * D + ci * 128:k * D + ci * 128 + 128, 0:w_])
        for ic in range(NI):
            for k in range(8):
                dma(p, "sync", ldw, WD[:, ic * D + k * 128:ic * D + (k + 1) * 128],
                    wd_ag[k * I + ic * 128:k * I + ic * 128 + 128, :])
    LDW = ldw.n

    ICH = [512] * 5 + [256]
    prev_tq = [None]
    for tt in range(NT):
        off = 0
        xoff = (tt % 2) * NC * 128
        if prev_tq[0] is not None:
            wait(p, "sync", va, prev_tq[0])
        tl = dma(p, "sync", ldx[tt % 2], XNT[:, off:off + D], xn_i[tt * 128:(tt + 1) * 128, :])
        def dep(tl=tl, tt=tt):
            wait(p, "vector", ldx[tt % 2], tl)
            if tt >= 1:
                wait(p, "vector", pe, pe.n)
        tq = emit_quant(p, sems, XNT, XQ, D, QS, src_off=off, dst_off=off, dep=dep)
        prev_tq[0] = tq
        tps = []
        for c in range(NC):
            wait(p, "tensor", va, tq)
            if c == 0:
                wait(p, "tensor", ldw, LDW)
            bank = c % 2
            if c >= 2:
                wait(p, "tensor", ev, tps[c - 2])
            tp = run(p, "tensor", lambda e, c=c, bank=bank, off=off: e.transpose(
                PT[bank][:, :], XQ[:, off + c * 128:off + (c + 1) * 128], IDB[:, :]), pe)
            wait(p, "vector", pe, tp)
            tps.append(run(p, "vector", lambda e, c=c, bank=bank, xoff=xoff: e.tensor_copy(
                XQF[:, xoff + c * 128:xoff + (c + 1) * 128], PT[bank][:, :]), ev))
        V = QS["V"]
        th = None
        io0 = 0
        for icn, cw in enumerate(ICH):
            wait(p, "tensor", ldw, LDW)
            wait(p, "tensor", ev, tps[-1])
            if icn > 0:
                wait(p, "tensor", va, th)
                wait(p, "tensor", ac, ac.n)
            for hc in range(NC):
                run(p, "tensor", lambda e, hc=hc, io0=io0, cw=cw, xoff=xoff:
                    e.matmul(PSB[0][:, 0:cw],
                             XQF[:, xoff + hc * 128:xoff + (hc + 1) * 128],
                             WG[:, hc * I + io0:hc * I + io0 + cw],
                             start=(hc == 0), stop=(hc == NC - 1)),
                    pe if hc == NC - 1 else None)
            tg = pe.n
            for hc in range(NC):
                run(p, "tensor", lambda e, hc=hc, io0=io0, cw=cw, xoff=xoff:
                    e.matmul(PSB[1][:, 0:cw],
                             XQF[:, xoff + hc * 128:xoff + (hc + 1) * 128],
                             WU[:, hc * I + io0:hc * I + io0 + cw],
                             start=(hc == 0), stop=(hc == NC - 1)),
                    pe if hc == NC - 1 else None)
            tu = pe.n
            wait(p, "scalar", pe, tg)
            tsg_ = run(p, "scalar", lambda e, cw=cw: e.activation(
                GS[:, 0:cw], PSB[0][:, 0:cw], AF.Silu), ac)
            wait(p, "vector", pe, tu)
            wait(p, "vector", ac, tsg_)
            th = run(p, "vector", lambda e, io0=io0, cw=cw: e.scalar_tensor_tensor(
                V[:, io0:io0 + cw], PSB[1][:, 0:cw], 1.0, GS[:, 0:cw],
                ALU.mult, ALU.mult), va)
            io0 += cw
        def dep2(th=th):
            wait(p, "vector", va, th)
        tqh = emit_quant(p, sems, V, HQ, I, QS, dep=dep2)
        tps2 = []
        for c in range(NI):
            wait(p, "tensor", va, tqh)
            bank = c % 2
            if c >= 2:
                wait(p, "tensor", ev, tps2[c - 2])
            tp = run(p, "tensor", lambda e, c=c, bank=bank: e.transpose(
                PT[bank][:, :], HQ[:, c * 128:(c + 1) * 128], IDB[:, :]), pe)
            wait(p, "vector", pe, tp)
            tps2.append(run(p, "vector", lambda e, c=c, bank=bank: e.tensor_copy(
                HQF[:, c * 128:(c + 1) * 128], PT[bank][:, :]), ev))
        yev = []
        for dh in range(2):
            bank = dh % 2
            wait(p, "tensor", ev, tps2[-1])
            wait(p, "tensor", ldw, LDW)

            for icn in range(NI):
                run(p, "tensor", lambda e, icn=icn, dh=dh, bank=bank:
                    e.matmul(PY[bank][:, :],
                             HQF[:, icn * 128:(icn + 1) * 128],
                             WD[:, icn * D + dh * 512:icn * D + dh * 512 + 512],
                             start=(icn == 0), stop=(icn == NI - 1)),
                    pe if icn == NI - 1 else None)
            wait(p, "vector", pe, pe.n)
            if tt > 0 and dh == 0:
                wait(p, "vector", st, 16 * tt)
            yev.append(run(p, "vector", lambda e, dh=dh, bank=bank: e.tensor_copy(
                YS[:, dh * 512:dh * 512 + 512], PY[bank][:, :]), ev))
        wait(p, "sync", ev, yev[-1])
        dma(p, "sync", st, y_o[tt * 128:(tt + 1) * 128, :], YS[:, :])
    wait(p, "sync", st, st.n)
    return p



def build_stage_l1qkv(zn_scale=False, rope=True, name="l1qkv", p=None, t=None):
    p = p or Prog(name)
    nc = p.nc
    xnfm_i = _io(nc, t, "xnfm", [D, T], BF16, "ExternalInput")
    wq_i = _io(nc, t, "wq", [D, D], F32, "ExternalInput")
    wk_i = _io(nc, t, "wk", [D, D], F32, "ExternalInput")
    wv_i = _io(nc, t, "wv", [D, D], F32, "ExternalInput")
    bq_i = _io(nc, t, "bq", [128, 8], F32, "ExternalInput")
    bk_i = _io(nc, t, "bk", [128, 8], F32, "ExternalInput")
    bv_i = _io(nc, t, "bv", [1, D], F32, "ExternalInput")
    if rope:
        cos_i = _io(nc, t, "cosr", [128, T], BF16, "ExternalInput")
        sin_i = _io(nc, t, "sinr", [128, T], BF16, "ExternalInput")
        rp_i = _io(nc, t, "rotperm", [128, 128], BF16, "ExternalInput")
    if zn_scale:
        lnw_i = _io(nc, t, "lnw", [128, 8], F32, "ExternalInput")
        lnb_i = _io(nc, t, "lnb", [128, 8], F32, "ExternalInput")
    qr_o = _io(nc, t, "qr", [D, T], BF16, "ExternalOutput")
    kr_o = _io(nc, t, "kr", [D, T], BF16, "ExternalOutput")
    vem = t.get("vemode") if t else None
    if vem is None:
        v_o = _io(nc, t, "v", [T, D], BF16, "ExternalOutput")

    XF = p.sbuf([128, NC * T], BF16, "XF")
    QF = p.sbuf([128, NC * T], BF16, "QF")
    QR = p.sbuf([128, NC * T], BF16, "QR")
    if rope:
        SCA = p.sbuf([128, 1024], F32, "SCA")
        RP = p.sbuf([128, 128], BF16, "RP")
        COSR = p.sbuf([128, T], BF16, "COSR")
        SINR = p.sbuf([128, T], BF16, "SINR")
    if zn_scale:
        LNW = p.sbuf([128, 8], F32, "LNW")
        LNB = p.sbuf([128, 8], F32, "LNB")
    BQ = p.sbuf([128, 8], F32, "BQ")
    BK = p.sbuf([128, 8], F32, "BK")
    BV = p.sbuf([1, D], BF16, "BV")
    ONESR = p.sbuf([1, 128], BF16, "ONESR")
    VT = p.sbuf([128, 2 * 512], BF16, "VT")
    PSB = [p.psum([128, 512], F32, f"pp{i}") for i in range(4)]
    lda = p.sem("lda"); gld = p.sem("gld"); va = p.sem("va"); ac = p.sem("ac")
    pe = p.sem("pe"); ev = p.sem("ev"); st = p.psem("st")
    stv = [p.psem("stv0"), p.psem("stv1")]
    sems = dict(va=va, ac=ac, pe=pe, ev=ev, psumpool=PSB)

    dma(p, "sync", lda, XF[:, :], fm_ap(xnfm_i))
    if rope:
        dma(p, "sync", lda, COSR[:, :], cos_i)
        dma(p, "sync", lda, SINR[:, :], sin_i)
    if zn_scale:
        dma(p, "sync", lda, LNW[:, :], lnw_i)
        dma(p, "sync", lda, LNB[:, :], lnb_i)
    dma(p, "sync", lda, BQ[:, :], bq_i)
    dma(p, "sync", lda, BK[:, :], bk_i)
    LDA = lda.n
    dma(p, "gpsimd", gld, BV[:, :], bv_i)
    if rope:
        dma(p, "gpsimd", gld, RP[:, :], rp_i)
    GLD = gld.n
    WQ, wsq = load_w_fm(p, None, wq_i, D, D, "WQ")
    WK, wsk = load_w_fm(p, None, wk_i, D, D, "WK")
    WV, wsv = load_w_fm(p, None, wv_i, D, D, "WV")
    run(p, "vector", lambda e: e.memset(ONESR[:, :], 1.0))

    def _rope_perm_fm(SRC, DST, deps_ev, outs):
        for c in range(NC):
            deps_ev(c)
            for hf in range(2):
                sl = slice(c * T + hf * 1024, c * T + (hf + 1) * 1024)
                bank = (c * 2 + hf) % 2
                wait(p, "tensor", gld, GLD)
                if c * 2 + hf >= 2:
                    wait(p, "tensor", va, outs[-2] if len(outs) >= 2 else outs[-1])
                tm_ = run(p, "tensor", lambda e, sl=sl, bank=bank: e.matmul(
                    PSB[bank][:, 0:512], RP[:, :], SRC[:, sl.start:sl.start + 512],
                    start=True, stop=True), pe)
                tm2 = run(p, "tensor", lambda e, sl=sl, bank=bank: e.matmul(
                    PSB[2 + bank][:, 0:512], RP[:, :], SRC[:, sl.start + 512:sl.start + 1024],
                    start=True, stop=True), pe)
                run(p, "vector", lambda e, sl=sl: e.tensor_mul(
                    DST[:, sl], SRC[:, sl], COSR[:, (sl.start % T):(sl.start % T) + 1024]))
                wait(p, "vector", pe, tm2)
                run(p, "vector", lambda e, sl=sl, bank=bank: e.tensor_mul(
                    SCA[:, 0:512], PSB[bank][:, 0:512], SINR[:, (sl.start % T):(sl.start % T) + 512]))
                run(p, "vector", lambda e, sl=sl, bank=bank: e.tensor_mul(
                    SCA[:, 512:1024], PSB[2 + bank][:, 0:512],
                    SINR[:, (sl.start % T) + 512:(sl.start % T) + 1024]))
                run(p, "vector", lambda e, sl=sl: e.tensor_add(
                    DST[:, sl.start:sl.start + 512], DST[:, sl.start:sl.start + 512],
                    SCA[:, 0:512]))
                outs.append(run(p, "vector", lambda e, sl=sl: e.tensor_add(
                    DST[:, sl.start + 512:sl.start + 1024], DST[:, sl.start + 512:sl.start + 1024],
                    SCA[:, 512:1024]), va))

    if zn_scale:
        wait(p, "vector", lda, LDA)
        zt = None
        for c in range(NC):
            zt = run(p, "vector", lambda e, c=c: e.tensor_scalar(
                XF[:, c * T:(c + 1) * T], XF[:, c * T:(c + 1) * T],
                LNW[:, c:c + 1], LNB[:, c:c + 1], ALU.mult, ALU.add), va)
        wait(p, "tensor", va, zt)
        wait(p, "scalar", va, zt)
    wait(p, "vector", lda, LDA)
    qo = emit_proj_fm(p, sems, XF, WQ, QF, D, bias_col=BQ, pre=[(lda, LDA), (wsq, wsq.n)],
                      evac_eng="vector")
    qrv = []
    if rope:
        def qdeps(c, qo=qo):
            wait(p, "vector", ev, qo[c * 4 + 3])
            wait(p, "tensor", ev, qo[-1])
        _rope_perm_fm(QF, QR, qdeps, qrv)
        wait(p, "sync", va, qrv[-1])
        dma(p, "sync", st, fm_ap(qr_o), QR[:, :])
    else:
        qrv = [va.tick(0)]
        wait(p, "sync", ev, qo[-1])
        dma(p, "sync", st, fm_ap(qr_o), QF[:, :])
    def kevw(k, qrv=qrv):
        if qrv and qrv[-1]:
            wait(p, "vector", va, qrv[-1])
        wait(p, "vector", st, 16)
    def kpre(k, qrv=qrv):
        if k == 0 and rope:
            wait(p, "tensor", va, qrv[-1])
    ko = emit_proj_fm(p, sems, XF, WK, QF, D, bias_col=BK, pre=[(lda, LDA), (wsk, wsk.n)],
                      evac_eng="vector", evac_wait=kevw, extra_wait=kpre)
    krv = []
    if rope:
        def kdeps(c, ko=ko):
            wait(p, "vector", ev, ko[c * 4 + 3])
            wait(p, "tensor", ev, ko[-1])
            wait(p, "vector", st, 16)
        _rope_perm_fm(QF, QR, kdeps, krv)
        wait(p, "sync", va, krv[-1])
        dma(p, "sync", st, fm_ap(kr_o), QR[:, :])
    else:
        krv = [va.tick(0)]
        wait(p, "sync", ev, ko[-1])
        dma(p, "sync", st, fm_ap(kr_o), QF[:, :])
    vev = []
    if rope:
        wait(p, "tensor", va, krv[-1])
    for tt in range(NT):
        for dh in range(2):
            k = tt * 2 + dh
            bank = k % 2 + 2
            wait(p, "tensor", gld, GLD)
            wait(p, "tensor", wsv, wsv.n)
            wait(p, "tensor", lda, LDA)
            if k >= 2:
                wait(p, "tensor", ev, vev[k - 2])
            for ci in range(NC):
                run(p, "tensor", lambda e, ci=ci, tt=tt, dh=dh, bank=bank:
                    e.matmul(PSB[bank][:, :],
                             XF[:, ci * T + tt * 128:ci * T + (tt + 1) * 128],
                             WV[:, ci * D + dh * 512:ci * D + dh * 512 + 512],
                             start=(ci == 0), stop=False), None)
            tv = run(p, "tensor", lambda e, tt=tt, dh=dh, bank=bank:
                e.matmul(PSB[bank][:, :], ONESR[0:1, 0:128],
                         BV[0:1, dh * 512:dh * 512 + 512], start=False, stop=True), pe)
            wait(p, "vector", pe, tv)
            if k >= 2:
                wait(p, "vector", stv[k % 2], 16 * (k // 2))
            vev.append(run(p, "vector", lambda e, k=k, bank=bank: e.tensor_copy(
                VT[:, (k % 2) * 512:(k % 2) * 512 + 512], PSB[bank][:, :]), ev))
            wait(p, "sync", ev, vev[-1])
            if vem is None:
                dma(p, "sync", stv[k % 2], v_o[tt * 128:(tt + 1) * 128, dh * 512:dh * 512 + 512],
                    VT[:, (k % 2) * 512:(k % 2) * 512 + 512])
            elif vem[0] == "ve16":
                dma(p, "sync", stv[k % 2],
                    bass.AP(vem[1], tt * 128 * 1040 + dh * 520, [[1040, 128], [65, 8], [1, 64]]),
                    VT[:, (k % 2) * 512:(k % 2) * 512 + 512])
            else:
                dma(p, "sync", stv[k % 2],
                    bass.AP(vem[1], dh * T * 520 + tt * 128 * 520, [[520, 128], [65, 8], [1, 64]]),
                    VT[:, (k % 2) * 512:(k % 2) * 512 + 512])
    wait(p, "sync", st, st.n)
    wait(p, "sync", stv[0], stv[0].n)
    wait(p, "sync", stv[1], stv[1].n)
    return p



def build_stage_mha(*, causal, scale, name="mha", nheads=16, p=None, t=None):
    p = p or Prog(name)
    nc = p.nc
    NKT = S // 128
    qr_i = _io(nc, t, "qr", [D, T], BF16, "ExternalInput")
    kag = t.get("krf_ag") if t else None
    if kag is None:
        krf_i = _io(nc, t, "krf", [D, S], BF16, "ExternalInput")
    vef_i = _io(nc, t, "vef", [S, nheads * 65], BF16, "ExternalInput")
    wo_i = _io(nc, t, "wo", [D, D], F32, "ExternalInput")
    bo_i = _io(nc, t, "bo", [1, D], F32, "ExternalInput")
    if causal:
        bcn_i = _io(nc, t, "bcn", [128, 4 * 32], F32, "ExternalInput")
        smsk2_i = _io(nc, t, "smsk2", [128, 8 * 512], BF16, "ExternalInput")
    ya_o = _io(nc, t, "ya", [T, D], F32, "ExternalOutput")
    VEF = p.sbuf([128, NKT * nheads * 65], BF16, "VEF")
    if causal:
        SMSK = p.sbuf([128, 8 * 512], BF16, "SMSK")
        BCN = p.sbuf([128, 4 * 32], F32, "BCN")
    QE = p.sbuf([65, 2 * T], BF16, "QE")
    KE = p.sbuf([65, 2 * S], BF16, "KE")
    SQ = p.sbuf([64, S], BF16, "SQ")
    QN = p.sbuf([1, T], F32, "QN")
    KN = p.sbuf([1, S], F32, "KN")
    KM = p.sbuf([1, 1], F32, "KM")
    MQ = p.sbuf([1, T], F32, "MQ")
    ONE64 = p.sbuf([64, 1], BF16, "ONE64")
    ONEB = p.sbuf([128, 64], F32, "ONEB")
    EXPT = p.sbuf([128, 2 * 1024], BF16, "EXPT")
    SCR = p.sbuf([128, 1024], F32, "SCR")
    AFM = p.sbuf([65, 512], F32, "AFM")
    RDS = p.sbuf([128, 512], F32, "RDS")
    AO = p.sbuf([128, NC * T], BF16, "AO")
    BO = p.sbuf([1, D], BF16, "BO")
    YS = p.sbuf([128, 2 * 512], F32, "YS")
    ONES1 = p.sbuf([1, 128], BF16, "ONES1")
    PSC = [p.psum([128, 1024], F32, f"psc{i}") for i in range(2)]
    PAV = p.psum([65, 512], F32, "pav")
    PY = [p.psum([128, 512], F32, f"py{i}") for i in range(2)]
    PQN = PY[0]
    PBC = PY[1]

    lda = p.sem("lda"); gld = p.sem("gld"); va = p.sem("va"); ac = p.sem("ac")
    pe = p.sem("pe"); ev = p.sem("ev"); st = p.psem("st")
    hl = [p.sem("hl0"), p.sem("hl1")]
    sty = [p.psem("sty0"), p.psem("sty1")]

    dma(p, "sync", lda, VEF[:, :], vef_i.rearrange("(r p) n -> p r n", p=128))
    if causal:
        dma(p, "sync", lda, SMSK[:, :], smsk2_i)
        dma(p, "sync", lda, BCN[:, :], bcn_i)
    LDA = lda.n
    dma(p, "gpsimd", gld, BO[:, :], bo_i)
    GLD = gld.n
    WO, wso = load_w_fm(p, None, wo_i, D, D, "WO")
    run(p, "vector", lambda e: e.memset(ONE64[:, :], 1.0))
    run(p, "vector", lambda e: e.memset(ONES1[:, :], 1.0))
    t_ones = run(p, "vector", lambda e: e.memset(ONEB[64:128, :], 1.0), va)

    t_hl = {}
    def load_head(h):
        boq = (h % 2) * T
        bok = (h % 2) * S
        dma(p, "sync", hl[h % 2], QE[0:64, boq:boq + T], qr_i[h * 64:(h + 1) * 64, :])
        if kag is None:
            t_hl[h] = dma(p, "sync", hl[h % 2], KE[0:64, bok:bok + S], krf_i[h * 64:(h + 1) * 64, :])
        else:
            dma(p, "sync", hl[h % 2], KE[0:64, bok:bok + T], kag[h * 64:h * 64 + 64, :])
            t_hl[h] = dma(p, "sync", hl[h % 2], KE[0:64, bok + T:bok + S],
                          kag[D + h * 64:D + h * 64 + 64, :])
    load_head(0)
    load_head(1)

    aov = []
    headdone = []
    for h in range(nheads):
        boq = (h % 2) * T
        bok = (h % 2) * S
        wait(p, "scalar", hl[h % 2], t_hl[h])
        tsq = run(p, "scalar", lambda e, bok=bok: e.activation(
            SQ[:, 0:S], KE[0:64, bok:bok + S], AF.Square), ac)
        ksm = []
        kevs = []
        for rc in range(S // 512):
            wait(p, "tensor", ac, tsq)
            wait(p, "tensor", va, t_ones)
            if kevs:
                wait(p, "tensor", ev, kevs[-1])
            ksm.append(run(p, "tensor", lambda e, rc=rc: e.matmul(
                PQN[0:1, 0:512], ONE64[:, :], SQ[:, rc * 512:(rc + 1) * 512],
                start=True, stop=True), pe))
            wait(p, "vector", pe, ksm[-1])
            kevs.append(run(p, "vector", lambda e, rc=rc: e.tensor_copy(
                KN[:, rc * 512:(rc + 1) * 512], PQN[0:1, 0:512]), ev))
        wait(p, "vector", ev, ev.n)
        tkm = run(p, "vector", lambda e: e.reduce_max(KM[:, :], KN[:, :], axis=AX), va)
        wait(p, "scalar", pe, ksm[-1])
        tsq2 = run(p, "scalar", lambda e, boq=boq: e.activation(
            SQ[:, 0:T], QE[0:64, boq:boq + T], AF.Square), ac)
        qsm = []
        for tc in range(T // 512):
            wait(p, "tensor", ac, tsq2)
            if kevs:
                wait(p, "tensor", ev, kevs[-1])
            qsm.append(run(p, "tensor", lambda e, tc=tc: e.matmul(
                PQN[0:1, 0:512], ONE64[:, :], SQ[:, tc * 512:(tc + 1) * 512],
                start=True, stop=True), pe))
            wait(p, "vector", pe, qsm[-1])
            kevs.append(run(p, "vector", lambda e, tc=tc: e.tensor_copy(
                QN[:, tc * 512:(tc + 1) * 512], PQN[0:1, 0:512]), ev))
        wait(p, "vector", ev, ev.n)
        wait(p, "vector", va, tkm)
        t1 = run(p, "vector", lambda e: e.tensor_scalar(
            MQ[:, :], QN[:, :], KM[0:1, 0:1], None, ALU.mult), va)
        wait(p, "scalar", va, t1)
        t2 = run(p, "scalar", lambda e: e.activation(MQ[:, :], MQ[:, :], AF.Sqrt), ac)
        wait(p, "vector", ac, t2)
        t3 = run(p, "vector", lambda e, boq=boq: e.tensor_scalar(
            QE[64:65, boq:boq + T], MQ[:, :], -1.0, None, ALU.mult), va)
        t4 = run(p, "vector", lambda e, bok=bok: e.memset(KE[64:65, bok:bok + S], 1.0), va)
        for qc in range(4):
            slotp = ([2 * qc, 2 * qc + 1, 8 + 2 * qc, 8 + 2 * qc + 1]) if causal else []
            expn = []
            pvnp = []

            def emit_pv(kp, expn=expn, pvnp=pvnp, h=h):
                wait(p, "tensor", ac, expn[kp])
                if kp == 0:
                    wait(p, "tensor", ev, ev.n)
                tpv = None
                for j2 in range(2):
                    kt = 2 * kp + j2
                    tpv = run(p, "tensor", lambda e, kt=kt, h=h, kp=kp, j2=j2: e.matmul(
                        PAV[:, :], VEF[:, kt * nheads * 65 + h * 65:kt * nheads * 65 + (h + 1) * 65],
                        EXPT[:, (kp % 2) * 1024 + j2 * 512:(kp % 2) * 1024 + (j2 + 1) * 512],
                        start=(kt == 0), stop=(kt == NKT - 1)), pe)
                pvnp.append(tpv)

            for kp in range(NKT // 2):
                bank = kp % 2
                wait(p, "tensor", va, max(t3, t4))
                wait(p, "tensor", hl[h % 2], t_hl[h])
                wait(p, "tensor", lda, LDA)
                if len(expn) >= 2:
                    wait(p, "tensor", ac, expn[-2])
                for j2 in range(2):
                    kt = 2 * kp + j2
                    run(p, "tensor", lambda e, kt=kt, qc=qc, bank=bank, j2=j2, bok=bok, boq=boq:
                        e.matmul(PSC[bank][:, j2 * 512:(j2 + 1) * 512],
                                 KE[0:65, bok + kt * 128:bok + (kt + 1) * 128],
                                 QE[0:65, boq + qc * 512:boq + qc * 512 + 512],
                                 start=True, stop=True), pe)
                if pvnp:
                    wait(p, "scalar", pe, pvnp[-1])
                bcol = qc * 32 + 2 * kp
                if causal and kp in slotp:
                    wait(p, "vector", pe, pe.n)
                    if expn:
                        wait(p, "vector", ac, expn[-1])
                    wait(p, "vector", lda, LDA)
                    for j2 in range(2):
                        kt = 2 * kp + j2
                        sj = (kt - qc * 4) if kt < 16 else 4 + (kt - 16 - qc * 4)
                        tvm = run(p, "vector", lambda e, bank=bank, sj=sj, j2=j2: e.tensor_add(
                            SCR[:, j2 * 512:(j2 + 1) * 512],
                            PSC[bank][:, j2 * 512:(j2 + 1) * 512],
                            SMSK[:, sj * 512:(sj + 1) * 512]), va)
                    wait(p, "scalar", va, tvm)
                    expn.append(run(p, "scalar", lambda e, kp=kp, bcol=bcol: e.activation(
                        EXPT[:, (kp % 2) * 1024:(kp % 2) * 1024 + 1024], SCR[:, :], AF.Exp,
                        scale=scale, bias=BCN[:, bcol:bcol + 1]), ac))
                elif causal:
                    wait(p, "scalar", pe, pe.n)
                    expn.append(run(p, "scalar", lambda e, kp=kp, bank=bank, bcol=bcol:
                        e.activation(
                            EXPT[:, (kp % 2) * 1024:(kp % 2) * 1024 + 1024], PSC[bank][:, :],
                            AF.Exp, scale=scale, bias=BCN[:, bcol:bcol + 1]), ac))
                else:
                    wait(p, "scalar", pe, pe.n)
                    expn.append(run(p, "scalar", lambda e, kp=kp, bank=bank: e.activation(
                        EXPT[:, (kp % 2) * 1024:(kp % 2) * 1024 + 1024], PSC[bank][:, :],
                        AF.Exp, scale=scale), ac))
                if kp >= 1:
                    emit_pv(kp - 1)
            emit_pv(NKT // 2 - 1)
            wait(p, "vector", pe, pvnp[-1])
            if aov:
                wait(p, "vector", va, aov[-1])
            te = run(p, "vector", lambda e: e.tensor_copy(AFM[:, :], PAV[:, :]), ev)
            wait(p, "vector", ev, te)
            tr = run(p, "vector", lambda e: e.reciprocal(RDS[64:65, :], AFM[64:65, :]), va)
            wait(p, "tensor", va, tr)
            if aov:
                wait(p, "tensor", va, aov[-1])
            tb = run(p, "tensor", lambda e: e.matmul(
                PBC[0:64, 0:512], ONEB[64:65, 0:64], RDS[64:65, :], start=True, stop=True), pe)
            wait(p, "vector", pe, tb)
            aov.append(run(p, "vector", lambda e, h=h, qc=qc: e.tensor_mul(
                AO[(h % 2) * 64:(h % 2) * 64 + 64,
                   (h // 2) * T + qc * 512:(h // 2) * T + qc * 512 + 512],
                AFM[0:64, :], PBC[0:64, 0:512]), va))
        if h + 2 < nheads:
            wait(p, "sync", pe, pe.n)
            load_head(h + 2)
    yv = []
    wait(p, "tensor", va, aov[-1])
    for tt in range(NT):
        for dc in range(2):
            k = tt * 2 + dc
            bank = k % 2
            wait(p, "tensor", gld, GLD)
            wait(p, "tensor", wso, wso.n)
            if k >= 2:
                wait(p, "tensor", ev, yv[k - 2])
            for ci in range(NC):
                run(p, "tensor", lambda e, ci=ci, tt=tt, dc=dc, bank=bank:
                    e.matmul(PY[bank][:, :],
                             AO[:, ci * T + tt * 128:ci * T + (tt + 1) * 128],
                             WO[:, ci * D + dc * 512:ci * D + dc * 512 + 512],
                             start=(ci == 0), stop=False), None)
            tv = run(p, "tensor", lambda e, dc=dc, bank=bank:
                e.matmul(PY[bank][:, :], ONES1[0:1, 0:128],
                         BO[0:1, dc * 512:dc * 512 + 512], start=False, stop=True), pe)
            wait(p, "vector", pe, tv)
            if k >= 2:
                wait(p, "vector", sty[k % 2], 16 * (k // 2))
            yv.append(run(p, "vector", lambda e, k=k, bank=bank: e.tensor_copy(
                YS[:, (k % 2) * 512:(k % 2) * 512 + 512], PY[bank][:, :]), ev))
            wait(p, "sync", ev, yv[-1])
            dma(p, "sync", sty[k % 2], ya_o[tt * 128:(tt + 1) * 128, dc * 512:dc * 512 + 512],
                YS[:, (k % 2) * 512:(k % 2) * 512 + 512])
    wait(p, "sync", sty[0], sty[0].n)
    wait(p, "sync", sty[1], sty[1].n)
    return p



def build_stage_l1ffn(p=None, t=None):
    I = 2048
    NI = I // 128
    p = p or Prog("l1ffn")
    nc = p.nc
    xnfm_i = _io(nc, t, "xnfm", [D, T], BF16, "ExternalInput")
    sgw_i = _io(nc, t, "sgw", [D, I], F32, "ExternalInput")
    suw_i = _io(nc, t, "suw", [D, I], F32, "ExternalInput")
    sdw_i = _io(nc, t, "sdw", [I, D], F32, "ExternalInput")
    rw_i = _io(nc, t, "rw", [D, 8], F32, "ExternalInput")
    rb_i = _io(nc, t, "rb", [1, 8], F32, "ExternalInput")
    wg_i = _io(nc, t, "wg", [8, D, I], F32, "ExternalInput")
    wu_i = _io(nc, t, "wu", [8, D, I], F32, "ExternalInput")
    wd_i = _io(nc, t, "wd", [8, I, D], F32, "ExternalInput")
    y_o = _io(nc, t, "y1f", [T, D], F32, "ExternalOutput")

    XF = p.sbuf([128, NC * T], BF16, "XF")
    WGB = p.sbuf([128, NC * I], BF16, "WGB")
    WUB = p.sbuf([128, NC * I], BF16, "WUB")
    WDB = p.sbuf([128, NI * D], BF16, "WDB")
    RW = p.sbuf([128, NC * 8], BF16, "RW")
    RB = p.sbuf([1, 8], BF16, "RB")
    ONES1 = p.sbuf([1, 128], BF16, "ONES1")
    PL = p.sbuf([128, NT * 8], F32, "PL")
    WE = p.sbuf([128, NT * 8], F32, "WE")
    M1 = p.sbuf([128, NT], F32, "M1")
    M2 = p.sbuf([128, NT], F32, "M2")
    EQ = p.sbuf([128, 8], F32, "EQ")
    P2 = p.sbuf([128, 8], F32, "P2")
    WS = p.sbuf([128, NT], F32, "WS")
    RWS = p.sbuf([128, NT], F32, "RWS")
    GS = p.sbuf([128, 512], F32, "GS")
    HFM = p.sbuf([128, NI * 256], BF16, "HFM")
    YACC = p.sbuf([128, NT * D], F32, "YACC")
    PSA = p.psum([128, 512], F32, "psa")
    PSB2 = p.psum([128, 512], F32, "psb2")
    PR = p.psum([128, 8], F32, "pr")
    PY = [p.psum([128, 512], F32, f"py{i}") for i in range(2)]

    lda = p.sem("lda"); va = p.sem("va"); ac = p.sem("ac")
    pe = p.sem("pe"); ev = p.sem("ev"); st = p.psem("st")
    ldw = [p.sem("ldw0"), p.sem("ldw1")]

    dma(p, "sync", lda, XF[:, :], fm_ap(xnfm_i))
    gl = p.sem("gl")
    dma(p, "gpsimd", gl, RW[:, :], fm_ap(rw_i))
    dma(p, "gpsimd", gl, RB[:, :], rb_i)
    GL = gl.n
    LDA = lda.n
    run(p, "vector", lambda e: e.memset(ONES1[:, :], 1.0))

    rtev = []
    for tt in range(NT):
        wait(p, "tensor", lda, LDA)
        wait(p, "tensor", gl, GL)
        if rtev:
            wait(p, "tensor", ac, rtev[-1])
        for ci in range(NC):
            run(p, "tensor", lambda e, ci=ci, tt=tt: e.matmul(
                PR[:, :], XF[:, ci * T + tt * 128:ci * T + (tt + 1) * 128],
                RW[:, ci * 8:(ci + 1) * 8], start=(ci == 0), stop=False), None)
        tb = run(p, "tensor", lambda e, tt=tt: e.matmul(
            PR[:, :], ONES1[0:1, 0:128], RB[0:1, :], start=False, stop=True), pe)
        wait(p, "scalar", pe, tb)
        rtev.append(run(p, "scalar", lambda e, tt=tt: e.activation(
            PL[:, tt * 8:(tt + 1) * 8], PR[:, :], AF.Sigmoid), ac))
    wev = []
    for tt in range(NT):
        wait(p, "vector", ac, rtev[tt])
        t1 = run(p, "vector", lambda e, tt=tt: e.reduce_max(
            M1[:, tt:tt + 1], PL[:, tt * 8:(tt + 1) * 8], axis=AX), va)
        wait(p, "vector", va, t1)
        teq = run(p, "vector", lambda e, tt=tt: e.tensor_scalar(
            EQ[:, :], PL[:, tt * 8:(tt + 1) * 8], M1[:, tt:tt + 1], -1e9, ALU.is_ge, ALU.mult), va)
        wait(p, "vector", va, teq)
        t2 = run(p, "vector", lambda e, tt=tt: e.tensor_add(
            P2[:, :], PL[:, tt * 8:(tt + 1) * 8], EQ[:, :]), va)
        wait(p, "vector", va, t2)
        t3 = run(p, "vector", lambda e, tt=tt: e.reduce_max(
            M2[:, tt:tt + 1], P2[:, :], axis=AX), va)
        wait(p, "vector", va, t3)
        t4 = run(p, "vector", lambda e, tt=tt: e.tensor_add(
            WS[:, tt:tt + 1], M1[:, tt:tt + 1], M2[:, tt:tt + 1]), va)
        wait(p, "vector", va, t4)
        t5 = run(p, "vector", lambda e, tt=tt: e.reciprocal(
            RWS[:, tt:tt + 1], WS[:, tt:tt + 1]), va)
        wait(p, "vector", va, t5)
        teq2 = run(p, "vector", lambda e, tt=tt: e.tensor_scalar(
            EQ[:, :], PL[:, tt * 8:(tt + 1) * 8], M2[:, tt:tt + 1], None, ALU.is_ge), va)
        wait(p, "vector", va, teq2)
        t6 = run(p, "vector", lambda e, tt=tt: e.tensor_mul(
            P2[:, :], PL[:, tt * 8:(tt + 1) * 8], EQ[:, :]), va)
        wait(p, "vector", va, t6)
        wev.append(run(p, "vector", lambda e, tt=tt: e.tensor_scalar(
            WE[:, tt * 8:(tt + 1) * 8], P2[:, :], RWS[:, tt:tt + 1], None, ALU.mult), va))

    def load_expert(e):
        s = ldw[(e + 1) % 2]
        if e < 0:
            dma(p, "gpsimd", s, WGB[:, :], fm_ap(sgw_i))
            dma(p, "gpsimd", s, WUB[:, :], fm_ap(suw_i))
            dma(p, "gpsimd", s, WDB[:, :], fm_ap(sdw_i))
        else:
            dma(p, "gpsimd", s, WGB[:, :], wg_i[e].rearrange("(c p) n -> p c n", p=128))
            dma(p, "gpsimd", s, WUB[:, :], wu_i[e].rearrange("(c p) n -> p c n", p=128))
            dma(p, "gpsimd", s, WDB[:, :], wd_i[e].rearrange("(c p) n -> p c n", p=128))
        return s, s.n

    exp_pe_done = [0]
    lw = {}
    lw[-1] = load_expert(-1)
    for e in range(-1, 8):
        s, tot = lw[e]
        hready = []
        yadd = []
        last_tsg = [None]
        for tc in range(8):
            for ic in range(NI):
                wait(p, "tensor", s, tot)
                wait(p, "tensor", lda, LDA)
                if last_tsg[0] is not None:
                    wait(p, "tensor", ac, last_tsg[0])
                if hready:
                    wait(p, "tensor", va, hready[-1])
                for hc in range(NC):
                    run(p, "tensor", lambda e2, hc=hc, ic=ic, tc=tc: e2.matmul(
                        PSA[:, 0:256], WGB[:, hc * I + ic * 128:hc * I + (ic + 1) * 128],
                        XF[:, hc * T + tc * 256:hc * T + tc * 256 + 256],
                        start=(hc == 0), stop=(hc == NC - 1)), pe if hc == NC - 1 else None)
                tg = pe.n
                for hc in range(NC):
                    run(p, "tensor", lambda e2, hc=hc, ic=ic, tc=tc: e2.matmul(
                        PSB2[:, 0:256], WUB[:, hc * I + ic * 128:hc * I + (ic + 1) * 128],
                        XF[:, hc * T + tc * 256:hc * T + tc * 256 + 256],
                        start=(hc == 0), stop=(hc == NC - 1)), pe if hc == NC - 1 else None)
                tu = pe.n
                wait(p, "scalar", pe, tg)
                tsg = run(p, "scalar", lambda e2: e2.activation(
                    GS[:, 0:256], PSA[:, 0:256], AF.Silu), ac)
                last_tsg[0] = tsg
                wait(p, "vector", pe, tu)
                wait(p, "vector", ac, tsg)
                hready.append(run(p, "vector", lambda e2, ic=ic: e2.scalar_tensor_tensor(
                    HFM[:, ic * 256:(ic + 1) * 256], PSB2[:, 0:256], 1.0, GS[:, 0:256],
                    ALU.mult, ALU.mult), va))
            for ts in range(2):
                tt = tc * 2 + ts
                for dc in range(2):
                    bank = (ts * 2 + dc) % 2
                    wait(p, "tensor", va, hready[-1])
                    if len(yadd) >= 2:
                        wait(p, "tensor", va, yadd[-2])
                    for ic in range(NI):
                        run(p, "tensor", lambda e2, ic=ic, ts=ts, dc=dc, bank=bank: e2.matmul(
                            PY[bank][:, :], HFM[:, ic * 256 + ts * 128:ic * 256 + (ts + 1) * 128],
                            WDB[:, ic * D + dc * 512:ic * D + dc * 512 + 512],
                            start=(ic == 0), stop=(ic == NI - 1)), pe if ic == NI - 1 else None)
                    wait(p, "vector", pe, pe.n)
                    if e < 0:
                        yadd.append(run(p, "vector", lambda e2, tt=tt, dc=dc, bank=bank: e2.tensor_copy(
                            YACC[:, tt * D + dc * 512:tt * D + (dc + 1) * 512], PY[bank][:, :]), va))
                    else:
                        wait(p, "vector", va, wev[tt])
                        yadd.append(run(p, "vector", lambda e2, tt=tt, dc=dc, bank=bank, e=e:
                            e2.scalar_tensor_tensor(
                                YACC[:, tt * D + dc * 512:tt * D + (dc + 1) * 512],
                                PY[bank][:, :], WE[:, tt * 8 + e:tt * 8 + e + 1],
                                YACC[:, tt * D + dc * 512:tt * D + (dc + 1) * 512],
                                ALU.mult, ALU.add), va))
        if e + 1 < 8:
            wait(p, "gpsimd", pe, pe.n)
            lw[e + 1] = load_expert(e + 1)
    wait(p, "sync", va, va.n)
    dma(p, "sync", st, tm_ap(y_o), YACC[:, :])
    wait(p, "sync", st, st.n)
    return p



def build_stage_l2locqkv(p=None, t=None):
    DO = 512
    p = p or Prog("l2locqkv")
    nc = p.nc
    xnfm_i = _io(nc, t, "xnfm", [D, T], BF16, "ExternalInput")
    lnw_i = _io(nc, t, "lnw", [128, 8], F32, "ExternalInput")
    lnb_i = _io(nc, t, "lnb", [128, 8], F32, "ExternalInput")
    wq_i = _io(nc, t, "wq", [D, DO], F32, "ExternalInput")
    wk_i = _io(nc, t, "wk", [D, DO], F32, "ExternalInput")
    wv_i = _io(nc, t, "wv", [D, DO], F32, "ExternalInput")
    ql_o = _io(nc, t, "ql", [DO, T], BF16, "ExternalOutput")
    kl_o = _io(nc, t, "kl", [DO, T], BF16, "ExternalOutput")
    vem = t.get("vemode") if t else None
    if vem is None:
        vl_o = _io(nc, t, "vl", [T, DO], BF16, "ExternalOutput")

    XF = p.sbuf([128, NC * T], BF16, "XF")
    QF = p.sbuf([128, 4 * T], BF16, "QF")
    LNW = p.sbuf([128, 8], F32, "LNW")
    LNB = p.sbuf([128, 8], F32, "LNB")
    VT = p.sbuf([128, 2 * 512], BF16, "VT")
    PSB = [p.psum([128, 512], F32, f"pp{i}") for i in range(4)]
    lda = p.sem("lda"); va = p.sem("va"); ac = p.sem("ac"); pe = p.sem("pe")
    ev = p.sem("ev"); st = p.psem("st")
    stv = [p.psem("stv0"), p.psem("stv1")]
    sems = dict(va=va, ac=ac, pe=pe, ev=ev, psumpool=PSB)

    dma(p, "sync", lda, XF[:, :], fm_ap(xnfm_i))
    dma(p, "sync", lda, LNW[:, :], lnw_i)
    dma(p, "sync", lda, LNB[:, :], lnb_i)
    LDA = lda.n
    WQ, wsq = load_w_fm(p, None, wq_i, D, DO, "WQ")
    WK, wsk = load_w_fm(p, None, wk_i, D, DO, "WK")
    WV, wsv = load_w_fm(p, None, wv_i, D, DO, "WV")

    wait(p, "vector", lda, LDA)
    zt = None
    for c in range(NC):
        zt = run(p, "vector", lambda e, c=c: e.tensor_scalar(
            XF[:, c * T:(c + 1) * T], XF[:, c * T:(c + 1) * T],
            LNW[:, c:c + 1], LNB[:, c:c + 1], ALU.mult, ALU.add), va)
    wait(p, "tensor", va, zt)
    wait(p, "scalar", va, zt)
    qo = emit_proj_fm(p, sems, XF, WQ, QF, DO, pre=[(wsq, wsq.n)])
    wait(p, "sync", ev, qo[-1])
    dma(p, "sync", st, ql_o.rearrange("(c p) n -> p c n", p=128), QF[:, 0:4 * T])
    def kevw(k):
        wait(p, "vector", st, 16)
    ko = emit_proj_fm(p, sems, XF, WK, QF, DO, pre=[(wsk, wsk.n)], evac_wait=kevw)
    wait(p, "sync", ev, ko[-1])
    dma(p, "sync", st, kl_o.rearrange("(c p) n -> p c n", p=128), QF[:, 0:4 * T])
    vev = []
    for tt in range(NT):
        bank = tt % 2
        wait(p, "tensor", wsv, wsv.n)
        wait(p, "tensor", va, zt)
        if tt >= 2:
            wait(p, "tensor", ev, vev[tt - 2])
        for ci in range(NC):
            run(p, "tensor", lambda e, ci=ci, tt=tt, bank=bank:
                e.matmul(PSB[bank][:, :],
                         XF[:, ci * T + tt * 128:ci * T + (tt + 1) * 128],
                         WV[:, ci * DO:ci * DO + DO],
                         start=(ci == 0), stop=(ci == NC - 1)),
                pe if ci == NC - 1 else None)
        wait(p, "vector", pe, pe.n)
        if tt >= 2:
            wait(p, "vector", stv[tt % 2], 16 * (tt // 2))
        vev.append(run(p, "vector", lambda e, tt=tt, bank=bank: e.tensor_copy(
            VT[:, (tt % 2) * 512:(tt % 2) * 512 + 512], PSB[bank][:, :]), ev))
        wait(p, "sync", ev, vev[-1])
        if vem is None:
            dma(p, "sync", stv[tt % 2], vl_o[tt * 128:(tt + 1) * 128, :],
                VT[:, (tt % 2) * 512:(tt % 2) * 512 + 512])
        else:
            dma(p, "sync", stv[tt % 2],
                bass.AP(vem[1], tt * 128 * 520, [[520, 128], [65, 8], [1, 64]]),
                VT[:, (tt % 2) * 512:(tt % 2) * 512 + 512])
    wait(p, "sync", st, st.n)
    wait(p, "sync", stv[0], stv[0].n)
    wait(p, "sync", stv[1], stv[1].n)
    return p


def build_stage_l2locattn(p=None, t=None):
    NH = 8
    p = p or Prog("l2locattn")
    nc = p.nc
    qlag = t.get("ql_ag") if t else None
    if qlag is None:
        ql_i = _io(nc, t, "ql", [512, S], BF16, "ExternalInput")
        klf_i = _io(nc, t, "klf", [512, S], BF16, "ExternalInput")
    else:
        klag = t["kl_ag"]
    vlef_i = _io(nc, t, "vlef", [S, NH * 65], BF16, "ExternalInput")
    wol_i = _io(nc, t, "wol", [512, D], F32, "ExternalInput")
    lmask_i = _io(nc, t, "lmask", [128, 2 * 128], F32, "ExternalInput")
    li_o = _io(nc, t, "light", [S, D], F32, "ExternalOutput")

    QL = p.sbuf([128, 4 * S], BF16, "QL")
    KL = p.sbuf([128, 4 * S], BF16, "KL")
    VLE = p.sbuf([128, (S // 128) * NH * 65], BF16, "VLE")
    LM = p.sbuf([128, 2 * 128], F32, "LM")
    SCR = p.sbuf([128, 128], F32, "SCR")
    EXPT = p.sbuf([128, 2 * 128], BF16, "EXPT")
    AFM = p.sbuf([65, 128], F32, "AFM")
    RDS = p.sbuf([128, 128], F32, "RDS")
    ONEB = p.sbuf([128, 64], F32, "ONEB")
    CB20 = p.sbuf([128, 1], F32, "CB20")
    AO = p.sbuf([128, 4 * S], BF16, "AO")
    YS = p.sbuf([128, 2 * 512], F32, "YS")
    PSC = [p.psum([128, 128], F32, f"psc{i}") for i in range(2)]
    PAV = p.psum([65, 128], F32, "pav")
    PBC = p.psum([64, 128], F32, "pbc")
    PY = [p.psum([128, 512], F32, f"py{i}") for i in range(2)]

    lda = p.sem("lda"); va = p.sem("va"); ac = p.sem("ac"); pe = p.sem("pe")
    ev = p.sem("ev"); st = p.psem("st")
    sty = [p.psem("sty0"), p.psem("sty1")]

    if qlag is None:
        dma(p, "sync", lda, QL[:, :], fm_ap(ql_i))
        dma(p, "sync", lda, KL[:, :], fm_ap(klf_i))
    else:
        for hc in range(4):
            for gg in range(2):
                dma(p, "sync", lda, QL[:, hc * S + gg * T:hc * S + gg * T + T],
                    qlag[gg * 512 + hc * 128:gg * 512 + hc * 128 + 128, :])
                dma(p, "sync", lda, KL[:, hc * S + gg * T:hc * S + gg * T + T],
                    klag[gg * 512 + hc * 128:gg * 512 + hc * 128 + 128, :])
    dma(p, "sync", lda, VLE[:, :], vlef_i.rearrange("(r p) n -> p r n", p=128))
    dma(p, "sync", lda, LM[:, :], lmask_i)
    LDA = lda.n
    WO, wso = load_w_fm(p, None, wol_i, 512, D, "WOL")
    run(p, "vector", lambda e: e.memset(CB20[:, :], -20.0))
    t_ones = run(p, "vector", lambda e: e.memset(ONEB[:, :], 1.0), va)
    wait(p, "scalar", va, t_ones)

    aov = []
    for h in range(NH):
        hc = h // 2
        hb = (h % 2) * 64
        for a in range(32):
            ag = a
            kts = [k for k in (ag - 1, ag, ag + 1) if 0 <= k < S // 128]
            expn = []
            pvn = []
            for j, kt in enumerate(kts):
                bank = j % 2
                wait(p, "tensor", lda, LDA)
                if len(expn) >= 2:
                    wait(p, "tensor", ac, expn[-2])
                run(p, "tensor", lambda e, hc=hc, hb=hb, kt=kt, a=a, bank=bank:
                    e.matmul(PSC[bank][:, :],
                             KL[hb:hb + 64, hc * S + kt * 128:hc * S + (kt + 1) * 128],
                             QL[hb:hb + 64, hc * S + a * 128:hc * S + (a + 1) * 128],
                             start=True, stop=True), pe)
                if len(pvn) >= 2:
                    wait(p, "scalar", pe, pvn[-2])
                dj = kt - ag
                if dj == 0:
                    wait(p, "scalar", pe, pe.n)
                    expn.append(run(p, "scalar", lambda e, j=j, bank=bank: e.activation(
                        EXPT[:, (j % 2) * 128:(j % 2) * 128 + 128], PSC[bank][:, :], AF.Exp,
                        scale=0.125, bias=CB20[:, 0:1]), ac))
                else:
                    mo = 0 if dj == -1 else 128
                    wait(p, "vector", pe, pe.n)
                    if expn:
                        wait(p, "vector", ac, expn[-1])
                    tvm = run(p, "vector", lambda e, bank=bank, mo=mo: e.tensor_add(
                        SCR[:, :], PSC[bank][:, :], LM[:, mo:mo + 128]), va)
                    wait(p, "scalar", va, tvm)
                    expn.append(run(p, "scalar", lambda e, j=j: e.activation(
                        EXPT[:, (j % 2) * 128:(j % 2) * 128 + 128], SCR[:, :], AF.Exp,
                        scale=0.125, bias=CB20[:, 0:1]), ac))
                wait(p, "tensor", ac, expn[-1])
                if j == 0:
                    wait(p, "tensor", ev, ev.n)
                pvn.append(run(p, "tensor", lambda e, kt=kt, h=h, j=j, nk=len(kts): e.matmul(
                    PAV[:, :], VLE[:, kt * NH * 65 + h * 65:kt * NH * 65 + (h + 1) * 65],
                    EXPT[:, (j % 2) * 128:(j % 2) * 128 + 128],
                    start=(j == 0), stop=(j == nk - 1)), pe))
            wait(p, "vector", pe, pvn[-1])
            if aov:
                wait(p, "vector", va, aov[-1])
            te = run(p, "vector", lambda e: e.tensor_copy(AFM[:, :], PAV[:, :]), ev)
            wait(p, "vector", ev, te)
            tr = run(p, "vector", lambda e: e.reciprocal(RDS[64:65, :], AFM[64:65, :]), va)
            wait(p, "tensor", va, tr)
            if aov:
                wait(p, "tensor", va, aov[-1])
            tb = run(p, "tensor", lambda e: e.matmul(
                PBC[:, :], ONEB[64:65, 0:64], RDS[64:65, :], start=True, stop=True), pe)
            wait(p, "vector", pe, tb)
            aov.append(run(p, "vector", lambda e, h=h, a=a: e.tensor_mul(
                AO[(h % 2) * 64:(h % 2) * 64 + 64,
                   (h // 2) * S + a * 128:(h // 2) * S + (a + 1) * 128],
                AFM[0:64, :], PBC[:, :]), va))
    yv = []
    wait(p, "tensor", va, aov[-1])
    for tt in range(S // 128):
        for dc in range(2):
            k = tt * 2 + dc
            bank = k % 2
            wait(p, "tensor", wso, wso.n)
            if k >= 2:
                wait(p, "tensor", ev, yv[k - 2])
            for ci in range(4):
                run(p, "tensor", lambda e, ci=ci, tt=tt, dc=dc, bank=bank:
                    e.matmul(PY[bank][:, :],
                             AO[:, ci * S + tt * 128:ci * S + (tt + 1) * 128],
                             WO[:, ci * D + dc * 512:ci * D + dc * 512 + 512],
                             start=(ci == 0), stop=(ci == 3)),
                    pe if ci == 3 else None)
            wait(p, "vector", pe, pe.n)
            if k >= 2:
                wait(p, "vector", sty[k % 2], 16 * (k // 2))
            yv.append(run(p, "vector", lambda e, k=k, bank=bank: e.tensor_copy(
                YS[:, (k % 2) * 512:(k % 2) * 512 + 512], PY[bank][:, :]), ev))
            wait(p, "sync", ev, yv[-1])
            dma(p, "sync", sty[k % 2], li_o[tt * 128:(tt + 1) * 128, dc * 512:dc * 512 + 512],
                YS[:, (k % 2) * 512:(k % 2) * 512 + 512])
    wait(p, "sync", sty[0], sty[0].n)
    wait(p, "sync", sty[1], sty[1].n)
    return p


def build_stage_l2hkv(p=None, t=None):
    DO = 512
    p = p or Prog("l2hkv")
    nc = p.nc
    xnfm_i = _io(nc, t, "xnfm", [D, T], BF16, "ExternalInput")
    gam_i = _io(nc, t, "gam", [128, 8], F32, "ExternalInput")
    c2_i = _io(nc, t, "c2", [128, NT], F32, "ExternalInput")
    rt_i = _io(nc, t, "rt", [D, 2], F32, "ExternalInput")
    wq_i = _io(nc, t, "wq", [D, DO], F32, "ExternalInput")
    wk_i = _io(nc, t, "wk", [D, DO], F32, "ExternalInput")
    wv_i = _io(nc, t, "wv", [D, DO], F32, "ExternalInput")
    qh_o = _io(nc, t, "qh", [DO, T], BF16, "ExternalOutput")
    kh_o = _io(nc, t, "kh", [DO, T], BF16, "ExternalOutput")
    vem = t.get("vemode") if t else None
    if vem is None:
        vh_o = _io(nc, t, "vh", [T, DO], BF16, "ExternalOutput")
    s_o = _io(nc, t, "s", [128, 2 * NT], F32, "ExternalOutput")

    XF = p.sbuf([128, NC * T], BF16, "XF")
    RM = p.sbuf([128, NC * T], BF16, "RM")
    QF = p.sbuf([128, 4 * T], BF16, "QF")
    GAM = p.sbuf([128, 8], F32, "GAM")
    C2C = p.sbuf([128, NT], F32, "C2C")
    RC2 = p.sbuf([128, NT], F32, "RC2")
    RT = p.sbuf([128, NC * 2], BF16, "RT")
    SS = p.sbuf([128, 2 * NT], F32, "SS")
    VT = p.sbuf([128, 2 * 512], BF16, "VT")
    PSB = [p.psum([128, 512], F32, f"pp{i}") for i in range(4)]
    PRT = p.psum([128, 2], F32, "prt")
    lda = p.sem("lda"); gl = p.sem("gl"); va = p.sem("va"); ac = p.sem("ac")
    pe = p.sem("pe"); ev = p.sem("ev"); st = p.psem("st")
    stv = [p.psem("stv0"), p.psem("stv1")]
    sems = dict(va=va, ac=ac, pe=pe, ev=ev, psumpool=PSB)

    dma(p, "sync", lda, XF[:, :], fm_ap(xnfm_i))
    dma(p, "sync", lda, GAM[:, :], gam_i)
    dma(p, "sync", lda, C2C[:, :], c2_i)
    LDA = lda.n
    dma(p, "gpsimd", gl, RT[:, :], fm_ap(rt_i))
    GL = gl.n
    WQ, wsq = load_w_fm(p, None, wq_i, D, DO, "WQ")
    WK, wsk = load_w_fm(p, None, wk_i, D, DO, "WK")
    WV, wsv = load_w_fm(p, None, wv_i, D, DO, "WV")

    wait(p, "vector", lda, LDA)
    trc = run(p, "vector", lambda e: e.reciprocal(RC2[:, :], C2C[:, :]), va)
    sev = []
    for tt in range(NT):
        wait(p, "tensor", lda, LDA)
        wait(p, "tensor", gl, GL)
        if sev:
            wait(p, "tensor", va, sev[-1])
        for ci in range(NC):
            run(p, "tensor", lambda e, ci=ci, tt=tt: e.matmul(
                PRT[:, :], XF[:, ci * T + tt * 128:ci * T + (tt + 1) * 128],
                RT[:, ci * 2:(ci + 1) * 2], start=(ci == 0), stop=(ci == NC - 1)),
                pe if ci == NC - 1 else None)
        wait(p, "vector", pe, pe.n)
        wait(p, "vector", va, trc)
        sev.append(run(p, "vector", lambda e, tt=tt: e.tensor_scalar(
            SS[:, 2 * tt:2 * tt + 2], PRT[:, :], RC2[:, tt:tt + 1], None, ALU.mult), va))
    wait(p, "sync", va, sev[-1])
    dma(p, "sync", st, s_o, SS[:, :])

    rmv = None
    for c in range(NC):
        wait(p, "vector", lda, LDA)
        rmv = run(p, "vector", lambda e, c=c: e.tensor_scalar(
            RM[:, c * T:(c + 1) * T], XF[:, c * T:(c + 1) * T],
            GAM[:, c:c + 1], None, ALU.mult), va)
    wait(p, "tensor", va, rmv)
    qo = emit_proj_fm(p, sems, RM, WQ, QF, DO, pre=[(wsq, wsq.n)])
    wait(p, "sync", ev, qo[-1])
    dma(p, "sync", st, qh_o.rearrange("(c p) n -> p c n", p=128), QF[:, 0:4 * T])
    def kevw(k):
        wait(p, "vector", st, 32)
    ko = emit_proj_fm(p, sems, RM, WK, QF, DO, pre=[(wsk, wsk.n)], evac_wait=kevw)
    wait(p, "sync", ev, ko[-1])
    dma(p, "sync", st, kh_o.rearrange("(c p) n -> p c n", p=128), QF[:, 0:4 * T])
    vev = []
    for tt in range(NT):
        bank = tt % 2
        wait(p, "tensor", wsv, wsv.n)
        wait(p, "tensor", va, rmv)
        if tt >= 2:
            wait(p, "tensor", ev, vev[tt - 2])
        for ci in range(NC):
            run(p, "tensor", lambda e, ci=ci, tt=tt, bank=bank:
                e.matmul(PSB[bank][:, :],
                         RM[:, ci * T + tt * 128:ci * T + (tt + 1) * 128],
                         WV[:, ci * DO:ci * DO + DO],
                         start=(ci == 0), stop=(ci == NC - 1)),
                pe if ci == NC - 1 else None)
        wait(p, "vector", pe, pe.n)
        if tt >= 2:
            wait(p, "vector", stv[tt % 2], 16 * (tt // 2))
        vev.append(run(p, "vector", lambda e, tt=tt, bank=bank: e.tensor_copy(
            VT[:, (tt % 2) * 512:(tt % 2) * 512 + 512], PSB[bank][:, :]), ev))
        wait(p, "sync", ev, vev[-1])
        if vem is None:
            dma(p, "sync", stv[tt % 2], vh_o[tt * 128:(tt + 1) * 128, :],
                VT[:, (tt % 2) * 512:(tt % 2) * 512 + 512])
        else:
            dma(p, "sync", stv[tt % 2],
                bass.AP(vem[1], tt * 128 * 520, [[520, 128], [65, 8], [1, 64]]),
                VT[:, (tt % 2) * 512:(tt % 2) * 512 + 512])
    wait(p, "sync", st, st.n)
    wait(p, "sync", stv[0], stv[0].n)
    wait(p, "sync", stv[1], stv[1].n)
    return p



def build_stage_l2heavy(p=None, t=None):
    NH = 8
    SN = S + 128
    NKT = SN // 128
    p = p or Prog("l2heavy")
    nc = p.nc
    fzh = t is not None and "kh_ag" in t
    qh_i = _io(nc, t, "qh", [512, T], BF16, "ExternalInput")
    if fzh:
        khag = t["kh_ag"]
        vh_ag = t["vh_ag"]
        s_ag = t["s_ag"]
        nk_i = t["nk"]
        nv_i = t["nv"]
        lof_i = t["lof"]
        idf_i = t["identf"]
        mrow_h = t["mrow_h"]
        light_i = t["light"]
    else:
        khf_i = _io(nc, t, "khf", [512, SN], BF16, "ExternalInput")
        vhef_i = _io(nc, t, "vhef", [SN, NH * 65], BF16, "ExternalInput")
        sbis_i = _io(nc, t, "sbis", [128, 64], F32, "ExternalInput")
        skvrow_i = _io(nc, t, "skvrow", [1, S], F32, "ExternalInput")
        light_i = _io(nc, t, "light", [T, D], F32, "ExternalInput")
    sqtm_i = _io(nc, t, "sqtm", [128, NT], F32, "ExternalInput")
    kcon_i = _io(nc, t, "kcon", [1, 2], F32, "ExternalInput")
    woh_i = _io(nc, t, "woh", [512, D], F32, "ExternalInput")
    nqb_i = _io(nc, t, "nqb", [128, D], F32, "ExternalInput")
    y_o = _io(nc, t, "y2f", [T, D], F32, "ExternalOutput")

    QH = p.sbuf([128, 4 * T], BF16, "QH")
    SB = p.sbuf([128, 64], F32, "SB")
    if fzh:
        MKF = p.sbuf([128, 32], F32, "MKF")
        MKT = p.sbuf([32, 128], F32, "MKT")
        NKB = p.sbuf([128, 4], BF16, "NKB")
        LOF = p.sbuf([128, NT], mybir.dt.int32, "LOF")
        IDF = p.sbuf([128, 128], F32, "IDF")
    else:
        SKR = p.sbuf([1, S], F32, "SKR")
    SQTM = p.sbuf([128, NT], F32, "SQTM")
    KCON = p.sbuf([1, 2], F32, "KCON")
    TAU = p.sbuf([1, 2], F32, "TAU")
    LO = p.sbuf([1, 2], F32, "LO")
    HI = p.sbuf([1, 2], F32, "HI")
    LO2 = p.sbuf([1, 2], F32, "LO2")
    HI2 = p.sbuf([1, 2], F32, "HI2")
    CNT = p.sbuf([1, 2], F32, "CNT")
    CONDU = p.sbuf([1, 2], mybir.dt.uint8, "CONDU")
    TAUB = p.sbuf([128, 2], F32, "TAUB")
    CMP = p.sbuf([128, 64], F32, "CMP")
    CN12 = p.sbuf([128, 2], F32, "CN12")
    ONES1 = p.sbuf([1, 128], F32, "ONES1")
    ONESF = p.sbuf([128, 1], F32, "ONESF")
    if not fzh:
        MKV = p.sbuf([1, SN], BF16, "MKV")
        MKS = p.sbuf([1, SN], F32, "MKS")
    M01 = p.sbuf([128, NT], F32, "M01")
    QE = p.sbuf([65, T], BF16, "QE")
    KE = p.sbuf([65, 2 * SN], BF16, "KE")
    VHE = p.sbuf([128, NKT * NH * 65], BF16, "VHE")
    EXPT = p.sbuf([128, 2 * 512], BF16, "EXPT")
    AFM = p.sbuf([65, 512], F32, "AFM")
    RDS = p.sbuf([128, 512], F32, "RDS")
    ONEB = p.sbuf([128, 64], F32, "ONEB")
    CB20 = p.sbuf([128, 1], F32, "CB20")
    AO = p.sbuf([128, 4 * T], BF16, "AO")
    NQB = p.sbuf([128, D], F32, "NQB")
    LIT = p.sbuf([128, 2 * D], F32, "LIT")
    HV = p.sbuf([128, 512], F32, "HV")
    YS = p.sbuf([128, 2 * 512], F32, "YS")
    PSC = [p.psum([128, 512], F32, f"psc{i}") for i in range(2)]
    PAV = p.psum([65, 512], F32, "pav")
    PBB = p.psum([128, 2], F32, "pbb")
    PCN = p.psum([1, 2], F32, "pcn")
    PBC = p.psum([64, 512], F32, "pbc")
    PY = [p.psum([128, 512], F32, f"py{i}") for i in range(2)]

    lda = p.sem("lda"); gl = p.sem("gl"); va = p.sem("va"); ac = p.sem("ac")
    pe = p.sem("pe"); ev = p.sem("ev")
    hl = [p.sem("hl0"), p.sem("hl1")]
    sty = [p.psem("sty0"), p.psem("sty1")]
    ldl = [p.sem("ldl0"), p.sem("ldl1")]
    msk = p.sem("msk")

    dma(p, "sync", lda, QH[:, :], fm_ap(qh_i))
    if fzh:
        dma(p, "sync", lda, VHE[:, 0:32 * NH * 65],
            vh_ag.rearrange("(r p) n -> p r n", p=128))
        dma(p, "sync", lda, SB[:, 0:16], s_ag[0:128, 0:NT])
        dma(p, "sync", lda, SB[:, 16:32], s_ag[128:256, 0:NT])
        dma(p, "sync", lda, SB[:, 32:48], s_ag[0:128, NT:2 * NT])
        dma(p, "sync", lda, SB[:, 48:64], s_ag[128:256, NT:2 * NT])
        dma(p, "sync", lda, LOF[:, :], lof_i)
        dma(p, "sync", lda, IDF[:, :], idf_i)
    else:
        dma(p, "sync", lda, VHE[:, :], vhef_i.rearrange("(r p) n -> p r n", p=128))
        dma(p, "sync", lda, SB[:, :], sbis_i)
        dma(p, "sync", lda, SKR[:, :], skvrow_i)
    dma(p, "sync", lda, SQTM[:, :], sqtm_i)
    dma(p, "sync", lda, KCON[:, :], kcon_i)
    dma(p, "sync", lda, NQB[:, :], nqb_i)
    LDA = lda.n
    if fzh:
        dma(p, "gpsimd", gl, NKB[:, :], nk_i)
        GLNK = gl.n
        tvz0 = run(p, "vector", lambda e: e.memset(VHE[:, 32 * NH * 65:33 * NH * 65], 0.0), va)
        wait(p, "gpsimd", va, tvz0)
        dma(p, "gpsimd", gl, VHE[0:1, 32 * NH * 65:33 * NH * 65], nv_i)
        GLNV = gl.n
        wait(p, "sync", gl, GLNV)
        run(p, "vector", lambda e: e.memset(KE[0:64, S + 1:SN], 0.0))
        run(p, "vector", lambda e: e.memset(KE[0:64, SN + S + 1:2 * SN], 0.0), va)
    WO, wso = load_w_fm(p, None, woh_i, 512, D, "WOH")
    run(p, "vector", lambda e: e.memset(ONES1[:, :], 1.0))
    run(p, "vector", lambda e: e.memset(ONESF[:, :], 1.0))
    run(p, "vector", lambda e: e.memset(CB20[:, :], -20.0))
    tob_ = run(p, "vector", lambda e: e.memset(ONEB[:, :], 1.0), va)
    wait(p, "scalar", va, tob_)
    run(p, "vector", lambda e: e.memset(LO[:, :], -1100.0))
    run(p, "vector", lambda e: e.memset(HI[:, :], 1100.0))
    t0 = run(p, "vector", lambda e: e.memset(TAU[:, :], 0.0), va)

    wait(p, "vector", lda, LDA)
    wait(p, "tensor", lda, LDA)
    for it in range(30):
        wait(p, "tensor", va, t0)
        tb = run(p, "tensor", lambda e: e.matmul(
            PBB[:, :], ONES1[0:1, 0:128], TAU[:, :], start=True, stop=True), pe)
        wait(p, "vector", pe, tb)
        t1 = run(p, "vector", lambda e: e.tensor_copy(TAUB[:, :], PBB[:, :]), va)
        wait(p, "vector", va, t1)
        run(p, "vector", lambda e: e.tensor_scalar(
            CMP[:, 0:32], SB[:, 0:32], TAUB[:, 0:1], None, ALU.is_ge))
        t2 = run(p, "vector", lambda e: e.tensor_scalar(
            CMP[:, 32:64], SB[:, 32:64], TAUB[:, 1:2], None, ALU.is_ge), va)
        wait(p, "vector", va, t2)
        run(p, "vector", lambda e: e.reduce_sum(CN12[:, 0:1], CMP[:, 0:32], axis=AX))
        t3 = run(p, "vector", lambda e: e.reduce_sum(CN12[:, 1:2], CMP[:, 32:64], axis=AX), va)
        wait(p, "tensor", va, t3)
        tc_ = run(p, "tensor", lambda e: e.matmul(
            PCN[:, :], ONESF[:, 0:1], CN12[:, :], start=True, stop=True), pe)
        wait(p, "vector", pe, tc_)
        t4 = run(p, "vector", lambda e: e.tensor_copy(CNT[:, :], PCN[:, :]), va)
        wait(p, "vector", va, t4)
        t5 = run(p, "vector", lambda e: e.tensor_tensor(
            CONDU[:, :], CNT[:, :], KCON[:, :], ALU.is_ge), va)
        wait(p, "vector", va, t5)
        t6 = run(p, "vector", lambda e: e.select(LO2[:, :], CONDU[:, :], TAU[:, :], LO[:, :], add_drain=True), va)
        wait(p, "vector", va, t6)
        t7 = run(p, "vector", lambda e: e.select(HI2[:, :], CONDU[:, :], HI[:, :], TAU[:, :], add_drain=True), va)
        wait(p, "vector", va, t7)
        tcl = run(p, "vector", lambda e: e.tensor_copy(LO[:, :], LO2[:, :]), va)
        wait(p, "vector", va, tcl)
        t8 = run(p, "vector", lambda e: e.tensor_copy(HI[:, :], HI2[:, :]), va)
        wait(p, "vector", va, t8)
        t9 = run(p, "vector", lambda e: e.tensor_add(TAU[:, :], LO[:, :], HI[:, :]), va)
        wait(p, "vector", va, t9)
        t0 = run(p, "vector", lambda e: e.tensor_scalar_mul(TAU[:, :], TAU[:, :], 0.5), va)
        wait(p, "vector", va, t0)
    wait(p, "tensor", va, t0)
    tb2 = run(p, "tensor", lambda e: e.matmul(
        PBB[:, :], ONES1[0:1, 0:128], LO[:, :], start=True, stop=True), pe)
    wait(p, "vector", pe, tb2)
    tl1 = run(p, "vector", lambda e: e.tensor_copy(TAUB[:, :], PBB[:, :]), va)
    wait(p, "vector", va, tl1)
    tm4 = run(p, "vector", lambda e: e.tensor_scalar(
        M01[:, :], SQTM[:, :], TAUB[:, 0:1], None, ALU.is_ge), va)
    if fzh:
        wait(p, "vector", va, tm4)
        m1 = run(p, "vector", lambda e: e.tensor_scalar(
            MKF[:, :], SB[:, 32:64], TAUB[:, 1:2], None, ALU.is_ge), va)
        wait(p, "vector", va, m1)
        m2 = run(p, "vector", lambda e: e.tensor_scalar(
            MKF[:, :], MKF[:, :], 1.0, 1e9, ALU.subtract, ALU.mult), va)
        wait(p, "tensor", va, m2)
        wait(p, "tensor", lda, LDA)
        wait(p, "tensor", gl, GLNV)
        tt_ = run(p, "tensor", lambda e: e.transpose(PY[0][0:32, 0:128], MKF[:, :], IDF[:, :]), pe)
        wait(p, "vector", pe, tt_)
        t3_ = run(p, "vector", lambda e: e.tensor_copy(MKT[:, :], PY[0][0:32, 0:128]), ev)
        wait(p, "sync", ev, t3_)
        dma(p, "sync", msk, bass.AP(mrow_h, 0, [[128, 32], [1, 128]]), MKT[:, :])
        wait(p, "gpsimd", msk, msk.n)
        mrow_ap = mrow_h.ap()
        dma(p, "gpsimd", gl, KE[64:65, 0:S], mrow_ap)
        dma(p, "gpsimd", gl, KE[64:65, SN:SN + S], mrow_ap)
        GKE = gl.n
        wait(p, "tensor", gl, GKE)
        run(p, "vector", lambda e: e.memset(KE[64:65, S:S + 1], 0.0))
        run(p, "vector", lambda e: e.memset(KE[64:65, S + 1:SN], -1e9))
        run(p, "vector", lambda e: e.memset(KE[64:65, SN + S:SN + S + 1], 0.0))
        run(p, "vector", lambda e: e.memset(KE[64:65, SN + S + 1:2 * SN], -1e9), va)
        MSK = msk.n
    else:
        wait(p, "vector", va, t0)
        tm1 = run(p, "vector", lambda e: e.tensor_scalar(
            MKS[0:1, 0:S], SKR[:, :], LO[0:1, 1:2], None, ALU.is_ge), va)
        wait(p, "vector", va, tm1)
        tma = run(p, "vector", lambda e: e.memset(MKS[0:1, S:SN], 0.0), va)
        wait(p, "vector", va, tma)
        tm2 = run(p, "vector", lambda e: e.memset(MKS[0:1, S:S + 1], 1.0), va)
        wait(p, "vector", va, tm2)
        tm3 = run(p, "vector", lambda e: e.tensor_scalar(
            MKV[0:1, :], MKS[0:1, :], 1.0, 1e9, ALU.subtract, ALU.mult), va)
        wait(p, "sync", va, tm3)
        dma(p, "sync", msk, KE[64:65, 0:SN], MKV[0:1, :])
        dma(p, "sync", msk, KE[64:65, SN:2 * SN], MKV[0:1, :])
        MSK = msk.n

    t_hl = {}
    def load_head(h):
        bok = (h % 2) * SN
        if fzh:
            dma(p, "sync", hl[h % 2], KE[0:64, bok:bok + T], khag[h * 64:h * 64 + 64, :])
            dma(p, "sync", hl[h % 2], KE[0:64, bok + T:bok + S],
                khag[512 + h * 64:512 + h * 64 + 64, :])
            t_hl[h] = dma(p, "sync", hl[h % 2], KE[0:64, bok + S:bok + S + 1],
                          NKB[(h % 2) * 64:(h % 2) * 64 + 64, h // 2:h // 2 + 1])
        else:
            t_hl[h] = dma(p, "sync", hl[h % 2], KE[0:64, bok:bok + SN],
                          khf_i[h * 64:(h + 1) * 64, :])
    load_head(0)
    load_head(1)
    aov = []
    for h in range(NH):
        bok = (h % 2) * SN
        hc = h // 2
        hb = (h % 2) * 64
        wait(p, "sync", lda, LDA)
        wait(p, "sync", pe, pe.n)
        tqe = dma(p, "sync", hl[h % 2], QE[0:64, 0:T], QH[hb:hb + 64, hc * T:(hc + 1) * T])
        wait(p, "vector", hl[h % 2], tqe)
        tqo = run(p, "vector", lambda e: e.memset(QE[64:65, 0:T], 1.0), va)
        for qc in range(4):
            expn = []
            pvn = []
            for kt in range(NKT):
                bank = kt % 2
                wait(p, "tensor", hl[h % 2], tqe)
                wait(p, "tensor", msk, MSK)
                wait(p, "tensor", va, tqo)
                wait(p, "tensor", lda, LDA)
                if len(expn) >= 2:
                    wait(p, "tensor", ac, expn[-2])
                run(p, "tensor", lambda e, kt=kt, qc=qc, bank=bank, bok=bok:
                    e.matmul(PSC[bank][:, :],
                             KE[0:65, bok + kt * 128:bok + (kt + 1) * 128],
                             QE[0:65, qc * 512:qc * 512 + 512],
                             start=True, stop=True), pe)
                if len(pvn) >= 2:
                    wait(p, "scalar", pe, pvn[-2])
                wait(p, "scalar", pe, pe.n)
                expn.append(run(p, "scalar", lambda e, kt=kt, bank=bank: e.activation(
                    EXPT[:, (kt % 2) * 512:(kt % 2) * 512 + 512], PSC[bank][:, :], AF.Exp,
                    scale=0.125, bias=CB20[:, 0:1]), ac))
                wait(p, "tensor", ac, expn[-1])
                if kt == 0:
                    wait(p, "tensor", ev, ev.n)
                pvn.append(run(p, "tensor", lambda e, kt=kt, h=h: e.matmul(
                    PAV[:, :], VHE[:, kt * NH * 65 + h * 65:kt * NH * 65 + (h + 1) * 65],
                    EXPT[:, (kt % 2) * 512:(kt % 2) * 512 + 512],
                    start=(kt == 0), stop=(kt == NKT - 1)), pe))
            wait(p, "vector", pe, pvn[-1])
            if aov:
                wait(p, "vector", va, aov[-1])
            te = run(p, "vector", lambda e: e.tensor_copy(AFM[:, :], PAV[:, :]), ev)
            wait(p, "vector", ev, te)
            tr = run(p, "vector", lambda e: e.reciprocal(RDS[64:65, :], AFM[64:65, :]), va)
            wait(p, "tensor", va, tr)
            if aov:
                wait(p, "tensor", va, aov[-1])
            tb3 = run(p, "tensor", lambda e: e.matmul(
                PBC[:, :], ONEB[64:65, 0:64], RDS[64:65, :], start=True, stop=True), pe)
            wait(p, "vector", pe, tb3)
            aov.append(run(p, "vector", lambda e, h=h, qc=qc: e.tensor_mul(
                AO[(h % 2) * 64:(h % 2) * 64 + 64,
                   (h // 2) * T + qc * 512:(h // 2) * T + qc * 512 + 512],
                AFM[0:64, :], PBC[:, :]), va))
        if h + 2 < NH:
            wait(p, "sync", pe, pe.n)
            load_head(h + 2)
    yv = []
    t1s = []
    wait(p, "tensor", va, aov[-1])
    for tt in range(NT):
        if fzh:
            if tt == 0:
                wait(p, "gpsimd", lda, LDA)
            if tt >= 2:
                wait(p, "gpsimd", va, yv[2 * (tt - 2) + 1])
            tll = ldl[tt % 2].tick(16)
            p.on("gpsimd", lambda e, tt=tt, sm=ldl[tt % 2].h: e.indirect_dma_start(
                LIT[:, (tt % 2) * D:(tt % 2) * D + D], None, light_i,
                bass.IndirectOffsetOnAxis(LOF[:, tt:tt + 1], 0)).then_inc(sm, 16))
        else:
            if tt >= 2:
                wait(p, "sync", va, yv[2 * (tt - 2) + 1])
            tll = dma(p, "sync", ldl[tt % 2], LIT[:, (tt % 2) * D:(tt % 2) * D + D],
                      light_i[tt * 128:(tt + 1) * 128, :])
        for dc in range(2):
            k = tt * 2 + dc
            bank = k % 2
            wait(p, "tensor", wso, wso.n)
            if k >= 2:
                wait(p, "tensor", va, t1s[k - 2])
            for ci in range(4):
                run(p, "tensor", lambda e, ci=ci, tt=tt, dc=dc, bank=bank:
                    e.matmul(PY[bank][:, :],
                             AO[:, ci * T + tt * 128:ci * T + (tt + 1) * 128],
                             WO[:, ci * D + dc * 512:ci * D + dc * 512 + 512],
                             start=(ci == 0), stop=(ci == 3)),
                    pe if ci == 3 else None)
            wait(p, "vector", pe, pe.n)
            wait(p, "vector", va, tm4)
            wait(p, "vector", lda, LDA)
            wait(p, "vector", ldl[tt % 2], tll)
            if yv:
                wait(p, "vector", va, yv[-1])
            t1_ = run(p, "vector", lambda e, dc=dc, bank=bank: e.tensor_sub(
                HV[:, :], PY[bank][:, :], NQB[:, dc * 512:dc * 512 + 512]), va)
            t1s.append(t1_)
            wait(p, "vector", va, t1_)
            t2_ = run(p, "vector", lambda e, tt=tt: e.tensor_scalar(
                HV[:, :], HV[:, :], M01[:, tt:tt + 1], None, ALU.mult), va)
            wait(p, "vector", va, t2_)
            t3_ = run(p, "vector", lambda e, dc=dc: e.tensor_add(
                HV[:, :], HV[:, :], NQB[:, dc * 512:dc * 512 + 512]), va)
            wait(p, "vector", va, t3_)
            if k >= 2:
                wait(p, "vector", sty[k % 2], 16 * (k // 2))
            yv.append(run(p, "vector", lambda e, tt=tt, dc=dc, k=k: e.tensor_add(
                YS[:, (k % 2) * 512:(k % 2) * 512 + 512], HV[:, :],
                LIT[:, (tt % 2) * D + dc * 512:(tt % 2) * D + dc * 512 + 512]), va))
            wait(p, "sync", va, yv[-1])
            dma(p, "sync", sty[k % 2], y_o[tt * 128:(tt + 1) * 128, dc * 512:dc * 512 + 512],
                YS[:, (k % 2) * 512:(k % 2) * 512 + 512])
    wait(p, "sync", sty[0], sty[0].n)
    wait(p, "sync", sty[1], sty[1].n)
    return p



def build_stage_l3attn(p=None, t=None):
    NKT = S // 128
    JW = 512
    p = p or Prog("l3attn")
    nc = p.nc
    fz3 = t is not None and "qk_ag" in t
    if fz3:
        qkag = t["qk_ag"]
        kkag = t["kk_ag"]
        vag3 = t["v_ag"]
        ofq_i = t["ofq"]
        ofk_i = t["ofk"]
        ofv_i = t["ofv"]
        aocc_ap = t["aocc"]
    else:
        qr_i = _io(nc, t, "qh8", [512, S], BF16, "ExternalInput")
        krf_i = _io(nc, t, "krf8", [512, S], BF16, "ExternalInput")
        vef_i = _io(nc, t, "vef8", [S, 8 * 65], BF16, "ExternalInput")
        ao_o = _io(nc, t, "aofm", [512, S], BF16, "ExternalOutput")
    embt_i = _io(nc, t, "embt", [64, 73], F32, "ExternalInput")
    identf_i = _io(nc, t, "identf", [128, 128], F32, "ExternalInput")
    epad = nc.dram_tensor("epad", [S, JW], F32)

    QH = p.sbuf([128, 4 * S], BF16, "QH")
    VEF = p.sbuf([128, NKT * 8 * 65], BF16, "VEF")
    EMBT = p.sbuf([64, 73], F32, "EMBT")
    EMBB = p.sbuf([64, JW], BF16, "EMBB")
    DIFF = p.sbuf([64, 1], F32, "DIFF")
    E2C = p.sbuf([64, 2], BF16, "E2C")
    IDF = p.sbuf([128, 128], F32, "IDF")
    QE = p.sbuf([66, S], BF16, "QE")
    ZR = p.sbuf([1, S], BF16, "ZR")
    QROWS = p.sbuf([2, S], BF16, "QROWS")
    KE = p.sbuf([66, 2 * S], BF16, "KE")
    EPS_ = p.sbuf([128, JW], F32, "EPS_")
    SH = p.sbuf([128, 2 * 384], F32, "SH")
    EXPT = p.sbuf([128, 2 * 512], BF16, "EXPT")
    AFM = p.sbuf([65, 128], F32, "AFM")
    RDS = p.sbuf([128, 128], F32, "RDS")
    ONEB = p.sbuf([128, 64], F32, "ONEB")
    CB20 = p.sbuf([128, 1], F32, "CB20")
    AO = p.sbuf([128, 4 * S], BF16, "AO")
    PSC = [p.psum([128, 512], F32, f"psc{i}") for i in range(2)]
    PAV = p.psum([65, 128], F32, "pav")
    PBC = p.psum([64, 128], F32, "pbc")
    PEB = p.psum([128, JW], F32, "peb")

    lda = p.sem("lda"); va = p.sem("va"); ac = p.sem("ac")
    pe = p.sem("pe"); ev = p.sem("ev")
    hl = p.sem("hl"); qrw = p.sem("qrw")
    epw = p.sem("epw"); epr = [p.sem("epr0"), p.sem("epr1")]
    st = p.psem("st")

    if fz3:
        OFQ = p.sbuf([128, 16], mybir.dt.int32, "OFQ")
        OFK = p.sbuf([64, 16], mybir.dt.int32, "OFK")
        OFV = p.sbuf([128, 32], mybir.dt.int32, "OFV")
        gld = p.sem("gld")
        dma(p, "sync", lda, OFQ[:, :], ofq_i)
        dma(p, "sync", lda, OFK[:, :], ofk_i)
        dma(p, "sync", lda, OFV[:, :], ofv_i)
        wait(p, "gpsimd", lda, lda.n)
        for hc in range(4):
            for gg in range(2):
                tg = gld.tick(16)
                p.on("gpsimd", lambda e, hc=hc, gg=gg, sm=gld.h: e.indirect_dma_start(
                    QH[:, hc * S + gg * T:hc * S + gg * T + T], None, qkag,
                    bass.IndirectOffsetOnAxis(OFQ[:, hc * 2 + gg:hc * 2 + gg + 1], 0)
                ).then_inc(sm, 16))
        for kt in range(NKT):
            tg = gld.tick(16)
            p.on("gpsimd", lambda e, kt=kt, sm=gld.h: e.indirect_dma_start(
                VEF[:, kt * 8 * 65:(kt + 1) * 8 * 65], None, vag3,
                bass.IndirectOffsetOnAxis(OFV[:, kt:kt + 1], 0)).then_inc(sm, 16))
        GQV = gld.n
        wait(p, "sync", gld, GQV)
        wait(p, "tensor", gld, GQV)
    else:
        dma(p, "sync", lda, QH[:, :], fm_ap(qr_i))
        dma(p, "sync", lda, VEF[:, :], vef_i.rearrange("(r p) n -> p r n", p=128))
    dma(p, "sync", lda, EMBT[:, :], embt_i)
    dma(p, "sync", lda, IDF[:, :], identf_i)
    LDA = lda.n
    tcb = run(p, "vector", lambda e: e.memset(CB20[:, :], -20.0), va)
    wait(p, "scalar", va, tcb)
    run(p, "vector", lambda e: e.memset(ONEB[:, :], 1.0))
    run(p, "vector", lambda e: e.memset(ZR[:, :], 0.0))
    run(p, "vector", lambda e: e.memset(KE[64:66, 0:S], 1.0))
    t_ke0 = run(p, "vector", lambda e: e.memset(KE[64:66, S:2 * S], 1.0), va)
    wait(p, "sync", va, t_ke0)
    t_zr = dma(p, "sync", qrw, KE[65:66, 0:S], ZR[:, :])
    t_ke1 = t_ke0
    wait(p, "vector", lda, LDA)
    run(p, "vector", lambda e: e.memset(EMBB[:, 0:192], 0.0))
    tdf = run(p, "vector", lambda e: e.tensor_sub(
        DIFF[:, 0:1], EMBT[:, 72:73], EMBT[:, 0:1]), va)
    teb1 = run(p, "vector", lambda e: e.tensor_scalar(
        EMBB[:, 192:264], EMBT[:, 1:73], EMBT[:, 0:1], None, ALU.subtract), va)
    wait(p, "vector", va, max(tdf, teb1))
    tob = run(p, "vector", lambda e: e.memset(EMBB[:, 264:JW], 1.0), va)
    wait(p, "vector", va, tob)
    teb2 = run(p, "vector", lambda e: e.tensor_scalar(
        EMBB[:, 264:JW], EMBB[:, 264:JW], DIFF[:, 0:1], None, ALU.mult), va)
    run(p, "vector", lambda e: e.tensor_copy(E2C[:, 0:1], EMBT[:, 0:1]))
    tdfb = run(p, "vector", lambda e: e.tensor_copy(E2C[:, 1:2], DIFF[:, 0:1]), va)

    aov = []
    for h in range(8):
        hc = h // 2
        hb = (h % 2) * 64
        wait(p, "sync", pe, pe.n)
        wait(p, "sync", lda, LDA)
        wait(p, "sync", epr[0], epr[0].n)
        wait(p, "sync", epr[1], epr[1].n)
        if fz3:
            for cp in range(2):
                for gg in range(2):
                    tk0 = hl.tick(16)
                    p.on("gpsimd", lambda e, cp=cp, gg=gg, h=h, sm=hl.h: e.indirect_dma_start(
                        KE[0:64, cp * S + gg * T:cp * S + gg * T + T], None, kkag,
                        bass.IndirectOffsetOnAxis(OFK[:, h * 2 + gg:h * 2 + gg + 1], 0)
                    ).then_inc(sm, 16))
            tk1 = tk0
        else:
            tk0 = dma(p, "sync", hl, KE[0:64, 0:S], krf_i[h * 64:(h + 1) * 64, :])
            tk1 = dma(p, "sync", hl, KE[0:64, S:2 * S], krf_i[h * 64:(h + 1) * 64, :])
        tqe = dma(p, "sync", qrw, QE[0:64, 0:S], QH[hb:hb + 64, hc * S:(hc + 1) * S])
        epv = []
        for lt in range(S // 128):
            wait(p, "tensor", lda, LDA)
            wait(p, "tensor", va, teb2)
            if epv:
                wait(p, "tensor", ev, epv[-1])
            wait(p, "tensor", qrw, qrw.n)
            tm_ = run(p, "tensor", lambda e, lt=lt: e.matmul(
                PEB[:, :], QE[0:64, lt * 128:(lt + 1) * 128],
                EMBB[:, :], start=True, stop=True), pe)
            wait(p, "vector", pe, tm_)
            wait(p, "vector", epw, epw.n)
            epv.append(run(p, "vector", lambda e: e.tensor_copy(EPS_[:, :], PEB[:, :]), ev))
            wait(p, "sync", ev, epv[-1])
            dma(p, "sync", epw, epad.ap()[lt * 128:(lt + 1) * 128, :], EPS_[:, :])
        EPW = epw.n
        q2v = []
        for tc2 in range(S // 512):
            wait(p, "tensor", lda, LDA)
            wait(p, "tensor", ev, (q2v[-1] if q2v else epv[-1]))
            wait(p, "tensor", va, tdfb)
            wait(p, "tensor", qrw, qrw.n)
            tqa = run(p, "tensor", lambda e, tc2=tc2: e.matmul(
                PEB[0:2, 0:512], E2C[:, :],
                QE[0:64, tc2 * 512:tc2 * 512 + 512],
                start=True, stop=True), pe)
            wait(p, "vector", pe, tqa)
            q2v.append(run(p, "vector", lambda e, tc2=tc2: e.tensor_copy(
                QROWS[0:2, tc2 * 512:tc2 * 512 + 512], PEB[0:2, 0:512]), ev))
        wait(p, "sync", ev, q2v[-1])
        dma(p, "sync", qrw, QE[64:66, 0:S], QROWS[:, :])
        QRW = qrw.n
        for lt in range(S // 128):
            l0g = lt * 128
            shsl = (lt % 2) * 384
            wait(p, "sync", epw, EPW)
            if lt >= 2:
                wait(p, "sync", pe, pe.n)
            off = lt * 128 * JW + 127
            tsh = dma(p, "sync", epr[lt % 2], SH[:, shsl:shsl + 384],
                      bass.AP(epad, off, [[JW - 1, 128], [1, 384]]))
            expn = []
            pvn = []

            def emit_pv3(grp, expn=expn, pvn=pvn, h=h):
                wait(p, "tensor", ac, expn[grp])
                if grp == 0:
                    wait(p, "tensor", ev, ev.n)
                for j in range(4):
                    kt = grp * 4 + j
                    pvn.append(run(p, "tensor", lambda e, kt=kt, h=h, grp=grp, j=j: e.matmul(
                        PAV[:, :], VEF[:, kt * 8 * 65 + h * 65:kt * 8 * 65 + (h + 1) * 65],
                        EXPT[:, (grp % 2) * 512 + j * 128:(grp % 2) * 512 + (j + 1) * 128],
                        start=(kt == 0), stop=(kt == NKT - 1)), pe))

            for grp in range(NKT // 4):
                bank = grp % 2
                wait(p, "tensor", hl, max(tk0, tk1))
                wait(p, "tensor", qrw, QRW)
                wait(p, "tensor", va, t_ke1)
                wait(p, "tensor", lda, LDA)
                if len(expn) >= 2:
                    wait(p, "tensor", ac, expn[-2])
                for j in range(4):
                    kt = grp * 4 + j
                    r0 = kt * 128
                    d = r0 - l0g
                    band = d in (-128, 0, 128)
                    flagr = 1 if d >= 256 else 0
                    kb = S if flagr else 0
                    last = (j == 3) and not band
                    tgrp = run(p, "tensor", lambda e, kt=kt, lt=lt, bank=bank, kb=kb, j=j, last=last:
                        e.matmul(PSC[bank][:, j * 128:(j + 1) * 128],
                                 KE[0:66, kb + kt * 128:kb + (kt + 1) * 128],
                                 QE[0:66, lt * 128:(lt + 1) * 128],
                                 start=(j == 0), stop=last), pe if last else None)
                    if band:
                        bidx = (d + 128) // 128
                        wait(p, "tensor", epr[lt % 2], tsh)
                        tgrp = run(p, "tensor", lambda e, bank=bank, shsl=shsl, bidx=bidx, j=j:
                            e.matmul(PSC[bank][:, j * 128:(j + 1) * 128],
                                     SH[:, shsl + bidx * 128:shsl + (bidx + 1) * 128],
                                     IDF[:, :], is_transpose=True,
                                     start=False, stop=(j == 3)), pe if j == 3 else None)
                wait(p, "scalar", pe, tgrp)
                if pvn:
                    wait(p, "scalar", pe, pvn[-1])
                expn.append(run(p, "scalar", lambda e, grp=grp, bank=bank: e.activation(
                    EXPT[:, (grp % 2) * 512:(grp % 2) * 512 + 512], PSC[bank][:, :], AF.Exp,
                    scale=0.125, bias=CB20[:, 0:1]), ac))
                if grp >= 1:
                    emit_pv3(grp - 1)
            emit_pv3(NKT // 4 - 1)
            wait(p, "vector", pe, pvn[-1])
            if aov:
                wait(p, "vector", va, aov[-1])
            te = run(p, "vector", lambda e: e.tensor_copy(AFM[:, :], PAV[:, :]), ev)
            wait(p, "vector", ev, te)
            tr = run(p, "vector", lambda e: e.reciprocal(RDS[64:65, :], AFM[64:65, :]), va)
            wait(p, "tensor", va, tr)
            if aov:
                wait(p, "tensor", va, aov[-1])
            tb3 = run(p, "tensor", lambda e: e.matmul(
                PBC[:, :], ONEB[64:65, 0:64], RDS[64:65, :], start=True, stop=True), pe)
            wait(p, "vector", pe, tb3)
            aov.append(run(p, "vector", lambda e, h=h, lt=lt: e.tensor_mul(
                AO[(h % 2) * 64:(h % 2) * 64 + 64,
                   (h // 2) * S + lt * 128:(h // 2) * S + (lt + 1) * 128],
                AFM[0:64, :], PBC[:, :]), va))
    wait(p, "sync", va, aov[-1])
    if fz3:
        for ci in range(4):
            for gg in range(2):
                dma(p, "sync", st, aocc_ap[gg * 512 + ci * 128:gg * 512 + ci * 128 + 128, :],
                    AO[:, ci * S + gg * T:ci * S + gg * T + T])
    else:
        dma(p, "sync", st, fm_ap(ao_o), AO[:, :])
    wait(p, "sync", st, st.n)
    return p


def build_stage_l3proj(p=None, t=None):
    p = p or Prog("l3proj")
    nc = p.nc
    fzp = t is not None and "ao_ag" in t
    if fzp:
        aoag = t["ao_ag"]
        ofa_i = t["ofa"]
    else:
        af_i = _io(nc, t, "af", [D, T], BF16, "ExternalInput")
    wo_i = _io(nc, t, "wo", [D, D], F32, "ExternalInput")
    bo_i = _io(nc, t, "bo", [1, D], F32, "ExternalInput")
    ya_o = _io(nc, t, "ya", [T, D], F32, "ExternalOutput")
    AOF = p.sbuf([128, NC * T], BF16, "AOF")
    BO = p.sbuf([1, D], BF16, "BO")
    YS = p.sbuf([128, 2 * 512], F32, "YS")
    ONES1 = p.sbuf([1, 128], BF16, "ONES1")
    PY = [p.psum([128, 512], F32, f"py{i}") for i in range(2)]
    lda = p.sem("lda"); gld = p.sem("gld"); va = p.sem("va")
    pe = p.sem("pe"); ev = p.sem("ev")
    sty = [p.psem("sty0"), p.psem("sty1")]
    if fzp:
        OFA = p.sbuf([128, 8], mybir.dt.int32, "OFA")
        dma(p, "sync", lda, OFA[:, :], ofa_i)
        wait(p, "gpsimd", lda, lda.n)
        for ci in range(NC):
            gld.tick(16)
            p.on("gpsimd", lambda e, ci=ci, sm=gld.h: e.indirect_dma_start(
                AOF[:, ci * T:(ci + 1) * T], None, aoag,
                bass.IndirectOffsetOnAxis(OFA[:, ci:ci + 1], 0)).then_inc(sm, 16))
    else:
        dma(p, "sync", lda, AOF[:, :], fm_ap(af_i))
    LDA = lda.n
    dma(p, "gpsimd", gld, BO[:, :], bo_i)
    GLD = gld.n
    WO, wso = load_w_fm(p, None, wo_i, D, D, "WO")
    t_on = run(p, "vector", lambda e: e.memset(ONES1[:, :], 1.0), va)
    yv = []
    for tt in range(NT):
        for dc in range(2):
            k = tt * 2 + dc
            bank = k % 2
            wait(p, "tensor", wso, wso.n)
            wait(p, "tensor", gld, GLD)
            wait(p, "tensor", lda, LDA)
            wait(p, "tensor", va, t_on)
            if k >= 2:
                wait(p, "tensor", ev, yv[k - 2])
            for ci in range(NC):
                run(p, "tensor", lambda e, ci=ci, tt=tt, dc=dc, bank=bank:
                    e.matmul(PY[bank][:, :],
                             AOF[:, ci * T + tt * 128:ci * T + (tt + 1) * 128],
                             WO[:, ci * D + dc * 512:ci * D + dc * 512 + 512],
                             start=(ci == 0), stop=False), None)
            tv = run(p, "tensor", lambda e, dc=dc, bank=bank:
                e.matmul(PY[bank][:, :], ONES1[0:1, 0:128],
                         BO[0:1, dc * 512:dc * 512 + 512], start=False, stop=True), pe)
            wait(p, "vector", pe, tv)
            if k >= 2:
                wait(p, "vector", sty[k % 2], 16 * (k // 2))
            yv.append(run(p, "vector", lambda e, k=k, bank=bank: e.tensor_copy(
                YS[:, (k % 2) * 512:(k % 2) * 512 + 512], PY[bank][:, :]), ev))
            wait(p, "sync", ev, yv[-1])
            dma(p, "sync", sty[k % 2], ya_o[tt * 128:(tt + 1) * 128, dc * 512:dc * 512 + 512],
                YS[:, (k % 2) * 512:(k % 2) * 512 + 512])
    wait(p, "sync", sty[0], sty[0].n)
    wait(p, "sync", sty[1], sty[1].n)
    return p



def build_stage_l3ffn(TK=1024, p=None, t=None):
    I = 4096
    NI = I // 128
    NTK = TK // 128
    p = p or Prog("l3ffn")
    nc = p.nc
    xnfm_i = _io(nc, t, "xnfm", [D, TK], BF16, "ExternalInput")
    wg_i = _io(nc, t, "wg", [8, D, I], F32, "ExternalInput")
    wu_i = _io(nc, t, "wu", [8, D, I], F32, "ExternalInput")
    wd_i = _io(nc, t, "wd", [8, I, D], F32, "ExternalInput")
    y_o = _io(nc, t, "y3f", [TK, D], F32, "ExternalOutput")

    XF = p.sbuf([128, NC * TK], BF16, "XF")
    WGB = p.sbuf([128, 2 * NC * 512], BF16, "WGB")
    WUB = p.sbuf([128, 2 * NC * 512], BF16, "WUB")
    WDB = p.sbuf([128, NI * 512], BF16, "WDB")
    GS = p.sbuf([128, 512], F32, "GS")
    H = p.sbuf([128, NI * TK], BF16, "H")
    YACC = p.sbuf([128, NTK * D], F32, "YACC")
    PSA = p.psum([128, 512], F32, "psa")
    PSB2 = p.psum([128, 512], F32, "psb2")
    PY = [p.psum([128, 512], F32, f"py{i}") for i in range(2)]

    lda = p.sem("lda"); va = p.sem("va"); ac = p.sem("ac")
    pe = p.sem("pe"); ev = p.sem("ev"); st = p.psem("st")
    ldw = [p.sem("ldw0"), p.sem("ldw1")]
    ldd = [p.sem("ldd0"), p.sem("ldd1")]

    dma(p, "sync", lda, XF[:, :], fm_ap(xnfm_i))
    LDA = lda.n

    def load_gu(e, icg):
        k = (e * 8 + icg)
        s = ldw[k % 2]
        dma(p, "gpsimd", s, WGB[:, (k % 2) * NC * 512:((k % 2) + 1) * NC * 512],
            wg_i[e][:, icg * 512:(icg + 1) * 512].rearrange("(c p) n -> p c n", p=128))
        dma(p, "gpsimd", s, WUB[:, (k % 2) * NC * 512:((k % 2) + 1) * NC * 512],
            wu_i[e][:, icg * 512:(icg + 1) * 512].rearrange("(c p) n -> p c n", p=128))
        return s, s.n

    def load_wd(e, dc):
        k = e * 2 + dc
        s = ldd[k % 2]
        wait(p, "gpsimd", pe, pe.n)
        dma(p, "gpsimd", s, WDB[:, :],
            wd_i[e][:, dc * 512:(dc + 1) * 512].rearrange("(c p) n -> p c n", p=128))
        return s, s.n

    lg = load_gu(0, 0)
    yadd = []
    for e in range(8):
        hready = []
        for icg in range(8):
            s, tot = lg
            if e * 8 + icg + 1 < 64:
                nxt = (e * 8 + icg + 1)
                lg2 = None
            for ici in range(4):
                ic = icg * 4 + ici
                for tc in range(TK // 512):
                    wait(p, "tensor", s, tot)
                    wait(p, "tensor", lda, LDA)
                    if hready:
                        wait(p, "tensor", va, hready[-1])
                    if len(hready) >= 1:
                        wait(p, "tensor", ac, ac.n)
                    koff = ((e * 8 + icg) % 2) * NC * 512
                    for hcc in range(NC):
                        run(p, "tensor", lambda e2, hcc=hcc, ici=ici, tc=tc, koff=koff:
                            e2.matmul(PSA[:, :],
                                      WGB[:, koff + hcc * 512 + ici * 128:koff + hcc * 512 + (ici + 1) * 128],
                                      XF[:, hcc * TK + tc * 512:hcc * TK + tc * 512 + 512],
                                      start=(hcc == 0), stop=(hcc == NC - 1)),
                            pe if hcc == NC - 1 else None)
                    tg = pe.n
                    for hcc in range(NC):
                        run(p, "tensor", lambda e2, hcc=hcc, ici=ici, tc=tc, koff=koff:
                            e2.matmul(PSB2[:, :],
                                      WUB[:, koff + hcc * 512 + ici * 128:koff + hcc * 512 + (ici + 1) * 128],
                                      XF[:, hcc * TK + tc * 512:hcc * TK + tc * 512 + 512],
                                      start=(hcc == 0), stop=(hcc == NC - 1)),
                            pe if hcc == NC - 1 else None)
                    tu = pe.n
                    wait(p, "scalar", pe, tg)
                    tsg = run(p, "scalar", lambda e2: e2.activation(
                        GS[:, :], PSA[:, :], AF.Silu), ac)
                    wait(p, "vector", pe, tu)
                    wait(p, "vector", ac, tsg)
                    hready.append(run(p, "vector", lambda e2, ic=ic, tc=tc: e2.scalar_tensor_tensor(
                        H[:, ic * TK + tc * 512:ic * TK + tc * 512 + 512], PSB2[:, :], 1.0, GS[:, :],
                        ALU.mult, ALU.mult), va))
            if e * 8 + icg + 1 < 64:
                wait(p, "gpsimd", pe, pe.n)
                lg = load_gu((e * 8 + icg + 1) // 8, (e * 8 + icg + 1) % 8)
        for dc in range(2):
            sd, totd = load_wd(e, dc)
            for ts_ in range(NTK):
                bank = ts_ % 2
                wait(p, "tensor", sd, totd)
                wait(p, "tensor", va, hready[-1])
                if len(yadd) >= 2:
                    wait(p, "tensor", va, yadd[-2])
                for ic in range(NI):
                    run(p, "tensor", lambda e2, ic=ic, ts_=ts_, bank=bank: e2.matmul(
                        PY[bank][:, :], H[:, ic * TK + ts_ * 128:ic * TK + (ts_ + 1) * 128],
                        WDB[:, ic * 512:(ic + 1) * 512],
                        start=(ic == 0), stop=(ic == NI - 1)), pe if ic == NI - 1 else None)
                wait(p, "vector", pe, pe.n)
                if e == 0:
                    yadd.append(run(p, "vector", lambda e2, ts_=ts_, dc=dc, bank=bank: e2.tensor_copy(
                        YACC[:, ts_ * D + dc * 512:ts_ * D + (dc + 1) * 512], PY[bank][:, :]), va))
                else:
                    yadd.append(run(p, "vector", lambda e2, ts_=ts_, dc=dc, bank=bank: e2.tensor_add(
                        YACC[:, ts_ * D + dc * 512:ts_ * D + (dc + 1) * 512],
                        YACC[:, ts_ * D + dc * 512:ts_ * D + (dc + 1) * 512], PY[bank][:, :]), va))
    wait(p, "sync", va, yadd[-1])
    dma(p, "sync", st, y_o.rearrange("(t p) d -> p t d", p=128), YACC[:, :])
    wait(p, "sync", st, st.n)
    return p



def build_stage_final(p=None, t=None):
    p = p or Prog("final")
    nc = p.nc
    xin = _io(nc, t, "xin", [T, D], F32, "ExternalInput")
    ya = _io(nc, t, "ya", [T, D], F32, "ExternalInput")
    yf = _io(nc, t, "yf", [T, D], F32, "ExternalInput")
    out = _io(nc, t, "out", [T, D], F32, "ExternalOutput")
    X = p.sbuf([128, NT * D], F32, "X")
    YA = p.sbuf([128, 2 * D], F32, "YA")
    YF = p.sbuf([128, 2 * D], F32, "YF")
    lda = p.sem("lda"); va = p.sem("va"); st = p.psem("st")
    ldy = [p.sem("ldy0"), p.sem("ldy1")]
    dma(p, "sync", lda, X[:, :], tm_ap(xin))
    LDA = lda.n
    addv = []
    for t in range(NT):
        bsl = slice((t % 2) * D, (t % 2) * D + D)
        if t >= 2:
            wait(p, "sync", va, addv[t - 2])
        dma(p, "sync", ldy[t % 2], YA[:, bsl], ya[t * 128:(t + 1) * 128, :])
        ty = dma(p, "sync", ldy[t % 2], YF[:, bsl], yf[t * 128:(t + 1) * 128, :])
        wait(p, "vector", lda, LDA)
        wait(p, "vector", ldy[t % 2], ty)
        run(p, "vector", lambda e, t=t, bsl=bsl: e.tensor_add(
            X[:, t * D:(t + 1) * D], X[:, t * D:(t + 1) * D], YA[:, bsl]))
        addv.append(run(p, "vector", lambda e, t=t, bsl=bsl: e.tensor_add(
            X[:, t * D:(t + 1) * D], X[:, t * D:(t + 1) * D], YF[:, bsl]), va))
    wait(p, "sync", va, addv[-1])
    dma(p, "sync", st, tm_ap(out), X[:, :])
    wait(p, "sync", st, st.n)
    return p




_PROG_CACHE = {}

def _prog(key, builder, *args, **kw):
    if key not in _PROG_CACHE:
        nc = builder(*args, **kw).finish()
        nc._dbg_name = str(key)
        _PROG_CACHE[key] = nc
    return _PROG_CACHE[key]

def _shard_tm(x_full):
    return [np.ascontiguousarray(x_full[c // 2, (c % 2) * T:(c % 2 + 1) * T]) for c in range(NCORE)]

def _unshard_tm(parts):
    x = np.zeros((B, S, D), np.float32)
    for c in range(NCORE):
        x[c // 2, (c % 2) * T:(c % 2 + 1) * T] = parts[c]
    return x

def _ppcols(b):
    return np.ascontiguousarray(np.asarray(b, np.float32).reshape(8, 128).T)

import sys as _sys
def _run(nc, in_maps, cores=None):
    print(f"[kernel] launch {getattr(nc, '_dbg_name', '?')}", file=_sys.stderr, flush=True)
    res = run_bass_kernel_spmd(nc, in_maps, core_ids=cores or list(range(NCORE)))
    return res.results

def _l0_ropetabs(half):
    pos = np.arange(half * T, (half + 1) * T, dtype=np.float32)
    inv = 32.0 ** (-np.arange(0, D, 2, dtype=np.float32) / D)
    ang = pos[None, :] * inv[:, None]
    return np.cos(ang).astype(BF), np.sin(ang).astype(BF)

def _l1_perm():
    pm = np.zeros(1024, np.int64)
    for h in range(16):
        b0 = h * 64
        pm[b0:b0 + 32] = b0 + np.arange(0, 64, 2)
        pm[b0 + 32:b0 + 64] = b0 + np.arange(1, 64, 2)
    return pm

def _l1_ropetabs(half):
    pos = np.arange(half * T, (half + 1) * T, dtype=np.float32)
    inv = 1.0 / 10000.0 ** (np.arange(0, 64, 2, dtype=np.float32) / 64)
    ang = pos[None, :] * inv[:, None]
    c32, s32 = np.cos(ang), np.sin(ang)
    cosr = np.tile(np.concatenate([c32, c32], 0), (2, 1)).astype(BF)
    sinr = np.tile(np.concatenate([s32, s32], 0), (2, 1)).astype(BF)
    return cosr, sinr

def _l1_rotperm():
    R = np.zeros((128, 128), np.float32)
    for b0 in (0, 64):
        for j in range(32):
            R[b0 + 32 + j, b0 + j] = -1.0
            R[b0 + j, b0 + 32 + j] = 1.0
    return R.astype(BF)

def _cmask():
    m = np.zeros((128, 4, 512), np.float32)
    for j in range(4):
        for pp_ in range(128):
            m[pp_, j, :] = np.where(np.arange(512) >= pp_ + 128 * j, 0.0, -1e9)
    return np.ascontiguousarray(m.reshape(128, 2048))

def _causal_tabs(g, cm):
    bcn = np.zeros((128, 4, 32), np.float32)
    for qc in range(4):
        base = g * 16 + qc * 4
        bcn[:, qc, base + 4:] = -1e9
    smsk2 = np.zeros((128, 8, 512), np.float32)
    for j in range(4):
        smsk2[:, g * 4 + j] = cm[:, j * 512:(j + 1) * 512]
    return (np.ascontiguousarray(bcn.reshape(128, 128)),
            np.ascontiguousarray(smsk2.reshape(128, 4096)).astype(BF))

def _lmask():
    m = np.full((128, 2, 128), -1e9, np.float32)
    m[64:128, 0, 0:64] = 0.0
    m[0:64, 1, 64:128] = 0.0
    return np.ascontiguousarray(m.reshape(128, 256))

def _vef(vparts, b, nh, h0=0):
    vfull = np.concatenate([np.asarray(vparts[2 * b], np.float32),
                            np.asarray(vparts[2 * b + 1], np.float32)], 0)
    vef = np.ones((S, nh * 65), np.float32)
    for hh in range(nh):
        vef[:, hh * 65:hh * 65 + 64] = vfull[:, (h0 + hh) * 64:(h0 + hh + 1) * 64]
    return vef.astype(BF)

def _half_run(key, builder, per_core_inputs):
    outs = [None] * NCORE
    for half in (0, 1):
        nc = _prog((key, half), builder, half)
        cores = [c for c in range(NCORE) if c % 2 == half]
        res = run_bass_kernel_spmd(nc, [per_core_inputs[c] for c in cores], core_ids=cores)
        for i, c in enumerate(cores):
            outs[c] = res.results[i]
    return outs


FUSE_UPTO = 4


def build_fused(upto):
    p = Prog("mega")
    nc = p.nc
    cc = p.sem("cc")
    G2 = [[0, 1], [2, 3], [4, 5], [6, 7]]
    G8 = [list(range(8))]

    def xin_(name, shape, dt):
        return nc.dram_tensor(name, shape, dt, kind="ExternalInput").ap()

    def dram(name, shape, dt, ext=False, shared=False):
        if ext:
            return nc.dram_tensor(name, shape, dt, kind="ExternalOutput")
        if shared:
            return nc.dram_tensor(name, shape, dt, addr_space="Shared")
        return nc.dram_tensor(name, shape, dt)

    def ag(ins, outs, groups=G2):
        tcc = cc.tick(1)
        p.on("gpsimd", lambda e, i=ins, o=outs, g=groups: e.collective_compute(
            "AllGather", ALU.bypass, replica_groups=g,
            ins=[i.ap().opt()], outs=[o.ap().opt()]).then_inc(cc.h))
        wait(p, "sync", cc, tcc)
        wait(p, "gpsimd", cc, tcc)
        return tcc

    ID_BF = xin_("identbf", [128, 128], BF16)
    CCON = xin_("cconst", [128, 8], F32)

    x_in = xin_("xin", [T, D], F32)
    xn0 = dram("xn0", [T, D], F32)
    xnew0 = dram("xnew0", [T, D], F32)
    xnfm0 = dram("xnfm0", [D, T], BF16)
    p.phase("ln0")
    build_stage_ln(False, None, p=p, t=dict(
        xin=x_in, identbf=ID_BF, cconst=CCON,
        xnew=xnew0.ap(), xn=xn0.ap(), xnfm=xnfm0.ap()))

    qr0 = dram("qr0", [D, T], BF16)
    kcc0 = dram("kcc0", [D, T], BF16)
    p.phase("l0qk")
    build_stage_l0qk(p=p, t=dict(
        xnfm=xnfm0.ap(), wq=xin_("l0qk_wq", [D, D], F32), wk=xin_("l0qk_wk", [D, D], F32),
        cosf=xin_("l0qk_cosf", [512, T], BF16), sinf=xin_("l0qk_sinf", [512, T], BF16),
        qr=qr0.ap(), kr=kcc0.ap()))

    kag0 = dram("kag0", [2 * D, T], BF16)
    y0a = dram("y0a", [T, D], F32, ext=(upto == 1))
    p.phase("l0attn")
    ag(kcc0, kag0)
    build_stage_l0attn(p=p, t=dict(
        qr=qr0.ap(), krf_ag=kag0.ap(),
        fc1p=xin_("l0a_fc1p", [1, 16], F32), fc1b=xin_("l0a_fc1b", [16, 1], F32),
        fc2p=xin_("l0a_fc2p", [17, D], F32),
        bcn=(MBCN := xin_("l0a_bcn", [128, 128], F32)),
        smsk2=(MSMSK := xin_("l0a_smsk2", [128, 8 * 512], BF16)),
        y0a=y0a.ap()))

    WQI = 2816
    per_gu = [3, 3, 3, 3, 3, 3, 2, 2]
    wgt = dram("wgt", [1024, 384], BF16)
    wut = dram("wut", [1024, 384], BF16)
    wdt = dram("wdt", [WQI, 128], BF16)
    wgag = dram("wg_ag", [8 * 1024, 384], BF16, shared=True)
    wuag = dram("wu_ag", [8 * 1024, 384], BF16, shared=True)
    wdag = dram("wd_ag", [8 * WQI, 128], BF16, shared=True)
    p.phase("wqg")
    build_stage_l0wq(3, 1024, p=p, t=dict(
        w=xin_("wqg_w", [3 * 128, 1024], F32), wqt=wgt.ap(), identbf=ID_BF))
    p.phase("wqu")
    build_stage_l0wq(3, 1024, p=p, t=dict(
        w=xin_("wqu_w", [3 * 128, 1024], F32), wqt=wut.ap(), identbf=ID_BF))
    p.phase("wqd")
    build_stage_l0wq(1, WQI, p=p, t=dict(
        w=xin_("wqd_w", [128, WQI], F32), wqt=wdt.ap(), identbf=ID_BF))

    y0f = dram("y0f", [T, D], F32, ext=(upto == 1))
    p.phase("l0ffn")
    ag(wgt, wgag, G8)
    ag(wut, wuag, G8)
    ag(wdt, wdag, G8)
    build_stage_l0ffn(p=p, t=dict(
        xn=xn0.ap(), identbf=ID_BF,
        w_ags=(wgag.ap(), wuag.ap(), wdag.ap(), per_gu), y0f=y0f.ap()))
    if upto == 1:
        return p

    def vprefill(vt, width, rows):
        ONEV = p.sbuf([128, width], BF16, "ONEV")
        va = p.sem("va")
        vpre = p.sem("vpre")
        tm_ = run(p, "vector", lambda e, O=ONEV: e.memset(O[:, :], 1.0), va)
        wait(p, "sync", va, tm_)
        for tt in range(rows // 128):
            dma(p, "sync", vpre, vt.ap()[tt * 128:(tt + 1) * 128, :], ONEV[:, :])
        wait(p, "sync", vpre, vpre.n)

    xnew1 = dram("xnew1", [T, D], F32, ext=(upto == 2))
    xn1 = dram("xn1", [T, D], F32)
    xnfm1 = dram("xnfm1", [D, T], BF16)
    p.phase("ln1")
    build_stage_ln(True, None, p=p, t=dict(
        xin=x_in, ya=y0a.ap(), yf=y0f.ap(), identbf=ID_BF, cconst=CCON,
        xnew=xnew1.ap(), xn=xn1.ap(), xnfm=xnfm1.ap()))

    qr1 = dram("qr1", [D, T], BF16)
    kcc1 = dram("kcc1", [D, T], BF16)
    vcc1 = dram("vcc1", [T, 16 * 65], BF16)
    p.phase("l1qkv")
    vprefill(vcc1, 1040, T)
    build_stage_l1qkv(p=p, t=dict(
        xnfm=xnfm1.ap(),
        wq=xin_("l1_wq", [D, D], F32), wk=xin_("l1_wk", [D, D], F32),
        wv=xin_("l1_wv", [D, D], F32),
        bq=xin_("l1_bq", [128, 8], F32), bk=xin_("l1_bk", [128, 8], F32),
        bv=xin_("l1_bv", [1, D], F32),
        cosr=xin_("l1_cosr", [128, T], BF16), sinr=xin_("l1_sinr", [128, T], BF16),
        rotperm=xin_("l1_rotperm", [128, 128], BF16),
        qr=qr1.ap(), kr=kcc1.ap(), vemode=("ve16", vcc1)))

    kag1 = dram("kag1", [2 * D, T], BF16)
    vag1 = dram("vag1", [S, 16 * 65], BF16)
    y1a = dram("y1a", [T, D], F32, ext=(upto == 2))
    p.phase("l1attn")
    ag(kcc1, kag1)
    ag(vcc1, vag1)
    build_stage_mha(causal=True, scale=0.125, name="l1attn", p=p, t=dict(
        qr=qr1.ap(), krf_ag=kag1.ap(), vef=vag1.ap(),
        wo=xin_("l1_wo", [D, D], F32), bo=xin_("l1_bo", [1, D], F32),
        bcn=MBCN, smsk2=MSMSK, ya=y1a.ap()))

    y1f = dram("y1f", [T, D], F32, ext=(upto == 2))
    p.phase("l1ffn")
    build_stage_l1ffn(p=p, t=dict(
        xnfm=xnfm1.ap(),
        sgw=xin_("l1f_sgw", [D, 2048], F32), suw=xin_("l1f_suw", [D, 2048], F32),
        sdw=xin_("l1f_sdw", [2048, D], F32),
        rw=xin_("l1f_rw", [D, 8], F32), rb=xin_("l1f_rb", [1, 8], F32),
        wg=xin_("l1f_wg", [8, D, 2048], F32), wu=xin_("l1f_wu", [8, D, 2048], F32),
        wd=xin_("l1f_wd", [8, 2048, D], F32),
        y1f=y1f.ap()))
    if upto == 2:
        return p

    xnew2 = dram("xnew2", [T, D], F32, ext=(upto == 3))
    xn2 = dram("xn2", [T, D], F32, ext=(upto == 3))
    znfm = dram("znfm", [D, T], BF16)
    c2d = dram("c2d", [128, NT], F32)
    p.phase("ln2")
    build_stage_ln(True, "c2", p=p, t=dict(
        xin=xnew1.ap(), ya=y1a.ap(), yf=y1f.ap(), identbf=ID_BF, cconst=CCON,
        xnew=xnew2.ap(), xn=xn2.ap(), xnfm=znfm.ap(), c2=c2d.ap()))

    qr2 = dram("qr2", [D, T], BF16)
    kcc2 = dram("kcc2", [D, T], BF16)
    vcc2 = dram("vcc2", [T, 16 * 65], BF16)
    p.phase("l2qkv")
    vprefill(vcc2, 1040, T)
    build_stage_l1qkv(True, False, "l2qkv", p=p, t=dict(
        xnfm=znfm.ap(),
        wq=xin_("l2_wq", [D, D], F32), wk=xin_("l2_wk", [D, D], F32),
        wv=xin_("l2_wv", [D, D], F32),
        bq=xin_("l2_bq", [128, 8], F32), bk=xin_("l2_bk", [128, 8], F32),
        bv=xin_("l2_bv", [1, D], F32),
        lnw=xin_("l2_lnw", [128, 8], F32), lnb=xin_("l2_lnb", [128, 8], F32),
        qr=qr2.ap(), kr=kcc2.ap(), vemode=("ve16", vcc2)))

    kag2 = dram("kag2", [2 * D, T], BF16)
    vag2 = dram("vag2", [S, 16 * 65], BF16)
    y2a = dram("y2a", [T, D], F32, ext=(upto == 3))
    p.phase("l2attn")
    ag(kcc2, kag2)
    ag(vcc2, vag2)
    build_stage_mha(causal=False, scale=0.125, name="l2attn", p=p, t=dict(
        qr=qr2.ap(), krf_ag=kag2.ap(), vef=vag2.ap(),
        wo=xin_("l2_wo", [D, D], F32), bo=xin_("l2_bo", [1, D], F32), ya=y2a.ap()))

    qlcc = dram("qlcc", [512, T], BF16)
    klcc = dram("klcc", [512, T], BF16)
    vlcc = dram("vlcc", [T, 520], BF16)
    p.phase("l2locqkv")
    vprefill(vlcc, 520, T)
    build_stage_l2locqkv(p=p, t=dict(
        xnfm=znfm.ap(), lnw=xin_("l2l_lnw", [128, 8], F32), lnb=xin_("l2l_lnb", [128, 8], F32),
        wq=xin_("l2l_wq", [D, 512], F32), wk=xin_("l2l_wk", [D, 512], F32),
        wv=xin_("l2l_wv", [D, 512], F32),
        ql=qlcc.ap(), kl=klcc.ap(), vemode=("ve8", vlcc)))

    qlag = dram("qlag", [1024, T], BF16)
    klag = dram("klag", [1024, T], BF16)
    vlag = dram("vlag", [S, 520], BF16)
    lightf = dram("lightf", [S, D], F32)
    p.phase("l2locattn")
    ag(qlcc, qlag)
    ag(klcc, klag)
    ag(vlcc, vlag)
    build_stage_l2locattn(p=p, t=dict(
        ql_ag=qlag.ap(), kl_ag=klag.ap(), vlef=vlag.ap(),
        wol=xin_("l2l_wol", [512, D], F32), lmask=xin_("l2l_lmask", [128, 2 * 128], F32),
        light=lightf.ap()))

    qh2 = dram("qh2", [512, T], BF16)
    khcc = dram("khcc", [512, T], BF16)
    vhcc = dram("vhcc", [T, 520], BF16)
    scc = dram("scc", [128, 2 * NT], F32)
    p.phase("l2hkv")
    vprefill(vhcc, 520, T)
    build_stage_l2hkv(p=p, t=dict(
        xnfm=znfm.ap(), gam=xin_("l2h_gam", [128, 8], F32), c2=c2d.ap(),
        rt=xin_("l2h_rt", [D, 2], F32),
        wq=xin_("l2h_wq", [D, 512], F32), wk=xin_("l2h_wk", [D, 512], F32),
        wv=xin_("l2h_wv", [D, 512], F32),
        qh=qh2.ap(), kh=khcc.ap(), vemode=("ve8", vhcc), s=scc.ap()))

    khag2 = dram("khag2", [1024, T], BF16)
    vhag2 = dram("vhag2", [S, 520], BF16)
    sag2 = dram("sag2", [256, 2 * NT], F32)
    mrow = dram("mrow", [1, S], F32)
    y2f = dram("y2f", [T, D], F32, ext=(upto == 3))
    p.phase("l2heavy")
    ag(khcc, khag2)
    ag(vhcc, vhag2)
    ag(scc, sag2)
    build_stage_l2heavy(p=p, t=dict(
        qh=qh2.ap(), kh_ag=khag2.ap(), vh_ag=vhag2.ap(), s_ag=sag2.ap(),
        nk=xin_("l2h_nk", [128, 4], F32), nv=xin_("l2h_nv", [1, 520], F32),
        lof=xin_("l2h_lof", [128, NT], mybir.dt.int32),
        identf=(ID_F32 := xin_("identf", [128, 128], F32)), mrow_h=mrow, light=lightf.ap(),
        sqtm=scc.ap()[:, 0:NT],
        kcon=xin_("l2h_kcon", [1, 2], F32), woh=xin_("l2h_woh", [512, D], F32),
        nqb=xin_("l2h_nqb", [128, D], F32), y2f=y2f.ap()))
    if upto == 3:
        return p

    I32 = mybir.dt.int32
    xnew3 = dram("xnew3", [T, D], F32)
    xn3 = dram("xn3", [T, D], F32)
    xnfm3 = dram("xnfm3", [D, T], BF16)
    p.phase("ln3")
    build_stage_ln(True, None, True, p=p, t=dict(
        xin=xnew2.ap(), ya=y2a.ap(), yf=y2f.ap(), y3=xn2.ap(),
        identbf=ID_BF, cconst=CCON,
        xnew=xnew3.ap(), xn=xn3.ap(), xnfm=xnfm3.ap()))

    qcc3 = dram("qcc3", [D, T], BF16)
    kcc3 = dram("kcc3", [D, T], BF16)
    vcc3 = dram("vcc3", [2 * T, 520], BF16)
    p.phase("l3qkv")
    vprefill(vcc3, 520, 2 * T)
    build_stage_l1qkv(False, False, "l3qkv", p=p, t=dict(
        xnfm=xnfm3.ap(),
        wq=xin_("l3_wq", [D, D], F32), wk=xin_("l3_wk", [D, D], F32),
        wv=xin_("l3_wv", [D, D], F32),
        bq=xin_("l3_bq", [128, 8], F32), bk=xin_("l3_bk", [128, 8], F32),
        bv=xin_("l3_bv", [1, D], F32),
        qr=qcc3.ap(), kr=kcc3.ap(), vemode=("ve8x2", vcc3)))

    qag3 = dram("qag3", [2 * D, T], BF16)
    kag3 = dram("kag3", [2 * D, T], BF16)
    vag3 = dram("vag3", [4 * T, 520], BF16)
    aocc = dram("aocc", [1024, T], BF16)
    p.phase("l3attn")
    ag(qcc3, qag3)
    ag(kcc3, kag3)
    ag(vcc3, vag3)
    build_stage_l3attn(p=p, t=dict(
        qk_ag=qag3.ap(), kk_ag=kag3.ap(), v_ag=vag3.ap(),
        ofq=xin_("l3_ofq", [128, 16], I32), ofk=xin_("l3_ofk", [64, 16], I32),
        ofv=xin_("l3_ofv", [128, 32], I32), aocc=aocc.ap(),
        embt=xin_("l3_embt", [64, 73], F32), identf=ID_F32))

    aoag = dram("aoag", [2048, T], BF16)
    y3a = dram("y3a", [T, D], F32)
    p.phase("l3proj")
    ag(aocc, aoag)
    build_stage_l3proj(p=p, t=dict(
        ao_ag=aoag.ap(), ofa=xin_("l3_ofa", [128, 8], I32),
        wo=xin_("l3_wo", [D, D], F32), bo=xin_("l3_bo", [1, D], F32), ya=y3a.ap()))

    y3f = dram("y3f", [T, D], F32)
    wg3 = xin_("l3f_wg", [8, D, 4096], F32)
    wu3 = xin_("l3f_wu", [8, D, 4096], F32)
    wd3 = xin_("l3f_wd", [8, 4096, D], F32)
    for part in (0, 1):
        p.phase(f"l3ffn{part}")
        build_stage_l3ffn(1024, p=p, t=dict(
            xnfm=xnfm3.ap()[:, part * 1024:(part + 1) * 1024],
            wg=wg3, wu=wu3, wd=wd3,
            y3f=y3f.ap()[part * 1024:(part + 1) * 1024, :]))

    outd = dram("out", [T, D], F32, ext=True)
    p.phase("final")
    build_stage_final(p=p, t=dict(
        xin=xnew3.ap(), ya=y3a.ap(), yf=y3f.ap(), out=outd.ap()))
    return p


def _fused_inputs(Z, upto):
    g = Z
    pk = pack_lhsT
    cm = _cmask()
    ctabs = [_causal_tabs(gg, cm) for gg in (0, 1)]
    xs = _shard_tm(np.asarray(g["L0_attn_hidden_states"], np.float32))
    wq0 = pk(g["L0_attn_Wq"].T)
    wk0 = pk(g["L0_attn_Wk"].T)
    fc1p = np.ascontiguousarray(np.asarray(g["L0_attn_fc1_w"], np.float32).T)
    fc1b = np.ascontiguousarray(np.asarray(g["L0_attn_fc1_b"], np.float32)[:, None])
    fc2p = np.concatenate([np.asarray(g["L0_attn_fc2_w"], np.float32).T,
                           np.asarray(g["L0_attn_fc2_b"], np.float32)[None, :]], 0)
    shards = {}
    for nm, w, per in (("wqg_w", g["L0_ffn_w_gate"], [3, 3, 3, 3, 3, 3, 2, 2]),
                       ("wqu_w", g["L0_ffn_w_up"], [3, 3, 3, 3, 3, 3, 2, 2]),
                       ("wqd_w", g["L0_ffn_w_down"], [1] * 8)):
        w = np.asarray(w, np.float32)
        Wc = w.shape[1]
        offs = np.cumsum([0] + per)
        ls = []
        for c in range(NCORE):
            sl = w[offs[c] * 128:(offs[c] + per[c]) * 128]
            if per[c] < max(per):
                sl = np.concatenate([sl, np.zeros((128 * (max(per) - per[c]), Wc), np.float32)], 0)
            ls.append(np.ascontiguousarray(sl))
        shards[nm] = ls
    ims = []
    for c in range(NCORE):
        cosf, sinf = _l0_ropetabs(c % 2)
        im = dict(xin=xs[c], identbf=IDENT_BF, cconst=CCONST,
                  l0qk_wq=wq0, l0qk_wk=wk0, l0qk_cosf=cosf, l0qk_sinf=sinf,
                  l0a_fc1p=fc1p, l0a_fc1b=fc1b, l0a_fc2p=fc2p,
                  l0a_bcn=ctabs[c % 2][0], l0a_smsk2=ctabs[c % 2][1],
                  wqg_w=shards["wqg_w"][c], wqu_w=shards["wqu_w"][c],
                  wqd_w=shards["wqd_w"][c])
        ims.append(im)
    if upto >= 2:
        pm = _l1_perm()
        rpm = _l1_rotperm()
        com = dict(
            l1_wq=pk(np.asarray(g["L1_attn_Wq"], np.float32)[:, pm]),
            l1_wk=pk(np.asarray(g["L1_attn_Wk"], np.float32)[:, pm]),
            l1_wv=pk(g["L1_attn_Wv"]),
            l1_bq=_ppcols(np.asarray(g["L1_attn_bq"], np.float32)[pm]),
            l1_bk=_ppcols(np.asarray(g["L1_attn_bk"], np.float32)[pm]),
            l1_bv=np.ascontiguousarray(np.asarray(g["L1_attn_bv"], np.float32)[None, :]),
            l1_rotperm=rpm,
            l1_wo=pk(g["L1_attn_Wo"]),
            l1_bo=np.ascontiguousarray(np.asarray(g["L1_attn_bo"], np.float32)[None, :]),
            l1f_sgw=pk(np.asarray(g["L1_ffn_sg_w"], np.float32).T),
            l1f_suw=pk(np.asarray(g["L1_ffn_su_w"], np.float32).T),
            l1f_sdw=pk(np.asarray(g["L1_ffn_sd_w"], np.float32).T),
            l1f_rw=pk(np.asarray(g["L1_ffn_router_w"], np.float32).T),
            l1f_rb=np.ascontiguousarray(np.asarray(g["L1_ffn_routing_bias"], np.float32)[None, :]),
            l1f_wg=np.ascontiguousarray(np.asarray(g["L1_ffn_wg"], np.float32).transpose(0, 2, 1)),
            l1f_wu=np.ascontiguousarray(np.asarray(g["L1_ffn_wu"], np.float32).transpose(0, 2, 1)),
            l1f_wd=np.ascontiguousarray(np.asarray(g["L1_ffn_wd"], np.float32).transpose(0, 2, 1)))
        for c in range(NCORE):
            cosr, sinr = _l1_ropetabs(c % 2)
            ims[c].update(com)
            ims[c]["l1_cosr"] = cosr
            ims[c]["l1_sinr"] = sinr
    if upto >= 3:
        Wq2 = np.asarray(g["L2_attn_W_qkv"], np.float32)
        bqkv = np.asarray(g["L2_attn_b_qkv"], np.float32)
        Wl = np.asarray(g["L2_ffn_Wqkv"], np.float32)
        Wkv = np.asarray(g["L2_ffn_Wkv"], np.float32)
        wkv_k = np.concatenate([Wkv[h * 128:h * 128 + 64] for h in range(8)], 0)
        wkv_v = np.concatenate([Wkv[h * 128 + 64:h * 128 + 128] for h in range(8)], 0)
        nullk = np.asarray(g["L2_ffn_null_kv"], np.float32)[0]
        nullv = np.asarray(g["L2_ffn_null_kv"], np.float32)[1]
        nv = np.ones((1, 520), np.float32)
        for hh in range(8):
            nv[0, hh * 65:hh * 65 + 64] = nullv[hh]
        com = dict(
            l2_wq=pk(Wq2[:, 0:1024]), l2_wk=pk(Wq2[:, 1024:2048]), l2_wv=pk(Wq2[:, 2048:3072]),
            l2_bq=_ppcols(bqkv[0:1024]), l2_bk=_ppcols(bqkv[1024:2048]),
            l2_bv=np.ascontiguousarray(bqkv[2048:3072][None, :]),
            l2_lnw=_ppcols(g["L2_attn_ln_w"]), l2_lnb=_ppcols(g["L2_attn_ln_b"]),
            l2_wo=pk(g["L2_attn_W_proj"]),
            l2_bo=np.ascontiguousarray(np.asarray(g["L2_attn_b_proj"], np.float32)[None, :]),
            l2l_lnw=_ppcols(g["L2_ffn_ln_w"]), l2l_lnb=_ppcols(g["L2_ffn_ln_b"]),
            l2l_wq=pk(Wl[0:512].T), l2l_wk=pk(Wl[512:1024].T), l2l_wv=pk(Wl[1024:1536].T),
            l2l_wol=pk(np.asarray(g["L2_ffn_Wout_l"], np.float32).T),
            l2l_lmask=_lmask(),
            l2h_gam=_ppcols(g["L2_ffn_gamma"]),
            l2h_rt=pk(np.concatenate([np.asarray(g["L2_ffn_rt_q"], np.float32),
                                      np.asarray(g["L2_ffn_rt_kv"], np.float32)], 0).T),
            l2h_wq=pk(np.asarray(g["L2_ffn_Wq"], np.float32).T),
            l2h_wk=pk(wkv_k.T), l2h_wv=pk(wkv_v.T),
            l2h_nk=np.ascontiguousarray(nullk.reshape(4, 128).T),
            l2h_nv=nv,
            l2h_kcon=np.array([[512.0, 1024.0]], np.float32),
            l2h_woh=pk(np.asarray(g["L2_ffn_Wout_h"], np.float32).T),
            l2h_nqb=np.tile(np.asarray(g["L2_ffn_null_q"], np.float32)[None, :], (128, 1)),
            identf=IDENT_F32)
        for c in range(NCORE):
            lof = np.zeros((128, NT), np.int32)
            for tt in range(NT):
                lof[:, tt] = (c % 2) * T + tt * 128 + np.arange(128)
            ims[c].update(com)
            ims[c]["l2h_lof"] = lof
    if upto >= 4:
        com = dict(
            l3_wq=pk(np.asarray(g["L3_attn_Wq"], np.float32).T),
            l3_wk=pk(np.asarray(g["L3_attn_Wk"], np.float32).T),
            l3_wv=pk(np.asarray(g["L3_attn_Wv"], np.float32).T),
            l3_bq=_ppcols(g["L3_attn_bq"]), l3_bk=_ppcols(g["L3_attn_bk"]),
            l3_bv=np.ascontiguousarray(np.asarray(g["L3_attn_bv"], np.float32)[None, :]),
            l3_embt=np.ascontiguousarray(np.asarray(g["L3_attn_dist_emb"], np.float32).T),
            l3_wo=pk(np.asarray(g["L3_attn_Wo"], np.float32).T),
            l3_bo=np.ascontiguousarray(np.asarray(g["L3_attn_bo"], np.float32)[None, :]),
            l3f_wg=np.ascontiguousarray(np.asarray(g["L3_ffn_Wg"], np.float32)),
            l3f_wu=np.ascontiguousarray(np.asarray(g["L3_ffn_Wu"], np.float32)),
            l3f_wd=np.ascontiguousarray(np.asarray(g["L3_ffn_Wd"], np.float32)))
        pp = np.arange(128)
        for c in range(NCORE):
            gq = c % 2
            ofq = np.zeros((128, 16), np.int32)
            ofk = np.zeros((64, 16), np.int32)
            ofv = np.zeros((128, 32), np.int32)
            ofa = np.zeros((128, 8), np.int32)
            for hc in range(4):
                for gg in range(2):
                    ofq[:, hc * 2 + gg] = gg * D + gq * 512 + hc * 128 + pp
            for h in range(8):
                for gg in range(2):
                    ofk[:, h * 2 + gg] = gg * D + gq * 512 + h * 64 + np.arange(64)
            for kt in range(32):
                ofv[:, kt] = (gq + (2 if kt >= 16 else 0)) * T + (kt % 16) * 128 + pp
            for ci in range(8):
                H = 2 * ci + (pp >= 64).astype(np.int32)
                ofa[:, ci] = np.where(H < 8, 0, 1024) + gq * 512 + \
                    128 * ((H % 8) // 2) + 64 * (H % 2) + (pp % 64)
            ims[c]["l3_ofq"] = ofq
            ims[c]["l3_ofk"] = ofk
            ims[c]["l3_ofv"] = ofv
            ims[c]["l3_ofa"] = ofa
            ims[c].update(com)
    return ims, xs


def _fused_run(Z, st, upto):
    nc = _prog(("mega", upto), build_fused, upto)
    ims, xs = _fused_inputs(Z, upto)
    r = _run(nc, ims)
    st["x0"] = xs
    if upto == 1:
        st["y0a"] = [r[c]["y0a"] for c in range(NCORE)]
        st["y0f"] = [r[c]["y0f"] for c in range(NCORE)]
    elif upto == 2:
        st["x1"] = [r[c]["xnew1"] for c in range(NCORE)]
        st["y1a"] = [r[c]["y1a"] for c in range(NCORE)]
        st["y1f"] = [r[c]["y1f"] for c in range(NCORE)]
    elif upto == 3:
        st["x2"] = [r[c]["xnew2"] for c in range(NCORE)]
        st["xn2"] = [r[c]["xn2"] for c in range(NCORE)]
        st["y2a"] = [r[c]["y2a"] for c in range(NCORE)]
        st["y2f"] = [r[c]["y2f"] for c in range(NCORE)]
    elif upto == 4:
        st["out"] = [r[c]["out"] for c in range(NCORE)]


def _seg_l0(Z, st):
    g = Z
    pk = pack_lhsT
    nc = _prog("ln0", build_stage_ln, False, None)
    xs = _shard_tm(np.asarray(g["L0_attn_hidden_states"], np.float32))
    r = _run(nc, [{"xin": xs[c], "identbf": IDENT_BF, "cconst": CCONST} for c in range(NCORE)])
    xn0 = [r[c]["xn"] for c in range(NCORE)]
    xnfm0 = [r[c]["xnfm"] for c in range(NCORE)]
    x0 = xs

    nc = _prog("l0qk", build_stage_l0qk)
    wq = pk(g["L0_attn_Wq"].T); wk = pk(g["L0_attn_Wk"].T)
    ims = []
    for c in range(NCORE):
        cosf, sinf = _l0_ropetabs(c % 2)
        ims.append({"xnfm": xnfm0[c], "wq": wq, "wk": wk, "cosf": cosf, "sinf": sinf})
    r = _run(nc, ims)
    qr = [r[c]["qr"] for c in range(NCORE)]
    kr = [r[c]["kr"] for c in range(NCORE)]

    fc1p = np.ascontiguousarray(np.asarray(g["L0_attn_fc1_w"], np.float32).T)
    fc1b = np.ascontiguousarray(np.asarray(g["L0_attn_fc1_b"], np.float32)[:, None])
    fc2p = np.concatenate([np.asarray(g["L0_attn_fc2_w"], np.float32).T,
                           np.asarray(g["L0_attn_fc2_b"], np.float32)[None, :]], 0)
    cm = _cmask()
    ctabs = [_causal_tabs(gg, cm) for gg in (0, 1)]
    nc = _prog("l0attn", build_stage_l0attn)
    ims = []
    for c in range(NCORE):
        bb = c // 2
        krf = np.concatenate([kr[2 * bb], kr[2 * bb + 1]], axis=1)
        ims.append({"qr": qr[c], "krf": krf, "fc1p": fc1p, "fc1b": fc1b,
                    "fc2p": fc2p, "bcn": ctabs[c % 2][0], "smsk2": ctabs[c % 2][1]})
    r = _run(nc, ims)
    y0a = [r[c]["y0a"] for c in range(NCORE)]

    wqout = {}
    for name, w, per in (("wg", g["L0_ffn_w_gate"], [3, 3, 3, 3, 3, 3, 2, 2]),
                         ("wu", g["L0_ffn_w_up"], [3, 3, 3, 3, 3, 3, 2, 2]),
                         ("wd", g["L0_ffn_w_down"], [1] * 8)):
        w = np.asarray(w, np.float32)
        W = w.shape[1]
        nc = _prog(("l0wq", max(per), W), build_stage_l0wq, max(per), W)
        offs = np.cumsum([0] + per)
        ims = []
        for c in range(NCORE):
            sl = w[offs[c] * 128:(offs[c] + per[c]) * 128]
            if per[c] < max(per):
                sl = np.concatenate([sl, np.zeros((128 * (max(per) - per[c]), W), np.float32)], 0)
            ims.append({"w": np.ascontiguousarray(sl)})
        r = _run(nc, ims)
        wqout[name] = np.concatenate([r[c]["wq"][:per[c] * 128].astype(np.float32) for c in range(NCORE)], 0)
    nc = _prog("l0ffn", build_stage_l0ffn)
    wgP = np.ascontiguousarray(wqout["wg"].T).astype(BF)
    wuP = np.ascontiguousarray(wqout["wu"].T).astype(BF)
    wdP = np.ascontiguousarray(wqout["wd"].T).astype(BF)
    r = _run(nc, [{"xn": np.asarray(xn0[c]), "wgq": wgP, "wuq": wuP, "wdq": wdP,
                   "identbf": IDENT_BF} for c in range(NCORE)])
    y0f = [r[c]["y0f"] for c in range(NCORE)]
    st["x0"] = x0
    st["y0a"] = y0a
    st["y0f"] = y0f


def _seg_l1(Z, st):
    g = Z
    pk = pack_lhsT
    x0, y0a, y0f = st["x0"], st["y0a"], st["y0f"]
    cm = _cmask()
    ctabs = [_causal_tabs(gg, cm) for gg in (0, 1)]
    nc = _prog("ln1", build_stage_ln, True, None)
    r = _run(nc, [{"xin": x0[c], "ya": np.asarray(y0a[c]), "yf": np.asarray(y0f[c]),
                   "identbf": IDENT_BF, "cconst": CCONST} for c in range(NCORE)])
    x1 = [r[c]["xnew"] for c in range(NCORE)]
    xnfm1 = [r[c]["xnfm"] for c in range(NCORE)]

    pm = _l1_perm()
    nc = _prog("l1qkv", build_stage_l1qkv)
    wq = pk(np.asarray(g["L1_attn_Wq"], np.float32)[:, pm])
    wk = pk(np.asarray(g["L1_attn_Wk"], np.float32)[:, pm])
    wv = pk(g["L1_attn_Wv"])
    bq = _ppcols(np.asarray(g["L1_attn_bq"], np.float32)[pm])
    bk = _ppcols(np.asarray(g["L1_attn_bk"], np.float32)[pm])
    bv = np.ascontiguousarray(np.asarray(g["L1_attn_bv"], np.float32)[None, :])
    rpm = _l1_rotperm()
    ims = []
    for c in range(NCORE):
        cosr, sinr = _l1_ropetabs(c % 2)
        ims.append({"xnfm": xnfm1[c], "wq": wq, "wk": wk, "wv": wv, "bq": bq, "bk": bk,
                    "bv": bv, "cosr": cosr, "sinr": sinr, "rotperm": rpm})
    r = _run(nc, ims)
    qr = [r[c]["qr"] for c in range(NCORE)]
    kr = [r[c]["kr"] for c in range(NCORE)]
    vv = [r[c]["v"] for c in range(NCORE)]

    wo = pk(g["L1_attn_Wo"])
    bo = np.ascontiguousarray(np.asarray(g["L1_attn_bo"], np.float32)[None, :])
    ims = []
    for c in range(NCORE):
        bb = c // 2
        krf = np.concatenate([kr[2 * bb], kr[2 * bb + 1]], 1)
        ims.append({"qr": qr[c], "krf": krf, "vef": _vef(vv, bb, 16),
                    "wo": wo, "bo": bo, "bcn": ctabs[c % 2][0], "smsk2": ctabs[c % 2][1]})
    nc = _prog("l1attn", build_stage_mha, causal=True, scale=0.125, name="l1attn")
    r = _run(nc, ims)
    y1a = [r[c]["ya"] for c in range(NCORE)]

    nc = _prog("l1ffn", build_stage_l1ffn)
    ins1 = dict(
        sgw=pk(np.asarray(g["L1_ffn_sg_w"], np.float32).T), suw=pk(np.asarray(g["L1_ffn_su_w"], np.float32).T),
        sdw=pk(np.asarray(g["L1_ffn_sd_w"], np.float32).T),
        rw=pk(np.asarray(g["L1_ffn_router_w"], np.float32).T),
        rb=np.ascontiguousarray(np.asarray(g["L1_ffn_routing_bias"], np.float32)[None, :]),
        wg=np.ascontiguousarray(np.asarray(g["L1_ffn_wg"], np.float32).transpose(0, 2, 1)),
        wu=np.ascontiguousarray(np.asarray(g["L1_ffn_wu"], np.float32).transpose(0, 2, 1)),
        wd=np.ascontiguousarray(np.asarray(g["L1_ffn_wd"], np.float32).transpose(0, 2, 1)))
    r = _run(nc, [{**ins1, "xnfm": xnfm1[c]} for c in range(NCORE)])
    y1f = [r[c]["y1f"] for c in range(NCORE)]
    st["x1"] = x1
    st["y1a"] = y1a
    st["y1f"] = y1f


def _seg_l2(Z, st):
    g = Z
    pk = pack_lhsT
    x1, y1a, y1f = st["x1"], st["y1a"], st["y1f"]
    nc = _prog("ln2", build_stage_ln, True, "c2")
    r = _run(nc, [{"xin": np.asarray(x1[c]), "ya": np.asarray(y1a[c]), "yf": np.asarray(y1f[c]),
                   "identbf": IDENT_BF, "cconst": CCONST} for c in range(NCORE)])
    x2 = [r[c]["xnew"] for c in range(NCORE)]
    xn2 = [r[c]["xn"] for c in range(NCORE)]
    zncfm = [r[c]["xnfm"] for c in range(NCORE)]
    c2c = [r[c]["c2"] for c in range(NCORE)]

    W = np.asarray(g["L2_attn_W_qkv"], np.float32)
    bqkv = np.asarray(g["L2_attn_b_qkv"], np.float32)
    nc = _prog("l2qkv", build_stage_l1qkv, True, False, "l2qkv")
    ims = [{"xnfm": zncfm[c], "wq": pk(W[:, 0:1024]), "wk": pk(W[:, 1024:2048]),
            "wv": pk(W[:, 2048:3072]), "bq": _ppcols(bqkv[0:1024]), "bk": _ppcols(bqkv[1024:2048]),
            "bv": np.ascontiguousarray(bqkv[2048:3072][None, :]),
            "lnw": _ppcols(g["L2_attn_ln_w"]), "lnb": _ppcols(g["L2_attn_ln_b"])}
           for c in range(NCORE)]
    r = _run(nc, ims)
    qr = [r[c]["qr"] for c in range(NCORE)]
    kr = [r[c]["kr"] for c in range(NCORE)]
    vv = [r[c]["v"] for c in range(NCORE)]
    wo = pk(g["L2_attn_W_proj"])
    bo = np.ascontiguousarray(np.asarray(g["L2_attn_b_proj"], np.float32)[None, :])
    ims = []
    for c in range(NCORE):
        bb = c // 2
        krf = np.concatenate([kr[2 * bb], kr[2 * bb + 1]], 1)
        ims.append({"qr": qr[c], "krf": krf, "vef": _vef(vv, bb, 16),
                    "wo": wo, "bo": bo})
    nc = _prog("l2attn", build_stage_mha, causal=False, scale=0.125, name="l2attn")
    r = _run(nc, ims)
    y2a = [r[c]["ya"] for c in range(NCORE)]

    Wl = np.asarray(g["L2_ffn_Wqkv"], np.float32)
    nc = _prog("l2locqkv", build_stage_l2locqkv)
    ims = [{"xnfm": zncfm[c], "wq": pk(Wl[0:512].T), "wk": pk(Wl[512:1024].T),
            "wv": pk(Wl[1024:1536].T), "lnw": _ppcols(g["L2_ffn_ln_w"]),
            "lnb": _ppcols(g["L2_ffn_ln_b"])} for c in range(NCORE)]
    r = _run(nc, ims)
    ql = [r[c]["ql"] for c in range(NCORE)]
    kl = [r[c]["kl"] for c in range(NCORE)]
    vl = [r[c]["vl"] for c in range(NCORE)]
    wol = pk(np.asarray(g["L2_ffn_Wout_l"], np.float32).T)
    lm = _lmask()
    ims = []
    for c in range(NCORE):
        bb = c // 2
        klf = np.concatenate([kl[2 * bb], kl[2 * bb + 1]], 1)
        qlf = np.concatenate([ql[2 * bb], ql[2 * bb + 1]], 1)
        ims.append({"ql": qlf, "klf": klf, "vlef": _vef(vl, bb, 8), "wol": wol, "lmask": lm})
    nc = _prog("l2locattn", build_stage_l2locattn)
    r = _run(nc, ims)
    light = [np.ascontiguousarray(np.asarray(r[c]["light"])[(c % 2) * T:(c % 2 + 1) * T])
             for c in range(NCORE)]

    Wkv = np.asarray(g["L2_ffn_Wkv"], np.float32)
    wkv_k = np.concatenate([Wkv[h * 128:h * 128 + 64] for h in range(8)], 0)
    wkv_v = np.concatenate([Wkv[h * 128 + 64:h * 128 + 128] for h in range(8)], 0)
    rt = pk(np.concatenate([np.asarray(g["L2_ffn_rt_q"], np.float32),
                            np.asarray(g["L2_ffn_rt_kv"], np.float32)], 0).T)
    nc = _prog("l2hkv", build_stage_l2hkv)
    ims = [{"xnfm": zncfm[c], "gam": _ppcols(g["L2_ffn_gamma"]), "c2": np.asarray(c2c[c]),
            "rt": rt, "wq": pk(np.asarray(g["L2_ffn_Wq"], np.float32).T),
            "wk": pk(wkv_k.T), "wv": pk(wkv_v.T)} for c in range(NCORE)]
    r = _run(nc, ims)
    qh = [r[c]["qh"] for c in range(NCORE)]
    kh = [r[c]["kh"] for c in range(NCORE)]
    vh = [r[c]["vh"] for c in range(NCORE)]
    ssd = [r[c]["s"] for c in range(NCORE)]

    nullk = np.asarray(g["L2_ffn_null_kv"], np.float32)[0]
    nullv = np.asarray(g["L2_ffn_null_kv"], np.float32)[1]
    woh = pk(np.asarray(g["L2_ffn_Wout_h"], np.float32).T)
    nqb = np.tile(np.asarray(g["L2_ffn_null_q"], np.float32)[None, :], (128, 1))
    kcon = np.array([[512.0, 1024.0]], np.float32)
    nc = _prog("l2heavy", build_stage_l2heavy)
    ims = []
    for c in range(NCORE):
        bb = c // 2
        khf = np.zeros((512, S + 128), np.float32)
        khf[:, 0:S] = np.concatenate([np.asarray(kh[2 * bb], np.float32),
                                      np.asarray(kh[2 * bb + 1], np.float32)], 1)
        khf[:, S] = nullk.reshape(512)
        vhef = np.zeros((S + 128, 8 * 65), np.float32)
        vfull = np.concatenate([np.asarray(vh[2 * bb], np.float32),
                                np.asarray(vh[2 * bb + 1], np.float32)], 0)
        for hh in range(8):
            vhef[0:S, hh * 65:hh * 65 + 64] = vfull[:, hh * 64:(hh + 1) * 64]
            vhef[S, hh * 65:hh * 65 + 64] = nullv[hh]
            vhef[0:S + 1, hh * 65 + 64] = 1.0
        sqf = np.concatenate([ssd[2 * bb][:, 0::2].T.reshape(-1), ssd[2 * bb + 1][:, 0::2].T.reshape(-1)])
        skf = np.concatenate([ssd[2 * bb][:, 1::2].T.reshape(-1), ssd[2 * bb + 1][:, 1::2].T.reshape(-1)])
        sbis = np.zeros((128, 64), np.float32)
        sbis[:, 0:32] = sqf.reshape(32, 128).T
        sbis[:, 32:64] = skf.reshape(32, 128).T
        ims.append({"qh": qh[c], "khf": khf.astype(BF), "vhef": vhef.astype(BF),
                    "sbis": sbis, "sqrow": sqf[None, :], "skvrow": skf[None, :],
                    "sqtm": np.ascontiguousarray(ssd[c][:, 0::2]), "kcon": kcon,
                    "woh": woh, "nqb": nqb, "light": np.asarray(light[c])})
    r = _run(nc, ims)
    y2f = [r[c]["y2f"] for c in range(NCORE)]
    st["x2"] = x2
    st["y2a"] = y2a
    st["y2f"] = y2f
    st["xn2"] = xn2


def _seg_l3(Z, st):
    g = Z
    pk = pack_lhsT
    x2, y2a, y2f, xn2 = st["x2"], st["y2a"], st["y2f"], st["xn2"]
    nc = _prog("ln3", build_stage_ln, True, None, True)
    r = _run(nc, [{"xin": np.asarray(x2[c]), "ya": np.asarray(y2a[c]), "yf": np.asarray(y2f[c]),
                   "y3": np.asarray(xn2[c]), "identbf": IDENT_BF, "cconst": CCONST}
                  for c in range(NCORE)])
    x3 = [r[c]["xnew"] for c in range(NCORE)]
    xnfm3 = [r[c]["xnfm"] for c in range(NCORE)]

    nc = _prog("l3qkv", build_stage_l1qkv, False, False, "l3qkv")
    ims = [{"xnfm": xnfm3[c], "wq": pk(np.asarray(g["L3_attn_Wq"], np.float32).T),
            "wk": pk(np.asarray(g["L3_attn_Wk"], np.float32).T),
            "wv": pk(np.asarray(g["L3_attn_Wv"], np.float32).T),
            "bq": _ppcols(g["L3_attn_bq"]), "bk": _ppcols(g["L3_attn_bk"]),
            "bv": np.ascontiguousarray(np.asarray(g["L3_attn_bv"], np.float32)[None, :])}
           for c in range(NCORE)]
    r = _run(nc, ims)
    qr = [r[c]["qr"] for c in range(NCORE)]
    kr = [r[c]["kr"] for c in range(NCORE)]
    vv = [r[c]["v"] for c in range(NCORE)]
    wo = pk(np.asarray(g["L3_attn_Wo"], np.float32).T)
    bo = np.ascontiguousarray(np.asarray(g["L3_attn_bo"], np.float32)[None, :])
    embt = np.ascontiguousarray(np.asarray(g["L3_attn_dist_emb"], np.float32).T)
    nc = _prog("l3attn", build_stage_l3attn)
    ims = []
    for c in range(NCORE):
        bb = c // 2
        gg = c % 2
        krf = np.concatenate([kr[2 * bb], kr[2 * bb + 1]], 1)
        qrf = np.concatenate([qr[2 * bb], qr[2 * bb + 1]], 1)
        ims.append({"qh8": np.ascontiguousarray(qrf[gg * 512:(gg + 1) * 512]),
                    "krf8": np.ascontiguousarray(krf[gg * 512:(gg + 1) * 512]),
                    "vef8": _vef(vv, bb, 8, h0=gg * 8),
                    "embt": embt, "identf": IDENT_F32})
    r = _run(nc, ims)
    aofm = [np.asarray(r[c]["aofm"]) for c in range(NCORE)]
    nc = _prog("l3proj", build_stage_l3proj)
    ims = []
    for c in range(NCORE):
        bb = c // 2
        gc = c % 2
        af = np.zeros((D, T), BF)
        for gh in (0, 1):
            for hh in range(8):
                H = gh * 8 + hh
                af[H * 64:(H + 1) * 64] = aofm[2 * bb + gh][
                    128 * (hh // 2) + 64 * (hh % 2):128 * (hh // 2) + 64 * (hh % 2) + 64,
                    gc * T:(gc + 1) * T]
        ims.append({"af": af, "wo": wo, "bo": bo})
    r = _run(nc, ims)
    y3a = [r[c]["ya"] for c in range(NCORE)]

    nc = _prog("l3ffn", build_stage_l3ffn, 1024)
    wg3 = np.ascontiguousarray(np.asarray(g["L3_ffn_Wg"], np.float32))
    wu3 = np.ascontiguousarray(np.asarray(g["L3_ffn_Wu"], np.float32))
    wd3 = np.ascontiguousarray(np.asarray(g["L3_ffn_Wd"], np.float32))
    y3f = [np.zeros((T, D), np.float32) for _ in range(NCORE)]
    for part in (0, 1):
        ims = []
        for c in range(NCORE):
            xf = np.asarray(xnfm3[c])[:, part * 1024:(part + 1) * 1024]
            ims.append({"xnfm": np.ascontiguousarray(xf), "wg": wg3, "wu": wu3, "wd": wd3})
        r = _run(nc, ims)
        for c in range(NCORE):
            y3f[c][part * 1024:(part + 1) * 1024] = r[c]["y3f"]

    nc = _prog("final", build_stage_final)
    r = _run(nc, [{"xin": np.asarray(x3[c]), "ya": np.asarray(y3a[c]), "yf": np.asarray(y3f[c])}
                  for c in range(NCORE)])
    st["out"] = [r[c]["out"] for c in range(NCORE)]


def kernel(**Z):
    st = {}
    if FUSE_UPTO >= 1:
        _fused_run(Z, st, FUSE_UPTO)
    else:
        _seg_l0(Z, st)
    if FUSE_UPTO < 2:
        _seg_l1(Z, st)
    if FUSE_UPTO < 3:
        _seg_l2(Z, st)
    if FUSE_UPTO < 4:
        _seg_l3(Z, st)
    return _unshard_tm(st["out"])

